# version 13
# baseline (speedup 1.0000x reference)
import contextlib
import numpy as np
import ml_dtypes
import concourse.bass as bass
import concourse.mybir as mybir
from concourse.bass_utils import run_bass_kernel_spmd

F32 = mybir.dt.float32
BF16 = mybir.dt.bfloat16
I32 = mybir.dt.int32
AF = mybir.ActivationFunctionType
ALU = mybir.AluOpType

T = 4096
D = 1024
HID = 3072
NSUP = 8
NJ = 32
KC = D // 128
MC = HID // 128
NPAIR = 8
VW = 130
EPS_ATTN = 1e-6
EPS_LN = 1e-5
S127 = 1.0 / 127.0
S127SQ = float(np.float32(1.0) / np.float32(127.0)) ** 2

ENGINES = ("sync", "tensor", "vector", "scalar", "gpsimd")


class Sem:
    __slots__ = ("h", "n", "name")

    def __init__(self, h, name):
        self.h = h
        self.n = 0
        self.name = name


class Prog:

    def __init__(self, nc, stack, max_ops=None):
        self.nc = nc
        self.stack = stack
        self.ops = {e: [] for e in ENGINES}
        self._sems = []
        self.n_ops = 0
        self.max_ops = max_ops

    def sem(self, name):
        s = Sem(self.stack.enter_context(self.nc.semaphore(name)), name)
        self._sems.append(s)
        return s

    def add(self, eng, fn, waits=(), incs=()):
        self.n_ops += 1
        if self.max_ops is not None and self.n_ops > self.max_ops:
            for s, k in incs:
                s.n += k
            return
        ws = [(s, int(v)) for (s, v) in waits if v > 0]
        self.ops[eng].append((fn, ws, list(incs)))
        for s, k in incs:
            s.n += k

    def emit(self, block):
        for eng in ENGINES:
            ops = self.ops[eng]
            if not ops:
                continue

            def body(handle, ops=ops):
                waited = {}
                for fn, ws, incs in ops:
                    for s, v in ws:
                        if waited.get(s, 0) < v:
                            handle.wait_ge(s.h, v)
                            waited[s] = v
                    instr = fn(handle)
                    for s, k in incs:
                        instr.then_inc(s.h, k)

            getattr(block, eng)(body)


PERTURB = 3


def build_program(max_ops=None):
    nc = bass.Bass("TRN2", target_bir_lowering=False)

    x_in = nc.dram_tensor("x", [T, D], F32, kind="ExternalInput").ap()
    wq_in = nc.dram_tensor("wq", [D, D], F32, kind="ExternalInput").ap()
    wk_in = nc.dram_tensor("wk", [D, D], F32, kind="ExternalInput").ap()
    wv_in = nc.dram_tensor("wv", [D, D], F32, kind="ExternalInput").ap()
    wo_in = nc.dram_tensor("wo", [D, D], F32, kind="ExternalInput").ap()
    wg_in = nc.dram_tensor("wg", [HID, D], I32, kind="ExternalInput").ap()
    wu_in = nc.dram_tensor("wu", [HID, D], I32, kind="ExternalInput").ap()
    wd_in = nc.dram_tensor("wd", [D, HID], I32, kind="ExternalInput").ap()
    id_in = nc.dram_tensor("ident", [128, 128], BF16, kind="ExternalInput").ap()
    out = nc.dram_tensor("out", [T, D], F32, kind="ExternalOutput").ap()

    x_bf = nc.dram_tensor("x_bf", [T, D], BF16)
    wq_bf = nc.dram_tensor("wq_bf", [D, D], BF16)
    wo_bf = nc.dram_tensor("wo_bf", [D, D], BF16)
    wg_bf = nc.dram_tensor("wg_bf", [HID, D], BF16)
    wu_bf = nc.dram_tensor("wu_bf", [HID, D], BF16)
    wd_bf = nc.dram_tensor("wd_bf", [D, HID], BF16)
    wdt_bf = nc.dram_tensor("wdt_bf", [HID, D], BF16)
    x1_bf = nc.dram_tensor("x1_bf", [T, D], BF16)
    cc_in = nc.dram_tensor("cc_in", [128, NPAIR * VW], F32)
    cc_out = nc.dram_tensor("cc_out", [128, NPAIR * VW], F32)

    def ap3(t_ap, offset, dims):
        return bass.AP(t_ap.tensor, t_ap.offset + offset,
                       [list(t_ap.ap[0])] + [list(d) for d in dims])

    with contextlib.ExitStack() as st0:
        e0 = st0.enter_context
        p = Prog(nc, st0, max_ops=max_ops)

        ident = e0(nc.sbuf_tensor("ident_sb", [128, 128], BF16))
        wgt = e0(nc.sbuf_tensor("wgt", [128, KC * HID], BF16))
        wut = e0(nc.sbuf_tensor("wut", [128, KC * HID], BF16))
        xt = [e0(nc.sbuf_tensor(f"xt{i}", [128, KC * 512], BF16,
                                is_transpose_dest=True)) for i in range(2)]
        bd = e0(nc.sbuf_tensor("bd", [128, NPAIR * VW], BF16))
        wdt = [e0(nc.sbuf_tensor(f"wdt{i}", [128, D], BF16,
                                 is_transpose_dest=True))
               for i in range(2)]
        smalls = e0(nc.sbuf_tensor("smalls", [128, 64], F32))
        acc_kv = e0(nc.sbuf_tensor("acc_kv", [128, NPAIR * VW], F32))
        kvr = e0(nc.sbuf_tensor("kvr", [128, NPAIR * VW], F32))
        pb = [e0(nc.psum_tensor(f"pb{i}", [128, 512], F32)) for i in range(7)]

        if PERTURB == 3:
            p.sem("pad0")
        ld_xc = [p.sem(f"ld_xc{i}") for i in range(NSUP)]
        ld_wkv = p.sem("ld_wkv")
        ld_wqo = p.sem("ld_wqo")
        ld_wf = p.sem("ld_wf")
        mk_t = [p.sem("mk_t0"), p.sem("mk_t1")]
        mk_s = [p.sem("mk_s0"), p.sem("mk_s1")]
        ld_wq = p.sem("ld_wq")
        ld_wgt = p.sem("ld_wgt")
        ld_wut = p.sem("ld_wut")
        xt_sp = [p.sem("xt_s0"), p.sem("xt_s1")]
        ld_xfp = [p.sem("ld_xf0"), p.sem("ld_xf1")]
        ld_wdtp = [p.sem("ld_wdt0"), p.sem("ld_wdt1")]
        ld_x1jq = [p.sem(f"ld_x1j{i}") for i in range(4)]
        st_x1p = [p.sem("st_x10"), p.sem("st_x11")]
        st_outp = [p.sem("st_out0"), p.sem("st_out1")]
        cc = p.sem("cc")
        cc2 = p.sem("cc2")
        cc3 = p.sem("cc3")
        pe_k = p.sem("pe_k")
        pe_v = p.sem("pe_v")
        pe_kv = p.sem("pe_kv")
        pe_q = p.sem("pe_q")
        pe_num = p.sem("pe_num")
        pe_t = p.sem("pe_t")
        pe_wo = p.sem("pe_wo")
        pe_g = p.sem("pe_g")
        pe_u = p.sem("pe_u")
        pe_d = p.sem("pe_d")
        ve_km = p.sem("ve_km")
        ve_kf = p.sem("ve_kf")
        ve_acc = p.sem("ve_acc")
        ve_bd = p.sem("ve_bd")
        ve_qm = p.sem("ve_qm")
        ve_qf = p.sem("ve_qf")
        ve_div = p.sem("ve_div")
        ve_r1 = p.sem("ve_r1")
        ve_ln = p.sem("ve_ln")
        ve_h2 = p.sem("ve_h2")
        ve_r2 = p.sem("ve_r2")
        ve_ln2 = p.sem("ve_ln2")
        ve_bn = p.sem("ve_bn")
        ve_bs = p.sem("ve_bs")
        ve_dv = p.sem("ve_dv")
        ve_rc = p.sem("ve_rc")
        pe_ds = p.sem("pe_ds")
        ac_e = p.sem("ac_e")
        ac_vh = p.sem("ac_vh")
        ac_qe = p.sem("ac_qe")
        ac_tc = p.sem("ac_tc")
        ac_x1 = p.sem("ac_x1")
        ac_sg = p.sem("ac_sg")
        ac_ub = p.sem("ac_ub")
        ac_std = p.sem("ac_std")
        bar_a = p.sem("bar_a")
        pe_endA = p.sem("pe_endA")
        bar_b = p.sem("bar_b")
        dma_misc = p.sem("dma_misc")

        p.add("sync", lambda h: h.dma_start(out=ident[:], in_=id_in),
              incs=[(dma_misc, 16)])
        stA = contextlib.ExitStack()
        eA = stA.enter_context
        wk_sb = eA(nc.sbuf_tensor("wk_sb", [128, KC * D], BF16))
        wv_sb = eA(nc.sbuf_tensor("wv_sb", [128, KC * D], BF16))

        def w_to_sb(h, dst, src):
            return h.dma_start(
                out=dst[:].rearrange("p (k d) -> p k d", k=KC),
                in_=src.rearrange("(k p) d -> p k d", p=128))

        def x_cast(s):
            p.add("gpsimd",
                  lambda h, s=s: h.dma_start(
                      out=x_bf.ap()[512 * s:512 * (s + 1), :],
                      in_=x_in[512 * s:512 * (s + 1), :]),
                  incs=[(ld_xc[s], 16)])

        p.add("gpsimd", lambda h: w_to_sb(h, wk_sb, wk_in), incs=[(ld_wkv, 16)])
        p.add("gpsimd", lambda h: w_to_sb(h, wv_sb, wv_in), incs=[(ld_wkv, 16)])
        for s in range(NSUP):
            x_cast(s)
        p.add("gpsimd", lambda h: h.dma_start(out=wq_bf.ap(), in_=wq_in),
              incs=[(ld_wqo, 16)])
        p.add("gpsimd", lambda h: h.dma_start(out=wo_bf.ap(), in_=wo_in),
              incs=[(ld_wqo, 16)])
        p.add("gpsimd", lambda h: h.dma_start(out=wg_bf.ap(), in_=wg_in),
              incs=[(ld_wf, 16)])
        p.add("gpsimd", lambda h: h.dma_start(out=wu_bf.ap(), in_=wu_in),
              incs=[(ld_wf, 16)])
        p.add("gpsimd", lambda h: h.dma_start(out=wd_bf.ap(), in_=wd_in),
              incs=[(ld_wf, 16)])
        xt_need_p = {}
        v_done = {}

        def add_xtA(s):
            for k in range(KC):
                p.add("sync",
                      lambda h, s=s, k=k: h.dma_start(
                          out=xt[s % 2][:, 512 * k:512 * (k + 1)],
                          in_=x_bf.ap()[512 * s:512 * (s + 1),
                                        128 * k:128 * (k + 1)],
                          transpose=True),
                      waits=[(ld_xc[s], 16),
                             (pe_v, v_done.get(s - 2, 0))],
                      incs=[(xt_sp[s % 2], 16)])
            xt_need_p[s] = xt_sp[s % 2].n

        add_xtA(0)
        add_xtA(1)

        deferred_tr = []
        for k in range(KC):
            deferred_tr.append(lambda k=k: p.add(
                "sync",
                lambda h: h.dma_start(
                    out=wgt[:, HID * k:HID * (k + 1)],
                    in_=wg_bf.ap()[:, 128 * k:128 * (k + 1)], transpose=True),
                waits=[(ld_wf, 48)], incs=[(ld_wgt, 16)]))
        for k in range(KC):
            deferred_tr.append(lambda k=k: p.add(
                "sync",
                lambda h: h.dma_start(
                    out=wut[:, HID * k:HID * (k + 1)],
                    in_=wu_bf.ap()[:, 128 * k:128 * (k + 1)], transpose=True),
                waits=[(ld_wf, 48)], incs=[(ld_wut, 16)]))

        def mat_strip(k):
            q = k % 2
            p.add("sync",
                  lambda h, k=k, q=q: h.dma_start(
                      out=wdt[q][:],
                      in_=wd_bf.ap()[:, 128 * k:128 * (k + 1)], transpose=True),
                  waits=[(ld_wf, 48), (mk_s[q], mk_s[q].n)],
                  incs=[(mk_t[q], 16)])
            p.add("scalar",
                  lambda h, k=k, q=q: h.dma_start(
                      out=wdt_bf.ap()[128 * k:128 * (k + 1), :],
                      in_=wdt[q][:]),
                  waits=[(mk_t[q], mk_t[q].n)], incs=[(mk_s[q], 16)])
        for k in range(MC):
            deferred_tr.append(lambda k=k: mat_strip(k))

        with stA:
            kf_sb = eA(nc.sbuf_tensor("kf_sb", [128, 8 * D], BF16))
            vh_sb = eA(nc.sbuf_tensor("vh_sb", [128, 8 * NPAIR * VW], BF16))
            ktmp = [eA(nc.sbuf_tensor(f"ktmp{i}", [128, 512], BF16))
                    for i in range(2)]
            etmp = [eA(nc.sbuf_tensor(f"etmp{i}", [128, 512], BF16))
                    for i in range(2)]

            if PERTURB == 4:
                p.add("vector", lambda h: h.memset(smalls[:, 34:35], 0.0))
            p.add("vector", lambda h: h.memset(smalls[:, 62:63], EPS_LN),
                  incs=[(ve_bn, 1)])
            p.add("vector", lambda h: h.memset(acc_kv[:], 0.0),
                  incs=[(ve_acc, 1)])
            p.add("vector", lambda h: h.memset(bd[:], 0.0), incs=[(ve_bd, 1)])
            p.add("vector",
                  lambda h: h.memset(
                      ap3(vh_sb[:], 128, [[VW * NPAIR, 8], [VW, NPAIR], [1, 1]]),
                      1.0),
                  incs=[(ve_acc, 1)])

            kf_done = {}
            vh_done = {}
            k_banks = [(pb[0], pb[1]), (pb[4], pb[5])]
            v_banks = (pb[2], pb[3])

            for j in range(NJ):
                s, jq = j // 4, j % 4
                jj = j % 8
                if jq == 0 and s >= 2:
                    add_xtA(s)
                if jq == 0 and s >= 3:
                    for _ in range(8):
                        if deferred_tr:
                            deferred_tr.pop(0)()
                kb = k_banks[j % 2]
                for k in range(KC):
                    for n in range(2):
                        p.add("tensor",
                              lambda h, j=j, k=k, n=n, kb=kb: h.matmul(
                                  kb[n][:],
                                  xt[(j // 4) % 2][:, 512 * k + 128 * (j % 4):
                                                   512 * k + 128 * (j % 4) + 128],
                                  wk_sb[:, D * k + 512 * n:D * k + 512 * (n + 1)],
                                  start=(k == 0), stop=(k == KC - 1)),
                              waits=[(xt_sp[s % 2], xt_need_p[s]),
                                     (ld_wkv, 32),
                                     (ve_kf, kf_done.get((j - 2, n), 0))],
                              incs=([(pe_k, 1)] if k == KC - 1 else []))
                for k in range(KC):
                    for n in range(2):
                        p.add("tensor",
                              lambda h, j=j, k=k, n=n: h.matmul(
                                  v_banks[n][:],
                                  xt[(j // 4) % 2][:, 512 * k + 128 * (j % 4):
                                                   512 * k + 128 * (j % 4) + 128],
                                  wv_sb[:, D * k + 512 * n:D * k + 512 * (n + 1)],
                                  start=(k == 0), stop=(k == KC - 1)),
                              waits=[(ld_wkv, 32),
                                     (ac_vh, vh_done.get((j - 1, n), 0))],
                              incs=([(pe_v, 1)] if k == KC - 1 else []))
                if jq == 3:
                    v_done[s] = pe_v.n
                for n in range(2):
                    p.add("vector",
                          lambda h, n=n, kb=kb: h.tensor_scalar_min(
                              ktmp[n][:], kb[n][:], 0.0),
                          waits=[(pe_k, pe_k.n - (1 - n)),
                                 (ac_e, ac_e.n - 2 + n if j > 0 or n > 0 else 0)],
                          incs=[(ve_km, 1)])
                    p.add("scalar",
                          lambda h, n=n: h.activation(
                              etmp[n][:], ktmp[n][:], AF.Exp),
                          waits=[(ve_km, ve_km.n),
                                 (ve_kf, ve_kf.n - 1 if (j, n) != (0, 0) else 0)],
                          incs=[(ac_e, 1)])
                    p.add("vector",
                          lambda h, jj=jj, n=n, kb=kb: h.scalar_tensor_tensor(
                              kf_sb[:, D * jj + 512 * n:D * jj + 512 * n + 512],
                              kb[n][:], 0.0, etmp[n][:], ALU.max, ALU.add),
                          waits=[(ac_e, ac_e.n)],
                          incs=[(ve_kf, 1)])
                    kf_done[(j, n)] = ve_kf.n
                    p.add("scalar",
                          lambda h, jj=jj, n=n: h.activation(
                              ap3(vh_sb[:], VW * NPAIR * jj + VW * 4 * n,
                                  [[VW, 4], [1, 128]]),
                              v_banks[n][:], AF.Copy),
                          waits=[(pe_v, pe_v.n - (1 - n)),
                                 (pe_kv, pe_kv.n)],
                          incs=[(ac_vh, 1)])
                    vh_done[(j, n)] = ac_vh.n

                if jj == 7:
                    kf_need = ve_kf.n
                    vh_need = ac_vh.n
                    for pr in range(NPAIR):
                        for t in range(8):
                            p.add("tensor",
                                  lambda h, pr=pr, t=t: h.matmul(
                                      pb[6][:, 0:129],
                                      kf_sb[:, D * t + 128 * pr:
                                            D * t + 128 * pr + 128],
                                      vh_sb[:, VW * NPAIR * t + VW * pr:
                                            VW * NPAIR * t + VW * pr + 129],
                                      start=(t == 0), stop=(t == 7)),
                                  waits=[(ve_kf, kf_need), (ac_vh, vh_need),
                                         (ve_acc, ve_acc.n)],
                                  incs=([(pe_kv, 1)] if t == 7 else []))
                        p.add("vector",
                              lambda h, pr=pr: h.tensor_tensor(
                                  acc_kv[:, VW * pr:VW * pr + 129],
                                  acc_kv[:, VW * pr:VW * pr + 129],
                                  pb[6][:, 0:129], ALU.add),
                              waits=[(pe_kv, pe_kv.n), (ve_acc, ve_acc.n)],
                              incs=[(ve_acc, 1)])

            p.add("gpsimd",
                  lambda h: h.dma_start(out=cc_in.ap(), in_=acc_kv[:]),
                  waits=[(ve_acc, ve_acc.n)], incs=[(cc, 16)])
            p.add("gpsimd",
                  lambda h: h.collective_compute(
                      "AllReduce", ALU.add,
                      replica_groups=[[0, 1], [2, 3], [4, 5], [6, 7]],
                      ins=[cc_in.ap().opt()], outs=[cc_out.ap().opt()]),
                  waits=[(cc, 16)], incs=[(cc2, 1)])
            p.add("gpsimd",
                  lambda h: h.dma_start(out=kvr[:], in_=cc_out.ap()),
                  waits=[(cc2, 1)], incs=[(cc3, 16)])
            for pr in range(NPAIR):
                o = VW * pr
                p.add("vector", lambda h, o=o: h.tensor_copy(
                    bd[0:64, o:o + 64], kvr[0:64, o:o + 64]),
                    waits=[(cc3, 16), (ve_bd, 1)])
                p.add("vector", lambda h, o=o: h.tensor_copy(
                    bd[0:64, o + 64:o + 65], kvr[0:64, o + 128:o + 129]))
                p.add("vector", lambda h, o=o: h.tensor_copy(
                    bd[64:128, o + 65:o + 129], kvr[64:128, o + 64:o + 128]))
                p.add("vector", lambda h, o=o: h.tensor_copy(
                    bd[64:128, o + 129:o + 130], kvr[64:128, o + 128:o + 129]),
                    incs=[(ve_bd, 1)])

            p.add("tensor", lambda h: h.nop())
            p.add("tensor", lambda h: h.nop(), incs=[(pe_endA, 1)])

        with contextlib.ExitStack() as stB:
            eB = stB.enter_context
            pt = eB(nc.psum_tensor("pt", [128, 1024], BF16))
            wq_sb = eB(nc.sbuf_tensor("wq_sb", [128, KC * D], BF16))
            wo_sb = eB(nc.sbuf_tensor("wo_sb", [128, KC * D], BF16))
            qf_sb = eB(nc.sbuf_tensor("qf_sb", [128, KC * 512], BF16))
            qtmp = [eB(nc.sbuf_tensor(f"qtmp{i}", [128, 512], BF16))
                    for i in range(2)]
            eqtmp = [eB(nc.sbuf_tensor(f"eqtmp{i}", [128, 512], BF16))
                     for i in range(2)]
            attn = [eB(nc.sbuf_tensor(f"attn{i}", [128, D], BF16))
                    for i in range(2)]
            attnT = [eB(nc.sbuf_tensor(f"attnT{i}", [128, D], BF16))
                     for i in range(2)]
            xf = [eB(nc.sbuf_tensor(f"xf{i}", [128, D], F32))
                  for i in range(2)]
            r1 = [eB(nc.sbuf_tensor(f"r1_{i}", [128, D], F32))
                  for i in range(2)]
            x1 = [eB(nc.sbuf_tensor(f"x1_{i}", [128, D], BF16))
                  for i in range(2)]

            BW = [(pe_endA, 1), (ve_kf, ve_kf.n), (ac_vh, ac_vh.n),
                  (ac_e, ac_e.n), (ve_km, ve_km.n), (ve_acc, ve_acc.n)]
            p.add("sync", lambda h: w_to_sb(h, wq_sb, wq_bf.ap()),
                  waits=BW + [(ld_wqo, 32)], incs=[(ld_wq, 16)])
            p.add("sync", lambda h: w_to_sb(h, wo_sb, wo_bf.ap()),
                  waits=BW + [(ld_wqo, 32)], incs=[(ld_wq, 16)])

            div_done = {}
            tc_done = {}
            wo_done = {}
            r1_done = {}
            x1_done = {}
            xf_need = {}
            num_banks = [(pb[2], 3, 0), (pb[3], 3, 3), (pb[4], 2, 6)]
            q_done = {}

            ln_state = {"i": 0, "applies": []}

            def add_ln(src_buf, dst_buf, src_sem, war_waits, store_fn):
                i = ln_state["i"]
                base = 46 + 6 * (i % 2)
                mv, std, inv, ng = base, base + 2, base + 3, base + 4
                guard = (ln_state["applies"][i - 2] if i >= 2 else 0)
                p.add("vector", lambda h: h.bn_stats(
                    smalls[:, 0:6], src_buf[:, 0:512]),
                    waits=[src_sem, (ve_bn, ve_bn.n)],
                    incs=[(ve_bs, 1)])
                p.add("vector", lambda h: h.bn_stats(
                    smalls[:, 6:12], src_buf[:, 512:1024]),
                    waits=[src_sem], incs=[(ve_bs, 1)])
                p.add("vector", lambda h: h.bn_aggr(
                    smalls[:, mv:mv + 2], smalls[:, 0:12]),
                    waits=[(ve_bs, ve_bs.n), (ac_x1, guard)],
                    incs=[(ve_bn, 1)])
                p.add("scalar", lambda h: h.activation(
                    smalls[:, std:std + 1], smalls[:, mv + 1:mv + 2], AF.Sqrt,
                    bias=smalls[:, 62:63]),
                    waits=[(ve_bn, ve_bn.n)], incs=[(ac_std, 1)])
                p.add("vector", lambda h: h.reciprocal(
                    smalls[:, inv:inv + 1], smalls[:, std:std + 1]),
                    waits=[(ac_std, ac_std.n)], incs=[(ve_rc, 1)])
                p.add("vector", lambda h: h.scalar_tensor_tensor(
                    smalls[:, ng:ng + 1], smalls[:, mv:mv + 1], -1.0,
                    smalls[:, inv:inv + 1], ALU.mult, ALU.mult),
                    waits=[(ve_rc, ve_rc.n), (ve_bn, ve_bn.n)],
                    incs=[(ve_ln, 1)])
                p.add("scalar",
                      lambda h: h.activation(
                          dst_buf[:], src_buf[:], AF.Identity,
                          scale=smalls[:, inv:inv + 1],
                          bias=smalls[:, ng:ng + 1]),
                      waits=[(ve_ln, ve_ln.n)] + war_waits,
                      incs=[(ac_x1, 1)])
                ln_state["applies"].append(ac_x1.n)
                ln_state["i"] = i + 1
                store_fn(ac_x1.n)

            for s in range(NSUP):
                for k in range(KC):
                    p.add("sync",
                          lambda h, s=s, k=k: h.dma_start(
                              out=xt[s % 2][:, 512 * k:512 * (k + 1)],
                              in_=x_bf.ap()[512 * s:512 * (s + 1),
                                            128 * k:128 * (k + 1)],
                              transpose=True),
                          waits=BW + [(ld_xc[s], 16),
                                      (pe_q, q_done.get(s - 2, 0))],
                          incs=[(xt_sp[s % 2], 16)])
                xt_need = xt_sp[s % 2].n
                for m in range(KC):
                    qb = pb[m % 2]
                    for k in range(KC):
                        p.add("tensor",
                              lambda h, m=m, k=k, s=s, qb=qb: h.matmul(
                                  qb[:],
                                  wq_sb[:, D * k + 128 * m:D * k + 128 * m + 128],
                                  xt[s % 2][:, 512 * k:512 * (k + 1)],
                                  start=(k == 0), stop=(k == KC - 1)),
                              waits=[(xt_sp[s % 2], xt_need), (ld_wq, 32),
                                     (ve_qf, max(0, ve_qf.n - 1))],
                              incs=([(pe_q, 1)] if k == KC - 1 else []))
                    if m == KC - 1:
                        q_done[s] = pe_q.n
                    p.add("vector",
                          lambda h, m=m, qb=qb: h.tensor_scalar_min(
                              qtmp[m % 2][:], qb[:], 0.0),
                          waits=[(pe_q, pe_q.n),
                                 (ac_qe, max(0, ac_qe.n - 1))],
                          incs=[(ve_qm, 1)])
                    p.add("scalar",
                          lambda h, m=m: h.activation(
                              eqtmp[m % 2][:], qtmp[m % 2][:], AF.Exp),
                          waits=[(ve_qm, ve_qm.n),
                                 (ve_qf, max(0, ve_qf.n - 1))],
                          incs=[(ac_qe, 1)])
                    p.add("vector",
                          lambda h, m=m, qb=qb: h.scalar_tensor_tensor(
                              qf_sb[:, 512 * m:512 * (m + 1)],
                              qb[:], 0.0, eqtmp[m % 2][:], ALU.max, ALU.add),
                          waits=[(ac_qe, ac_qe.n)],
                          incs=[(ve_qf, 1)])
                qf_need = ve_qf.n

                num_done = {}

                def stage_xf_num(jq, s=s):
                    prev = (s, jq - 2) if jq >= 2 else (s - 1, jq + 2)
                    p.add("gpsimd",
                          lambda h: h.dma_start(
                              out=xf[jq % 2][:],
                              in_=x_in[512 * s + 128 * jq:
                                       512 * s + 128 * jq + 128, :]),
                          waits=BW + [(ve_r1, r1_done.get(prev, 0))],
                          incs=[(ld_xfp[jq % 2], 16)])
                    xf_need[(s, jq)] = ld_xfp[jq % 2].n
                    for bank, cnt, p0 in num_banks:
                        for q in range(cnt):
                            pr = p0 + q
                            p.add("tensor",
                                  lambda h, bank=bank, q=q, pr=pr: h.matmul(
                                      bank[:, VW * q:VW * q + VW],
                                      qf_sb[:, 512 * pr + 128 * jq:
                                            512 * pr + 128 * jq + 128],
                                      bd[:, VW * pr:VW * (pr + 1)],
                                      start=True, stop=True),
                                  waits=[(ve_qf, qf_need), (ve_bd, 9),
                                         (ve_div, div_done.get((s, jq - 1), 0))],
                                  incs=([(pe_num, 1)] if q == cnt - 1 else []))
                    num_done[(s, jq)] = pe_num.n

                def stage_div(jq, s=s):
                    for bank, cnt, p0 in num_banks:
                        p.add("vector",
                              lambda h, bank=bank, cnt=cnt, p0=p0:
                              h.tensor_scalar(
                                  smalls[:, 12 + 2 * p0:12 + 2 * (p0 + cnt)],
                                  ap3(bank[:], 64, [[VW, cnt], [65, 2]]),
                                  EPS_ATTN, None, ALU.add),
                              waits=[(pe_num, num_done[(s, jq)]),
                                     (ve_rc, ve_rc.n)],
                              incs=[(ve_dv, 1)])
                    dv_now = ve_dv.n
                    for bank, cnt, p0 in num_banks:
                        p.add("vector",
                              lambda h, cnt=cnt, p0=p0: h.reciprocal(
                                  smalls[:, 28 + 2 * p0:28 + 2 * (p0 + cnt)],
                                  smalls[:, 12 + 2 * p0:12 + 2 * (p0 + cnt)]),
                              waits=[(ve_dv, dv_now),
                                     (ve_div, ve_div.n)],
                              incs=[(ve_rc, 1)])
                    rc_now = ve_rc.n
                    for bank, cnt, p0 in num_banks:
                        p.add("vector",
                              lambda h, bank=bank, cnt=cnt, p0=p0:
                              h.scalar_tensor_tensor(
                                  ap3(attn[jq % 2][:], 128 * p0,
                                      [[128, cnt], [64, 2], [1, 64]]),
                                  ap3(bank[:], 0, [[VW, cnt], [65, 2], [1, 64]]),
                                  1.0,
                                  ap3(smalls[:], 28 + 2 * p0,
                                      [[2, cnt], [1, 2], [0, 64]]),
                                  ALU.mult, ALU.mult),
                              waits=[(ve_rc, rc_now),
                                     (ac_tc, tc_done.get((s, jq - 2), 0))],
                              incs=[(ve_div, 1)])
                    div_done[(s, jq)] = ve_div.n

                def stage_tail(jq, s=s):
                    for k in range(KC):
                        p.add("tensor",
                              lambda h, k=k: h.transpose(
                                  pt[:, 128 * k:128 * (k + 1)],
                                  attn[jq % 2][:, 128 * k:128 * (k + 1)],
                                  ident[:]),
                              waits=[(ve_div, div_done[(s, jq)]),
                                     (ac_tc, tc_done.get((s, jq - 1), 0)),
                                     (dma_misc, 16)],
                              incs=([(pe_t, 1)] if k == KC - 1 else []))
                    if jq + 1 < 4:
                        stage_xf_num(jq + 1)
                        stage_div(jq + 1)
                    p.add("scalar",
                          lambda h: h.activation(
                              attnT[jq % 2][:], pt[:], AF.Copy),
                          waits=[(pe_t, pe_t.n),
                                 (pe_wo, wo_done.get((s, jq - 2), 0))],
                          incs=[(ac_tc, 1)])
                    tc_done[(s, jq)] = ac_tc.n
                    for k in range(KC):
                        for n in range(2):
                            p.add("tensor",
                                  lambda h, k=k, n=n: h.matmul(
                                      pb[5 + n][:],
                                      attnT[jq % 2][:, 128 * k:128 * (k + 1)],
                                      wo_sb[:, D * k + 512 * n:
                                            D * k + 512 * (n + 1)],
                                      start=(k == 0), stop=(k == KC - 1)),
                                  waits=[(ac_tc, ac_tc.n), (ld_wq, 32),
                                         (ve_r1, r1_done.get((s, jq - 1), 0))],
                                  incs=([(pe_wo, 1)] if k == KC - 1 else []))
                    wo_done[(s, jq)] = pe_wo.n
                    for n in range(2):
                        p.add("vector",
                              lambda h, n=n: h.scalar_tensor_tensor(
                                  r1[jq % 2][:, 512 * n:512 * (n + 1)],
                                  pb[5 + n][:], 1.0,
                                  xf[jq % 2][:, 512 * n:512 * (n + 1)],
                                  ALU.mult, ALU.add),
                              waits=[(pe_wo, pe_wo.n),
                                     (ld_xfp[jq % 2], xf_need[(s, jq)]),
                                     (ac_x1, x1_done.get((s, jq - 2), 0))],
                              incs=[(ve_r1, 1)])
                    r1_done[(s, jq)] = ve_r1.n
                    if jq >= 1:
                        stage_ln1(jq - 1)

                def stage_ln1(jq, s=s):
                    def store_x1(apply_n):
                        p.add("scalar",
                              lambda h: h.dma_start(
                                  out=x1_bf.ap()[512 * s + 128 * jq:
                                                 512 * s + 128 * jq + 128, :],
                                  in_=x1[jq % 2][:]),
                              waits=[(ac_x1, apply_n)],
                              incs=[(st_x1p[jq % 2], 16)])
                    add_ln(r1[jq % 2], x1[jq % 2],
                           (ve_r1, r1_done[(s, jq)]),
                           [(st_x1p[jq % 2], st_x1p[jq % 2].n)], store_x1)
                    x1_done[(s, jq)] = ac_x1.n

                stage_xf_num(0)
                stage_div(0)
                for jq in range(4):
                    stage_tail(jq)
                stage_ln1(3)
            p.add("tensor", lambda h: h.transpose(
                pt[:, 0:128], attn[1][:, 0:128], ident[:]),
                waits=[(ac_tc, ac_tc.n)], incs=[(bar_b, 1)])
            p.add("vector", lambda h: h.nop(),
                  waits=[(ac_x1, ac_x1.n)], incs=[(bar_b, 1)])
            p.add("scalar", lambda h: h.nop(),
                  incs=[(bar_b, 1)])

        with contextlib.ExitStack() as stC:
            eC = stC.enter_context
            pb7 = eC(nc.psum_tensor("pb7", [128, 512], F32))
            h2t = eC(nc.sbuf_tensor("h2t", [128, MC * 512], BF16))
            sg = [eC(nc.sbuf_tensor(f"sg{i}", [128, 512], BF16))
                  for i in range(2)]
            ub = [eC(nc.sbuf_tensor(f"ub{i}", [128, 512], BF16))
                  for i in range(2)]
            x1j = [eC(nc.sbuf_tensor(f"x1j{i}", [128, D], BF16))
                   for i in range(4)]
            dbanks = pb + [pb7]
            r2 = [eC(nc.sbuf_tensor(f"r2_{i}", [128, D], F32))
                  for i in range(2)]
            x2 = [eC(nc.sbuf_tensor(f"x2_{i}", [128, D], F32))
                  for i in range(2)]

            CW = [(bar_b, 3)]
            u_done = {}
            h2_done = {}
            h2_seq = []
            r2_done = {}
            x2_done = {}
            x1j_need = {}
            strip_done = []

            x1t_need_p = {}

            def add_x1t(s):
                sn = [(st_x1p[0], 32 * (s + 1)), (st_x1p[1], 32 * (s + 1))]
                for k in range(KC):
                    p.add("sync",
                          lambda h, s=s, k=k: h.dma_start(
                              out=xt[s % 2][:, 512 * k:512 * (k + 1)],
                              in_=x1_bf.ap()[512 * s:512 * (s + 1),
                                             128 * k:128 * (k + 1)],
                              transpose=True),
                          waits=CW + sn + [(pe_u, u_done.get(s - 2, 0))],
                          incs=[(xt_sp[s % 2], 16)])
                x1t_need_p[s] = xt_sp[s % 2].n

            add_x1t(0)
            for s in range(NSUP):
                st_need = [(st_x1p[0], 32 * (s + 1)), (st_x1p[1], 32 * (s + 1))]
                x1t_need = x1t_need_p[s]
                for m in range(MC):
                    gb, ubk = pb[m % 2 * 2], pb[m % 2 * 2 + 1]
                    gw = ([(ve_r2, r2_done.get((s - 1, m), 0))]
                          if m < 2 else [])
                    for k in range(KC):
                        p.add("tensor",
                              lambda h, m=m, k=k, s=s, gb=gb: h.matmul(
                                  gb[:],
                                  wgt[:, HID * k + 128 * m:HID * k + 128 * m + 128],
                                  xt[s % 2][:, 512 * k:512 * (k + 1)],
                                  start=(k == 0), stop=(k == KC - 1)),
                              waits=CW + [(xt_sp[s % 2], x1t_need),
                                          (ld_wgt, 128),
                                          (ac_sg, max(0, ac_sg.n - 1))] +
                              (gw if k == 0 else []),
                              incs=([(pe_g, 1)] if k == KC - 1 else []))
                    for k in range(KC):
                        p.add("tensor",
                              lambda h, m=m, k=k, s=s, ubk=ubk: h.matmul(
                                  ubk[:],
                                  wut[:, HID * k + 128 * m:HID * k + 128 * m + 128],
                                  xt[s % 2][:, 512 * k:512 * (k + 1)],
                                  start=(k == 0), stop=(k == KC - 1)),
                              waits=CW + [(ld_wut, 128),
                                          (ac_ub, max(0, ac_ub.n - 1))] +
                              (gw if k == 0 else []),
                              incs=([(pe_u, 1)] if k == KC - 1 else []))
                    h2_war = h2_seq[-2] if len(h2_seq) >= 2 else 0
                    p.add("scalar",
                          lambda h, m=m, gb=gb: h.activation(
                              sg[m % 2][:], gb[:], AF.Silu, scale=S127),
                          waits=[(pe_g, pe_g.n), (ve_h2, h2_war)],
                          incs=[(ac_sg, 1)])
                    p.add("scalar",
                          lambda h, m=m, ubk=ubk: h.activation(
                              ub[m % 2][:], ubk[:], AF.Copy),
                          waits=[(pe_u, pe_u.n), (ve_h2, h2_war)],
                          incs=[(ac_ub, 1)])
                    p.add("vector",
                          lambda h, m=m: h.tensor_tensor(
                              h2t[:, 512 * m:512 * (m + 1)],
                              sg[m % 2][:], ub[m % 2][:], ALU.mult),
                          waits=[(ac_sg, ac_sg.n), (ac_ub, ac_ub.n),
                                 (pe_d, 4 * s)],
                          incs=[(ve_h2, 1)])
                    h2_done[(s, m)] = ve_h2.n
                    h2_seq.append(ve_h2.n)
                u_done[s] = pe_u.n
                if s + 1 < NSUP:
                    add_x1t(s + 1)
                for jq in range(4):
                    p.add("scalar",
                          lambda h, s=s, jq=jq: h.dma_start(
                              out=x1j[jq][:],
                              in_=x1_bf.ap()[512 * s + 128 * jq:
                                             512 * s + 128 * jq + 128, :]),
                          waits=CW + st_need +
                          [(ve_r2, r2_done.get((s - 1, jq), 0))],
                          incs=[(ld_x1jq[jq], 16)])
                    x1j_need[(s, jq)] = ld_x1jq[jq].n
                pe_d_base = pe_d.n
                for k in range(MC):
                    g = len(strip_done)
                    strip_war = (strip_done[g - 2] if g >= 2
                                 else (pe_ds, 0))
                    p.add("gpsimd",
                          lambda h, k=k: h.dma_start(
                              out=wdt[k % 2][:],
                              in_=wdt_bf.ap()[128 * k:128 * (k + 1), :]),
                          waits=CW + [(mk_s[0], 16 * (MC // 2)),
                                      (mk_s[1], 16 * (MC // 2)),
                                      strip_war],
                          incs=[(ld_wdtp[k % 2], 16)])
                    wdt_need = ld_wdtp[k % 2].n
                    for n in range(2):
                        for jq in range(4):
                            xw = []
                            if k == 0:
                                xw = [(ac_sg, ac_sg.n), (ac_ub, ac_ub.n),
                                      (ve_r2, r2_done.get((s - 1, 3), 0))]
                            incs = []
                            if k == MC - 1 and n == 1:
                                incs = [(pe_d, 1)]
                            elif n == 1 and jq == 3:
                                incs = [(pe_ds, 1)]
                            elif k == MC - 1 and n == 0 and jq == 3:
                                incs = [(pe_ds, 1)]
                            p.add("tensor",
                                  lambda h, k=k, jq=jq, n=n: h.matmul(
                                      dbanks[2 * jq + n][:],
                                      h2t[:, 512 * k + 128 * jq:
                                           512 * k + 128 * jq + 128],
                                      wdt[k % 2][:, 512 * n:512 * (n + 1)],
                                      start=(k == 0), stop=(k == MC - 1)),
                                  waits=[(ve_h2, h2_done[(s, k)]),
                                         (ld_wdtp[k % 2], wdt_need)] + xw,
                                  incs=incs)
                    strip_done.append((pe_d, pe_d_base + 4) if k == MC - 1
                                      else (pe_ds, pe_ds.n))
                for jq in range(4):
                    for n in range(2):
                        p.add("vector",
                              lambda h, jq=jq, n=n:
                              h.scalar_tensor_tensor(
                                  r2[jq % 2][:, 512 * n:512 * (n + 1)],
                                  dbanks[2 * jq + n][:], S127SQ,
                                  x1j[jq][:, 512 * n:512 * (n + 1)],
                                  ALU.mult, ALU.add),
                              waits=[(pe_d, pe_d_base + jq + 1),
                                     (ld_x1jq[jq], x1j_need[(s, jq)]),
                                     (ac_x1, x2_done.get((s, jq - 2), 0))],
                              incs=[(ve_r2, 1)])
                    r2_done[(s, jq)] = ve_r2.n

                    def store_out(apply_n, s=s, jq=jq):
                        p.add("scalar",
                              lambda h: h.dma_start(
                                  out=out[512 * s + 128 * jq:
                                          512 * s + 128 * jq + 128, :],
                                  in_=x2[jq % 2][:]),
                              waits=[(ac_x1, apply_n)],
                              incs=[(st_outp[jq % 2], 16)])
                    add_ln(r2[jq % 2], x2[jq % 2], (ve_r2, ve_r2.n),
                           [(st_outp[jq % 2], st_outp[jq % 2].n)],
                           store_out)
                    x2_done[(s, jq)] = ac_x1.n

            p.add("scalar", lambda h: h.nop(),
                  waits=[(st_outp[0], 16 * NJ // 2),
                         (st_outp[1], 16 * NJ // 2)])
            if PERTURB == 1:
                p.add("scalar", lambda h: h.nop())
            elif PERTURB == 2:
                p.add("scalar", lambda h: h.nop())
                p.add("vector", lambda h: h.nop())
            elif PERTURB == 5:
                p.add("tensor", lambda h: h.nop())
            elif PERTURB == 6:
                for _ in range(3):
                    p.add("scalar", lambda h: h.nop())
                    p.add("vector", lambda h: h.nop())
                    p.add("tensor", lambda h: h.nop())

        with nc.Block() as block:
            p.emit(block)

    nc._prog_n_ops = p.n_ops
    return nc


_NC_CACHE = {}


def kernel(L0_attn_x, L0_attn_Wq, L0_attn_Wk, L0_attn_Wv, L0_attn_Wo,
           L0_ffn_w_gate, L0_ffn_w_up, L0_ffn_w_down):
    x = np.ascontiguousarray(np.asarray(L0_attn_x, dtype=np.float32))
    wq = np.ascontiguousarray(np.asarray(L0_attn_Wq, dtype=np.float32))
    wk = np.ascontiguousarray(np.asarray(L0_attn_Wk, dtype=np.float32))
    wv = np.ascontiguousarray(np.asarray(L0_attn_Wv, dtype=np.float32))
    wo = np.ascontiguousarray(np.asarray(L0_attn_Wo, dtype=np.float32))
    wg = np.ascontiguousarray(np.asarray(L0_ffn_w_gate, dtype=np.int32))
    wu = np.ascontiguousarray(np.asarray(L0_ffn_w_up, dtype=np.int32))
    wd = np.ascontiguousarray(np.asarray(L0_ffn_w_down, dtype=np.int32))
    ident = np.eye(128, dtype=ml_dtypes.bfloat16)

    if "nc" not in _NC_CACHE:
        _NC_CACHE["nc"] = build_program()
    nc = _NC_CACHE["nc"]

    in_maps = []
    for c in range(8):
        b, half = c // 2, c % 2
        xo = np.ascontiguousarray(x[b, half * T:(half + 1) * T, :])
        in_maps.append(dict(x=xo, wq=wq, wk=wk, wv=wv, wo=wo,
                            wg=wg, wu=wu, wd=wd, ident=ident))

    res = run_bass_kernel_spmd(nc, in_maps, core_ids=list(range(8)))

    outp = np.empty((4, 8192, 1024), dtype=np.float32)
    for c in range(8):
        b, half = c // 2, c % 2
        outp[b, half * T:(half + 1) * T, :] = np.asarray(res.results[c]["out"])
    return outp


if __name__ == "__main__":
    nc = build_program()
    print("built OK; instructions:", nc.n_instructions())


# version 16
# speedup vs baseline: 1.0509x; 1.0509x over previous
import contextlib
import numpy as np
import ml_dtypes
import concourse.bass as bass
import concourse.mybir as mybir
from concourse.bass_utils import run_bass_kernel_spmd

F32 = mybir.dt.float32
BF16 = mybir.dt.bfloat16
I32 = mybir.dt.int32
AF = mybir.ActivationFunctionType
ALU = mybir.AluOpType

T = 4096
D = 1024
HID = 3072
NSUP = 8
NJ = 32
KC = D // 128
MC = HID // 128
NPAIR = 8
VW = 130
EPS_ATTN = 1e-6
EPS_LN = 1e-5
S127 = 1.0 / 127.0
S127SQ = float(np.float32(1.0) / np.float32(127.0)) ** 2

ENGINES = ("sync", "tensor", "vector", "scalar", "gpsimd")


class Sem:
    __slots__ = ("h", "n", "name")

    def __init__(self, h, name):
        self.h = h
        self.n = 0
        self.name = name


class Prog:

    def __init__(self, nc, stack, max_ops=None):
        self.nc = nc
        self.stack = stack
        self.ops = {e: [] for e in ENGINES}
        self._sems = []
        self.n_ops = 0
        self.max_ops = max_ops

    def sem(self, name):
        s = Sem(self.stack.enter_context(self.nc.semaphore(name)), name)
        self._sems.append(s)
        return s

    def add(self, eng, fn, waits=(), incs=()):
        self.n_ops += 1
        if self.max_ops is not None and self.n_ops > self.max_ops:
            for s, k in incs:
                s.n += k
            return
        ws = [(s, int(v)) for (s, v) in waits if v > 0]
        self.ops[eng].append((fn, ws, list(incs)))
        for s, k in incs:
            s.n += k

    def emit(self, block):
        for eng in ENGINES:
            ops = self.ops[eng]
            if not ops:
                continue

            def body(handle, ops=ops):
                waited = {}
                for fn, ws, incs in ops:
                    for s, v in ws:
                        if waited.get(s, 0) < v:
                            handle.wait_ge(s.h, v)
                            waited[s] = v
                    instr = fn(handle)
                    for s, k in incs:
                        instr.then_inc(s.h, k)

            getattr(block, eng)(body)


PERTURB = 3


def build_program(max_ops=None):
    nc = bass.Bass("TRN2", target_bir_lowering=False)

    x_in = nc.dram_tensor("x", [T, D], F32, kind="ExternalInput").ap()
    wq_in = nc.dram_tensor("wq", [D, D], F32, kind="ExternalInput").ap()
    wk_in = nc.dram_tensor("wk", [D, D], F32, kind="ExternalInput").ap()
    wv_in = nc.dram_tensor("wv", [D, D], F32, kind="ExternalInput").ap()
    wo_in = nc.dram_tensor("wo", [D, D], F32, kind="ExternalInput").ap()
    wg_in = nc.dram_tensor("wg", [HID, D], I32, kind="ExternalInput").ap()
    wu_in = nc.dram_tensor("wu", [HID, D], I32, kind="ExternalInput").ap()
    wd_in = nc.dram_tensor("wd", [D, HID], I32, kind="ExternalInput").ap()
    id_in = nc.dram_tensor("ident", [128, 128], BF16, kind="ExternalInput").ap()
    out = nc.dram_tensor("out", [T, D], F32, kind="ExternalOutput").ap()

    x_bf = nc.dram_tensor("x_bf", [T, D], BF16)
    wq_bf = nc.dram_tensor("wq_bf", [D, D], BF16)
    wo_bf = nc.dram_tensor("wo_bf", [D, D], BF16)
    wg_bf = nc.dram_tensor("wg_bf", [HID, D], BF16)
    wu_bf = nc.dram_tensor("wu_bf", [HID, D], BF16)
    wd_bf = nc.dram_tensor("wd_bf", [D, HID], BF16)
    wdt_bf = nc.dram_tensor("wdt_bf", [HID, D], BF16)
    x1_bf = nc.dram_tensor("x1_bf", [T, D], BF16)
    cc_in = nc.dram_tensor("cc_in", [128, NPAIR * VW], F32)
    cc_out = nc.dram_tensor("cc_out", [128, NPAIR * VW], F32)

    def ap3(t_ap, offset, dims):
        return bass.AP(t_ap.tensor, t_ap.offset + offset,
                       [list(t_ap.ap[0])] + [list(d) for d in dims])

    with contextlib.ExitStack() as st0:
        e0 = st0.enter_context
        p = Prog(nc, st0, max_ops=max_ops)

        ident = e0(nc.sbuf_tensor("ident_sb", [128, 128], BF16))
        wgt = e0(nc.sbuf_tensor("wgt", [128, KC * HID], BF16))
        wut = e0(nc.sbuf_tensor("wut", [128, KC * HID], BF16))
        xt = [e0(nc.sbuf_tensor(f"xt{i}", [128, KC * 512], BF16,
                                is_transpose_dest=True)) for i in range(2)]
        bd = e0(nc.sbuf_tensor("bd", [128, NPAIR * VW], BF16))
        wdt = [e0(nc.sbuf_tensor(f"wdt{i}", [128, 2 * D], BF16,
                                 is_transpose_dest=True))
               for i in range(2)]
        smalls = e0(nc.sbuf_tensor("smalls", [128, 64], F32))
        acc_kv = e0(nc.sbuf_tensor("acc_kv", [128, NPAIR * VW], F32))
        kvr = e0(nc.sbuf_tensor("kvr", [128, NPAIR * VW], F32))
        pb = [e0(nc.psum_tensor(f"pb{i}", [128, 512], F32)) for i in range(7)]

        if PERTURB == 3:
            p.sem("pad0")
        ld_xc = [p.sem(f"ld_xc{i}") for i in range(NSUP)]
        ld_wkv = p.sem("ld_wkv")
        ld_wqo = p.sem("ld_wqo")
        ld_wf = p.sem("ld_wf")
        mk_t = [p.sem("mk_t0"), p.sem("mk_t1")]
        mk_s = [p.sem("mk_s0"), p.sem("mk_s1")]
        ld_wq = p.sem("ld_wq")
        ld_wgt = p.sem("ld_wgt")
        ld_wut = p.sem("ld_wut")
        xt_sp = [p.sem("xt_s0"), p.sem("xt_s1")]
        ld_xfp = [p.sem("ld_xf0"), p.sem("ld_xf1")]
        ld_wdtp = [p.sem("ld_wdt0"), p.sem("ld_wdt1")]
        ld_x1jq = [p.sem(f"ld_x1j{i}") for i in range(4)]
        st_x1p = [p.sem("st_x10"), p.sem("st_x11")]
        st_outp = [p.sem("st_out0"), p.sem("st_out1")]
        cc = p.sem("cc")
        cc2 = p.sem("cc2")
        cc3 = p.sem("cc3")
        pe_k = p.sem("pe_k")
        pe_v = p.sem("pe_v")
        pe_kv = p.sem("pe_kv")
        pe_q = p.sem("pe_q")
        pe_num = p.sem("pe_num")
        pe_t = p.sem("pe_t")
        pe_wo = p.sem("pe_wo")
        pe_g = p.sem("pe_g")
        pe_u = p.sem("pe_u")
        pe_d = p.sem("pe_d")
        ve_km = p.sem("ve_km")
        ve_kf = p.sem("ve_kf")
        ve_acc = p.sem("ve_acc")
        ve_bd = p.sem("ve_bd")
        ve_qm = p.sem("ve_qm")
        ve_qf = p.sem("ve_qf")
        ve_div = p.sem("ve_div")
        ve_r1 = p.sem("ve_r1")
        ve_ln = p.sem("ve_ln")
        ve_h2 = p.sem("ve_h2")
        ve_r2 = p.sem("ve_r2")
        ve_ln2 = p.sem("ve_ln2")
        ve_bn = p.sem("ve_bn")
        ve_bs = p.sem("ve_bs")
        ve_dv = p.sem("ve_dv")
        ve_rc = p.sem("ve_rc")
        pe_ds = p.sem("pe_ds")
        ac_e = p.sem("ac_e")
        ac_vh = p.sem("ac_vh")
        ac_qe = p.sem("ac_qe")
        ac_tc = p.sem("ac_tc")
        ac_x1 = p.sem("ac_x1")
        ac_sg = p.sem("ac_sg")
        ac_ub = p.sem("ac_ub")
        ac_std = p.sem("ac_std")
        bar_a = p.sem("bar_a")
        pe_endA = p.sem("pe_endA")
        bar_b = p.sem("bar_b")
        dma_misc = p.sem("dma_misc")

        p.add("sync", lambda h: h.dma_start(out=ident[:], in_=id_in),
              incs=[(dma_misc, 16)])
        stA = contextlib.ExitStack()
        eA = stA.enter_context
        wk_sb = eA(nc.sbuf_tensor("wk_sb", [128, KC * D], BF16))
        wv_sb = eA(nc.sbuf_tensor("wv_sb", [128, KC * D], BF16))

        def w_to_sb(h, dst, src):
            return h.dma_start(
                out=dst[:].rearrange("p (k d) -> p k d", k=KC),
                in_=src.rearrange("(k p) d -> p k d", p=128))

        def x_cast(s):
            p.add("gpsimd",
                  lambda h, s=s: h.dma_start(
                      out=x_bf.ap()[512 * s:512 * (s + 1), :],
                      in_=x_in[512 * s:512 * (s + 1), :]),
                  incs=[(ld_xc[s], 16)])

        p.add("gpsimd", lambda h: w_to_sb(h, wk_sb, wk_in), incs=[(ld_wkv, 16)])
        p.add("gpsimd", lambda h: w_to_sb(h, wv_sb, wv_in), incs=[(ld_wkv, 16)])
        for s in range(NSUP):
            x_cast(s)
        p.add("gpsimd", lambda h: h.dma_start(out=wq_bf.ap(), in_=wq_in),
              incs=[(ld_wqo, 16)])
        p.add("gpsimd", lambda h: h.dma_start(out=wo_bf.ap(), in_=wo_in),
              incs=[(ld_wqo, 16)])
        p.add("gpsimd", lambda h: h.dma_start(out=wg_bf.ap(), in_=wg_in),
              incs=[(ld_wf, 16)])
        p.add("gpsimd", lambda h: h.dma_start(out=wu_bf.ap(), in_=wu_in),
              incs=[(ld_wf, 16)])
        p.add("gpsimd", lambda h: h.dma_start(out=wd_bf.ap(), in_=wd_in),
              incs=[(ld_wf, 16)])
        xt_need_p = {}
        v_done = {}

        def add_xtA(s):
            for k in range(KC):
                p.add("sync",
                      lambda h, s=s, k=k: h.dma_start(
                          out=xt[s % 2][:, 512 * k:512 * (k + 1)],
                          in_=x_bf.ap()[512 * s:512 * (s + 1),
                                        128 * k:128 * (k + 1)],
                          transpose=True),
                      waits=[(ld_xc[s], 16),
                             (pe_v, v_done.get(s - 2, 0))],
                      incs=[(xt_sp[s % 2], 16)])
            xt_need_p[s] = xt_sp[s % 2].n

        add_xtA(0)
        add_xtA(1)

        deferred_tr = []
        for k in range(KC):
            deferred_tr.append(lambda k=k: p.add(
                "sync",
                lambda h: h.dma_start(
                    out=wgt[:, HID * k:HID * (k + 1)],
                    in_=wg_bf.ap()[:, 128 * k:128 * (k + 1)], transpose=True),
                waits=[(ld_wf, 48)], incs=[(ld_wgt, 16)]))
        for k in range(KC):
            deferred_tr.append(lambda k=k: p.add(
                "sync",
                lambda h: h.dma_start(
                    out=wut[:, HID * k:HID * (k + 1)],
                    in_=wu_bf.ap()[:, 128 * k:128 * (k + 1)], transpose=True),
                waits=[(ld_wf, 48)], incs=[(ld_wut, 16)]))

        def mat_strip(k):
            q = k % 2
            p.add("sync",
                  lambda h, k=k, q=q: h.dma_start(
                      out=wdt[q][:, 0:D],
                      in_=wd_bf.ap()[:, 128 * k:128 * (k + 1)], transpose=True),
                  waits=[(ld_wf, 48), (mk_s[q], mk_s[q].n)],
                  incs=[(mk_t[q], 16)])
            p.add("scalar",
                  lambda h, k=k, q=q: h.dma_start(
                      out=wdt_bf.ap()[128 * k:128 * (k + 1), :],
                      in_=wdt[q][:, 0:D]),
                  waits=[(mk_t[q], mk_t[q].n)], incs=[(mk_s[q], 16)])
        for k in range(MC):
            deferred_tr.append(lambda k=k: mat_strip(k))

        with stA:
            kf_sb = eA(nc.sbuf_tensor("kf_sb", [128, 8 * D], BF16))
            vh_sb = eA(nc.sbuf_tensor("vh_sb", [128, 8 * NPAIR * VW], BF16))
            ktmp = [eA(nc.sbuf_tensor(f"ktmp{i}", [128, 512], BF16))
                    for i in range(2)]
            etmp = [eA(nc.sbuf_tensor(f"etmp{i}", [128, 512], BF16))
                    for i in range(2)]

            if PERTURB == 4:
                p.add("vector", lambda h: h.memset(smalls[:, 34:35], 0.0))
            p.add("vector", lambda h: h.memset(smalls[:, 62:63], EPS_LN),
                  incs=[(ve_bn, 1)])
            p.add("vector", lambda h: h.memset(acc_kv[:], 0.0),
                  incs=[(ve_acc, 1)])
            p.add("vector", lambda h: h.memset(bd[:], 0.0), incs=[(ve_bd, 1)])
            p.add("vector",
                  lambda h: h.memset(
                      ap3(vh_sb[:], 128, [[VW * NPAIR, 8], [VW, NPAIR], [1, 1]]),
                      1.0),
                  incs=[(ve_acc, 1)])

            kf_done = {}
            vh_done = {}
            k_banks = [(pb[0], pb[1]), (pb[4], pb[5])]
            v_banks = (pb[2], pb[3])

            for j in range(NJ):
                s, jq = j // 4, j % 4
                jj = j % 8
                if jq == 0 and s >= 2:
                    add_xtA(s)
                if jq == 0 and s >= 3:
                    for _ in range(8):
                        if deferred_tr:
                            deferred_tr.pop(0)()
                kb = k_banks[j % 2]
                for k in range(KC):
                    for n in range(2):
                        p.add("tensor",
                              lambda h, j=j, k=k, n=n, kb=kb: h.matmul(
                                  kb[n][:],
                                  xt[(j // 4) % 2][:, 512 * k + 128 * (j % 4):
                                                   512 * k + 128 * (j % 4) + 128],
                                  wk_sb[:, D * k + 512 * n:D * k + 512 * (n + 1)],
                                  start=(k == 0), stop=(k == KC - 1)),
                              waits=[(xt_sp[s % 2], xt_need_p[s]),
                                     (ld_wkv, 32),
                                     (ve_kf, kf_done.get((j - 2, n), 0))],
                              incs=([(pe_k, 1)] if k == KC - 1 else []))
                for k in range(KC):
                    for n in range(2):
                        p.add("tensor",
                              lambda h, j=j, k=k, n=n: h.matmul(
                                  v_banks[n][:],
                                  xt[(j // 4) % 2][:, 512 * k + 128 * (j % 4):
                                                   512 * k + 128 * (j % 4) + 128],
                                  wv_sb[:, D * k + 512 * n:D * k + 512 * (n + 1)],
                                  start=(k == 0), stop=(k == KC - 1)),
                              waits=[(ld_wkv, 32),
                                     (ac_vh, vh_done.get((j - 1, n), 0))],
                              incs=([(pe_v, 1)] if k == KC - 1 else []))
                if jq == 3:
                    v_done[s] = pe_v.n
                for n in range(2):
                    p.add("vector",
                          lambda h, n=n, kb=kb: h.tensor_scalar_min(
                              ktmp[n][:], kb[n][:], 0.0),
                          waits=[(pe_k, pe_k.n - (1 - n)),
                                 (ac_e, ac_e.n - 2 + n if j > 0 or n > 0 else 0)],
                          incs=[(ve_km, 1)])
                    p.add("scalar",
                          lambda h, n=n: h.activation(
                              etmp[n][:], ktmp[n][:], AF.Exp),
                          waits=[(ve_km, ve_km.n),
                                 (ve_kf, ve_kf.n - 1 if (j, n) != (0, 0) else 0)],
                          incs=[(ac_e, 1)])
                    p.add("vector",
                          lambda h, jj=jj, n=n, kb=kb: h.scalar_tensor_tensor(
                              kf_sb[:, D * jj + 512 * n:D * jj + 512 * n + 512],
                              kb[n][:], 0.0, etmp[n][:], ALU.max, ALU.add),
                          waits=[(ac_e, ac_e.n)],
                          incs=[(ve_kf, 1)])
                    kf_done[(j, n)] = ve_kf.n
                    p.add("scalar",
                          lambda h, jj=jj, n=n: h.activation(
                              ap3(vh_sb[:], VW * NPAIR * jj + VW * 4 * n,
                                  [[VW, 4], [1, 128]]),
                              v_banks[n][:], AF.Copy),
                          waits=[(pe_v, pe_v.n - (1 - n)),
                                 (pe_kv, pe_kv.n)],
                          incs=[(ac_vh, 1)])
                    vh_done[(j, n)] = ac_vh.n

                if jj == 7:
                    kf_need = ve_kf.n
                    vh_need = ac_vh.n
                    for pr in range(NPAIR):
                        for t in range(8):
                            p.add("tensor",
                                  lambda h, pr=pr, t=t: h.matmul(
                                      pb[6][:, 0:129],
                                      kf_sb[:, D * t + 128 * pr:
                                            D * t + 128 * pr + 128],
                                      vh_sb[:, VW * NPAIR * t + VW * pr:
                                            VW * NPAIR * t + VW * pr + 129],
                                      start=(t == 0), stop=(t == 7)),
                                  waits=[(ve_kf, kf_need), (ac_vh, vh_need),
                                         (ve_acc, ve_acc.n)],
                                  incs=([(pe_kv, 1)] if t == 7 else []))
                        p.add("vector",
                              lambda h, pr=pr: h.tensor_tensor(
                                  acc_kv[:, VW * pr:VW * pr + 129],
                                  acc_kv[:, VW * pr:VW * pr + 129],
                                  pb[6][:, 0:129], ALU.add),
                              waits=[(pe_kv, pe_kv.n), (ve_acc, ve_acc.n)],
                              incs=[(ve_acc, 1)])

            p.add("gpsimd",
                  lambda h: h.dma_start(out=cc_in.ap(), in_=acc_kv[:]),
                  waits=[(ve_acc, ve_acc.n)], incs=[(cc, 16)])
            p.add("gpsimd",
                  lambda h: h.collective_compute(
                      "AllReduce", ALU.add,
                      replica_groups=[[0, 1], [2, 3], [4, 5], [6, 7]],
                      ins=[cc_in.ap().opt()], outs=[cc_out.ap().opt()]),
                  waits=[(cc, 16)], incs=[(cc2, 1)])
            p.add("gpsimd",
                  lambda h: h.dma_start(out=kvr[:], in_=cc_out.ap()),
                  waits=[(cc2, 1)], incs=[(cc3, 16)])
            for pr in range(NPAIR):
                o = VW * pr
                p.add("vector", lambda h, o=o: h.tensor_copy(
                    bd[0:64, o:o + 64], kvr[0:64, o:o + 64]),
                    waits=[(cc3, 16), (ve_bd, 1)])
                p.add("vector", lambda h, o=o: h.tensor_copy(
                    bd[0:64, o + 64:o + 65], kvr[0:64, o + 128:o + 129]))
                p.add("vector", lambda h, o=o: h.tensor_copy(
                    bd[64:128, o + 65:o + 129], kvr[64:128, o + 64:o + 128]))
                p.add("vector", lambda h, o=o: h.tensor_copy(
                    bd[64:128, o + 129:o + 130], kvr[64:128, o + 128:o + 129]),
                    incs=[(ve_bd, 1)])

            p.add("tensor", lambda h: h.nop())
            p.add("tensor", lambda h: h.nop(), incs=[(pe_endA, 1)])

        with contextlib.ExitStack() as stB:
            eB = stB.enter_context
            pt = eB(nc.psum_tensor("pt", [128, 1024], BF16))
            wq_sb = eB(nc.sbuf_tensor("wq_sb", [128, KC * D], BF16))
            wo_sb = eB(nc.sbuf_tensor("wo_sb", [128, KC * D], BF16))
            qf_sb = eB(nc.sbuf_tensor("qf_sb", [128, KC * 512], BF16))
            qtmp = [eB(nc.sbuf_tensor(f"qtmp{i}", [128, 512], BF16))
                    for i in range(2)]
            eqtmp = [eB(nc.sbuf_tensor(f"eqtmp{i}", [128, 512], BF16))
                     for i in range(2)]
            attn = [eB(nc.sbuf_tensor(f"attn{i}", [128, D], BF16))
                    for i in range(2)]
            attnT = [eB(nc.sbuf_tensor(f"attnT{i}", [128, D], BF16))
                     for i in range(2)]
            xf = [eB(nc.sbuf_tensor(f"xf{i}", [128, D], F32))
                  for i in range(2)]
            r1 = [eB(nc.sbuf_tensor(f"r1_{i}", [128, D], F32))
                  for i in range(2)]
            x1 = [eB(nc.sbuf_tensor(f"x1_{i}", [128, D], BF16))
                  for i in range(2)]

            BW = [(pe_endA, 1), (ve_kf, ve_kf.n), (ac_vh, ac_vh.n),
                  (ac_e, ac_e.n), (ve_km, ve_km.n), (ve_acc, ve_acc.n)]
            p.add("scalar", lambda h: w_to_sb(h, wq_sb, wq_bf.ap()),
                  waits=BW + [(ld_wqo, 32)], incs=[(ld_wq, 16)])
            p.add("scalar", lambda h: w_to_sb(h, wo_sb, wo_bf.ap()),
                  waits=BW + [(ld_wqo, 32)], incs=[(ld_wq, 16)])

            div_done = {}
            tc_done = {}
            wo_done = {}
            r1_done = {}
            x1_done = {}
            xf_need = {}
            num_banks = [(pb[2], 3, 0), (pb[3], 3, 3), (pb[4], 2, 6)]
            q_done = {}

            ln_state = {"i": 0, "applies": []}

            def add_ln(src_buf, dst_buf, src_sem, war_waits, store_fn):
                i = ln_state["i"]
                base = 46 + 6 * (i % 2)
                mv, std, inv, ng = base, base + 2, base + 3, base + 4
                guard = (ln_state["applies"][i - 2] if i >= 2 else 0)
                p.add("vector", lambda h: h.bn_stats(
                    smalls[:, 0:6], src_buf[:, 0:512]),
                    waits=[src_sem, (ve_bn, ve_bn.n)],
                    incs=[(ve_bs, 1)])
                p.add("vector", lambda h: h.bn_stats(
                    smalls[:, 6:12], src_buf[:, 512:1024]),
                    waits=[src_sem], incs=[(ve_bs, 1)])
                p.add("vector", lambda h: h.bn_aggr(
                    smalls[:, mv:mv + 2], smalls[:, 0:12]),
                    waits=[(ve_bs, ve_bs.n), (ac_x1, guard)],
                    incs=[(ve_bn, 1)])
                p.add("scalar", lambda h: h.activation(
                    smalls[:, std:std + 1], smalls[:, mv + 1:mv + 2], AF.Sqrt,
                    bias=smalls[:, 62:63]),
                    waits=[(ve_bn, ve_bn.n)], incs=[(ac_std, 1)])
                p.add("vector", lambda h: h.reciprocal(
                    smalls[:, inv:inv + 1], smalls[:, std:std + 1]),
                    waits=[(ac_std, ac_std.n)], incs=[(ve_rc, 1)])
                p.add("vector", lambda h: h.scalar_tensor_tensor(
                    smalls[:, ng:ng + 1], smalls[:, mv:mv + 1], -1.0,
                    smalls[:, inv:inv + 1], ALU.mult, ALU.mult),
                    waits=[(ve_rc, ve_rc.n), (ve_bn, ve_bn.n)],
                    incs=[(ve_ln, 1)])
                p.add("scalar",
                      lambda h: h.activation(
                          dst_buf[:], src_buf[:], AF.Identity,
                          scale=smalls[:, inv:inv + 1],
                          bias=smalls[:, ng:ng + 1]),
                      waits=[(ve_ln, ve_ln.n)] + war_waits,
                      incs=[(ac_x1, 1)])
                ln_state["applies"].append(ac_x1.n)
                ln_state["i"] = i + 1
                store_fn(ac_x1.n)

            for s in range(NSUP):
                for k in range(KC):
                    p.add("sync",
                          lambda h, s=s, k=k: h.dma_start(
                              out=xt[s % 2][:, 512 * k:512 * (k + 1)],
                              in_=x_bf.ap()[512 * s:512 * (s + 1),
                                            128 * k:128 * (k + 1)],
                              transpose=True),
                          waits=BW + [(ld_xc[s], 16),
                                      (pe_q, q_done.get(s - 2, 0))],
                          incs=[(xt_sp[s % 2], 16)])
                xt_need = xt_sp[s % 2].n
                for m in range(KC):
                    qb = pb[m % 2]
                    for k in range(KC):
                        p.add("tensor",
                              lambda h, m=m, k=k, s=s, qb=qb: h.matmul(
                                  qb[:],
                                  wq_sb[:, D * k + 128 * m:D * k + 128 * m + 128],
                                  xt[s % 2][:, 512 * k:512 * (k + 1)],
                                  start=(k == 0), stop=(k == KC - 1)),
                              waits=[(xt_sp[s % 2], xt_need), (ld_wq, 32),
                                     (ve_qf, max(0, ve_qf.n - 1))],
                              incs=([(pe_q, 1)] if k == KC - 1 else []))
                    if m == KC - 1:
                        q_done[s] = pe_q.n
                    p.add("vector",
                          lambda h, m=m, qb=qb: h.tensor_scalar_min(
                              qtmp[m % 2][:], qb[:], 0.0),
                          waits=[(pe_q, pe_q.n),
                                 (ac_qe, max(0, ac_qe.n - 1))],
                          incs=[(ve_qm, 1)])
                    p.add("scalar",
                          lambda h, m=m: h.activation(
                              eqtmp[m % 2][:], qtmp[m % 2][:], AF.Exp),
                          waits=[(ve_qm, ve_qm.n),
                                 (ve_qf, max(0, ve_qf.n - 1))],
                          incs=[(ac_qe, 1)])
                    p.add("vector",
                          lambda h, m=m, qb=qb: h.scalar_tensor_tensor(
                              qf_sb[:, 512 * m:512 * (m + 1)],
                              qb[:], 0.0, eqtmp[m % 2][:], ALU.max, ALU.add),
                          waits=[(ac_qe, ac_qe.n)],
                          incs=[(ve_qf, 1)])
                qf_need = ve_qf.n

                num_done = {}

                def stage_xf_num(jq, s=s):
                    prev = (s, jq - 2) if jq >= 2 else (s - 1, jq + 2)
                    p.add("gpsimd",
                          lambda h: h.dma_start(
                              out=xf[jq % 2][:],
                              in_=x_in[512 * s + 128 * jq:
                                       512 * s + 128 * jq + 128, :]),
                          waits=BW + [(ve_r1, r1_done.get(prev, 0))],
                          incs=[(ld_xfp[jq % 2], 16)])
                    xf_need[(s, jq)] = ld_xfp[jq % 2].n
                    for bank, cnt, p0 in num_banks:
                        for q in range(cnt):
                            pr = p0 + q
                            p.add("tensor",
                                  lambda h, bank=bank, q=q, pr=pr: h.matmul(
                                      bank[:, VW * q:VW * q + VW],
                                      qf_sb[:, 512 * pr + 128 * jq:
                                            512 * pr + 128 * jq + 128],
                                      bd[:, VW * pr:VW * (pr + 1)],
                                      start=True, stop=True),
                                  waits=[(ve_qf, qf_need), (ve_bd, 9),
                                         (ve_div, div_done.get((s, jq - 1), 0))],
                                  incs=([(pe_num, 1)] if q == cnt - 1 else []))
                    num_done[(s, jq)] = pe_num.n

                def stage_div(jq, s=s):
                    for bank, cnt, p0 in num_banks:
                        p.add("vector",
                              lambda h, bank=bank, cnt=cnt, p0=p0:
                              h.tensor_scalar(
                                  smalls[:, 12 + 2 * p0:12 + 2 * (p0 + cnt)],
                                  ap3(bank[:], 64, [[VW, cnt], [65, 2]]),
                                  EPS_ATTN, None, ALU.add),
                              waits=[(pe_num, num_done[(s, jq)]),
                                     (ve_rc, ve_rc.n)],
                              incs=[(ve_dv, 1)])
                    dv_now = ve_dv.n
                    for bank, cnt, p0 in num_banks:
                        p.add("vector",
                              lambda h, cnt=cnt, p0=p0: h.reciprocal(
                                  smalls[:, 28 + 2 * p0:28 + 2 * (p0 + cnt)],
                                  smalls[:, 12 + 2 * p0:12 + 2 * (p0 + cnt)]),
                              waits=[(ve_dv, dv_now),
                                     (ve_div, ve_div.n)],
                              incs=[(ve_rc, 1)])
                    rc_now = ve_rc.n
                    for bank, cnt, p0 in num_banks:
                        p.add("vector",
                              lambda h, bank=bank, cnt=cnt, p0=p0:
                              h.scalar_tensor_tensor(
                                  ap3(attn[jq % 2][:], 128 * p0,
                                      [[128, cnt], [64, 2], [1, 64]]),
                                  ap3(bank[:], 0, [[VW, cnt], [65, 2], [1, 64]]),
                                  1.0,
                                  ap3(smalls[:], 28 + 2 * p0,
                                      [[2, cnt], [1, 2], [0, 64]]),
                                  ALU.mult, ALU.mult),
                              waits=[(ve_rc, rc_now),
                                     (ac_tc, tc_done.get((s, jq - 2), 0))],
                              incs=[(ve_div, 1)])
                    div_done[(s, jq)] = ve_div.n

                def stage_tail(jq, s=s):
                    for k in range(KC):
                        p.add("tensor",
                              lambda h, k=k: h.transpose(
                                  pt[:, 128 * k:128 * (k + 1)],
                                  attn[jq % 2][:, 128 * k:128 * (k + 1)],
                                  ident[:]),
                              waits=[(ve_div, div_done[(s, jq)]),
                                     (ac_tc, tc_done.get((s, jq - 1), 0)),
                                     (dma_misc, 16)],
                              incs=([(pe_t, 1)] if k == KC - 1 else []))
                    if jq + 1 < 4:
                        stage_xf_num(jq + 1)
                        stage_div(jq + 1)
                    p.add("scalar",
                          lambda h: h.activation(
                              attnT[jq % 2][:], pt[:], AF.Copy),
                          waits=[(pe_t, pe_t.n),
                                 (pe_wo, wo_done.get((s, jq - 2), 0))],
                          incs=[(ac_tc, 1)])
                    tc_done[(s, jq)] = ac_tc.n
                    for k in range(KC):
                        for n in range(2):
                            p.add("tensor",
                                  lambda h, k=k, n=n: h.matmul(
                                      pb[5 + n][:],
                                      attnT[jq % 2][:, 128 * k:128 * (k + 1)],
                                      wo_sb[:, D * k + 512 * n:
                                            D * k + 512 * (n + 1)],
                                      start=(k == 0), stop=(k == KC - 1)),
                                  waits=[(ac_tc, ac_tc.n), (ld_wq, 32),
                                         (ve_r1, r1_done.get((s, jq - 1), 0))],
                                  incs=([(pe_wo, 1)] if k == KC - 1 else []))
                    wo_done[(s, jq)] = pe_wo.n
                    for n in range(2):
                        p.add("vector",
                              lambda h, n=n: h.scalar_tensor_tensor(
                                  r1[jq % 2][:, 512 * n:512 * (n + 1)],
                                  pb[5 + n][:], 1.0,
                                  xf[jq % 2][:, 512 * n:512 * (n + 1)],
                                  ALU.mult, ALU.add),
                              waits=[(pe_wo, pe_wo.n),
                                     (ld_xfp[jq % 2], xf_need[(s, jq)]),
                                     (ac_x1, x1_done.get((s, jq - 2), 0))],
                              incs=[(ve_r1, 1)])
                    r1_done[(s, jq)] = ve_r1.n
                    if jq >= 1:
                        stage_ln1(jq - 1)

                def stage_ln1(jq, s=s):
                    def store_x1(apply_n):
                        p.add("scalar",
                              lambda h: h.dma_start(
                                  out=x1_bf.ap()[512 * s + 128 * jq:
                                                 512 * s + 128 * jq + 128, :],
                                  in_=x1[jq % 2][:]),
                              waits=[(ac_x1, apply_n)],
                              incs=[(st_x1p[jq % 2], 16)])
                    add_ln(r1[jq % 2], x1[jq % 2],
                           (ve_r1, r1_done[(s, jq)]),
                           [(st_x1p[jq % 2], st_x1p[jq % 2].n)], store_x1)
                    x1_done[(s, jq)] = ac_x1.n

                stage_xf_num(0)
                stage_div(0)
                for jq in range(4):
                    stage_tail(jq)
                stage_ln1(3)
            p.add("tensor", lambda h: h.transpose(
                pt[:, 0:128], attn[1][:, 0:128], ident[:]),
                waits=[(ac_tc, ac_tc.n)], incs=[(bar_b, 1)])
            p.add("vector", lambda h: h.nop(),
                  waits=[(ac_x1, ac_x1.n)], incs=[(bar_b, 1)])
            p.add("scalar", lambda h: h.nop(),
                  incs=[(bar_b, 1)])

        with contextlib.ExitStack() as stC:
            eC = stC.enter_context
            pb7 = eC(nc.psum_tensor("pb7", [128, 512], F32))
            h2t = eC(nc.sbuf_tensor("h2t", [128, MC * 512], BF16))
            sg = [eC(nc.sbuf_tensor(f"sg{i}", [128, 512], BF16))
                  for i in range(2)]
            ub = [eC(nc.sbuf_tensor(f"ub{i}", [128, 512], BF16))
                  for i in range(2)]
            x1j = [eC(nc.sbuf_tensor(f"x1j{i}", [128, D], BF16))
                   for i in range(4)]
            dbanks = pb + [pb7]
            r2 = [eC(nc.sbuf_tensor(f"r2_{i}", [128, D], F32))
                  for i in range(2)]
            x2 = [eC(nc.sbuf_tensor(f"x2_{i}", [128, D], F32))
                  for i in range(2)]

            CW = [(bar_b, 3)]
            u_done = {}
            h2_done = {}
            h2_seq = []
            r2_done = {}
            x2_done = {}
            x1j_need = {}
            strip_done = []

            x1t_need_p = {}

            def add_x1t(s):
                sn = [(st_x1p[0], 32 * (s + 1)), (st_x1p[1], 32 * (s + 1))]
                war = ([(pe_u, u_done[s - 2])] if s >= 2
                       else [(pe_q, q_done[6 + s])])
                for k in range(KC):
                    p.add("sync",
                          lambda h, s=s, k=k: h.dma_start(
                              out=xt[s % 2][:, 512 * k:512 * (k + 1)],
                              in_=x1_bf.ap()[512 * s:512 * (s + 1),
                                             128 * k:128 * (k + 1)],
                              transpose=True),
                          waits=sn + war,
                          incs=[(xt_sp[s % 2], 16)])
                x1t_need_p[s] = xt_sp[s % 2].n

            add_x1t(0)
            for s in range(NSUP):
                st_need = [(st_x1p[0], 32 * (s + 1)), (st_x1p[1], 32 * (s + 1))]
                x1t_need = x1t_need_p[s]
                for m in range(MC):
                    gb, ubk = pb[m % 2 * 2], pb[m % 2 * 2 + 1]
                    gw = ([(ve_r2, r2_done.get((s - 1, m), 0))]
                          if m < 2 else [])
                    for k in range(KC):
                        p.add("tensor",
                              lambda h, m=m, k=k, s=s, gb=gb: h.matmul(
                                  gb[:],
                                  wgt[:, HID * k + 128 * m:HID * k + 128 * m + 128],
                                  xt[s % 2][:, 512 * k:512 * (k + 1)],
                                  start=(k == 0), stop=(k == KC - 1)),
                              waits=CW + [(xt_sp[s % 2], x1t_need),
                                          (ld_wgt, 128),
                                          (ac_sg, max(0, ac_sg.n - 1))] +
                              (gw if k == 0 else []),
                              incs=([(pe_g, 1)] if k == KC - 1 else []))
                    for k in range(KC):
                        p.add("tensor",
                              lambda h, m=m, k=k, s=s, ubk=ubk: h.matmul(
                                  ubk[:],
                                  wut[:, HID * k + 128 * m:HID * k + 128 * m + 128],
                                  xt[s % 2][:, 512 * k:512 * (k + 1)],
                                  start=(k == 0), stop=(k == KC - 1)),
                              waits=CW + [(ld_wut, 128),
                                          (ac_ub, max(0, ac_ub.n - 1))] +
                              (gw if k == 0 else []),
                              incs=([(pe_u, 1)] if k == KC - 1 else []))
                    h2_war = h2_seq[-2] if len(h2_seq) >= 2 else 0
                    p.add("scalar",
                          lambda h, m=m, gb=gb: h.activation(
                              sg[m % 2][:], gb[:], AF.Silu, scale=S127),
                          waits=[(pe_g, pe_g.n), (ve_h2, h2_war)],
                          incs=[(ac_sg, 1)])
                    p.add("scalar",
                          lambda h, m=m, ubk=ubk: h.activation(
                              ub[m % 2][:], ubk[:], AF.Copy),
                          waits=[(pe_u, pe_u.n), (ve_h2, h2_war)],
                          incs=[(ac_ub, 1)])
                    p.add("vector",
                          lambda h, m=m: h.tensor_tensor(
                              h2t[:, 512 * m:512 * (m + 1)],
                              sg[m % 2][:], ub[m % 2][:], ALU.mult),
                          waits=[(ac_sg, ac_sg.n), (ac_ub, ac_ub.n),
                                 (pe_d, 4 * s)],
                          incs=[(ve_h2, 1)])
                    h2_done[(s, m)] = ve_h2.n
                    h2_seq.append(ve_h2.n)
                u_done[s] = pe_u.n
                if s + 1 < NSUP:
                    add_x1t(s + 1)
                for jq in range(4):
                    p.add("scalar",
                          lambda h, s=s, jq=jq: h.dma_start(
                              out=x1j[jq][:],
                              in_=x1_bf.ap()[512 * s + 128 * jq:
                                             512 * s + 128 * jq + 128, :]),
                          waits=CW + st_need +
                          [(ve_r2, r2_done.get((s - 1, jq), 0))],
                          incs=[(ld_x1jq[jq], 16)])
                    x1j_need[(s, jq)] = ld_x1jq[jq].n
                pe_d_base = pe_d.n
                for k in range(MC):
                    kp = k // 2
                    if k % 2 == 0:
                        g = len(strip_done)
                        strip_war = (strip_done[g - 2] if g >= 2
                                     else (pe_ds, 0))
                        p.add("sync",
                              lambda h, k=k: h.dma_start(
                                  out=wdt[(k // 2) % 2][:]
                                  .rearrange("p (b d) -> p b d", b=2),
                                  in_=wdt_bf.ap()[128 * k:128 * (k + 2), :]
                                  .rearrange("(b p) d -> p b d", p=128)),
                              waits=[(mk_s[0], 16 * (MC // 2)),
                                     (mk_s[1], 16 * (MC // 2)),
                                     strip_war],
                              incs=[(ld_wdtp[(k // 2) % 2], 16)])
                        wdt_need = ld_wdtp[kp % 2].n
                    for n in range(2):
                        for jq in range(4):
                            xw = []
                            if k == 0:
                                xw = [(ac_sg, ac_sg.n), (ac_ub, ac_ub.n),
                                      (ve_r2, r2_done.get((s - 1, 3), 0))]
                            incs = []
                            if k == MC - 1 and n == 1:
                                incs = [(pe_d, 1)]
                            elif n == 1 and jq == 3 and k % 2 == 1:
                                incs = [(pe_ds, 1)]
                            elif k == MC - 1 and n == 0 and jq == 3:
                                incs = [(pe_ds, 1)]
                            p.add("tensor",
                                  lambda h, k=k, jq=jq, n=n: h.matmul(
                                      dbanks[2 * jq + n][:],
                                      h2t[:, 512 * k + 128 * jq:
                                           512 * k + 128 * jq + 128],
                                      wdt[(k // 2) % 2][:, 1024 * (k % 2) +
                                                        512 * n:
                                                        1024 * (k % 2) +
                                                        512 * (n + 1)],
                                      start=(k == 0), stop=(k == MC - 1)),
                                  waits=[(ve_h2, h2_done[(s, k)]),
                                         (ld_wdtp[kp % 2], wdt_need)] + xw,
                                  incs=incs)
                    if k % 2 == 1:
                        strip_done.append((pe_d, pe_d_base + 4)
                                          if k == MC - 1
                                          else (pe_ds, pe_ds.n))
                for jq in range(4):
                    for n in range(2):
                        p.add("vector",
                              lambda h, jq=jq, n=n:
                              h.scalar_tensor_tensor(
                                  r2[jq % 2][:, 512 * n:512 * (n + 1)],
                                  dbanks[2 * jq + n][:], S127SQ,
                                  x1j[jq][:, 512 * n:512 * (n + 1)],
                                  ALU.mult, ALU.add),
                              waits=[(pe_d, pe_d_base + jq + 1),
                                     (ld_x1jq[jq], x1j_need[(s, jq)]),
                                     (ac_x1, x2_done.get((s, jq - 2), 0))],
                              incs=[(ve_r2, 1)])
                    r2_done[(s, jq)] = ve_r2.n

                    def store_out(apply_n, s=s, jq=jq):
                        p.add("scalar",
                              lambda h: h.dma_start(
                                  out=out[512 * s + 128 * jq:
                                          512 * s + 128 * jq + 128, :],
                                  in_=x2[jq % 2][:]),
                              waits=[(ac_x1, apply_n)],
                              incs=[(st_outp[jq % 2], 16)])
                    add_ln(r2[jq % 2], x2[jq % 2], (ve_r2, ve_r2.n),
                           [(st_outp[jq % 2], st_outp[jq % 2].n)],
                           store_out)
                    x2_done[(s, jq)] = ac_x1.n

            p.add("scalar", lambda h: h.nop(),
                  waits=[(st_outp[0], 16 * NJ // 2),
                         (st_outp[1], 16 * NJ // 2)])
            if PERTURB == 1:
                p.add("scalar", lambda h: h.nop())
            elif PERTURB == 2:
                p.add("scalar", lambda h: h.nop())
                p.add("vector", lambda h: h.nop())
            elif PERTURB == 5:
                p.add("tensor", lambda h: h.nop())
            elif PERTURB == 6:
                for _ in range(3):
                    p.add("scalar", lambda h: h.nop())
                    p.add("vector", lambda h: h.nop())
                    p.add("tensor", lambda h: h.nop())

        with nc.Block() as block:
            p.emit(block)

    nc._prog_n_ops = p.n_ops
    return nc


_NC_CACHE = {}


def kernel(L0_attn_x, L0_attn_Wq, L0_attn_Wk, L0_attn_Wv, L0_attn_Wo,
           L0_ffn_w_gate, L0_ffn_w_up, L0_ffn_w_down):
    x = np.ascontiguousarray(np.asarray(L0_attn_x, dtype=np.float32))
    wq = np.ascontiguousarray(np.asarray(L0_attn_Wq, dtype=np.float32))
    wk = np.ascontiguousarray(np.asarray(L0_attn_Wk, dtype=np.float32))
    wv = np.ascontiguousarray(np.asarray(L0_attn_Wv, dtype=np.float32))
    wo = np.ascontiguousarray(np.asarray(L0_attn_Wo, dtype=np.float32))
    wg = np.ascontiguousarray(np.asarray(L0_ffn_w_gate, dtype=np.int32))
    wu = np.ascontiguousarray(np.asarray(L0_ffn_w_up, dtype=np.int32))
    wd = np.ascontiguousarray(np.asarray(L0_ffn_w_down, dtype=np.int32))
    ident = np.eye(128, dtype=ml_dtypes.bfloat16)

    if "nc" not in _NC_CACHE:
        _NC_CACHE["nc"] = build_program()
    nc = _NC_CACHE["nc"]

    in_maps = []
    for c in range(8):
        b, half = c // 2, c % 2
        xo = np.ascontiguousarray(x[b, half * T:(half + 1) * T, :])
        in_maps.append(dict(x=xo, wq=wq, wk=wk, wv=wv, wo=wo,
                            wg=wg, wu=wu, wd=wd, ident=ident))

    res = run_bass_kernel_spmd(nc, in_maps, core_ids=list(range(8)))

    outp = np.empty((4, 8192, 1024), dtype=np.float32)
    for c in range(8):
        b, half = c // 2, c % 2
        outp[b, half * T:(half + 1) * T, :] = np.asarray(res.results[c]["out"])
    return outp


if __name__ == "__main__":
    nc = build_program()
    print("built OK; instructions:", nc.n_instructions())


# version 18
# speedup vs baseline: 1.1075x; 1.0539x over previous
import contextlib
import numpy as np
import ml_dtypes
import concourse.bass as bass
import concourse.mybir as mybir
from concourse.bass_utils import run_bass_kernel_spmd

F32 = mybir.dt.float32
BF16 = mybir.dt.bfloat16
I32 = mybir.dt.int32
AF = mybir.ActivationFunctionType
ALU = mybir.AluOpType

T = 4096
D = 1024
HID = 3072
NSUP = 8
NJ = 32
KC = D // 128
MC = HID // 128
NPAIR = 8
VW = 130
EPS_ATTN = 1e-6
EPS_LN = 1e-5
S127 = 1.0 / 127.0
S127SQ = float(np.float32(1.0) / np.float32(127.0)) ** 2

ENGINES = ("sync", "tensor", "vector", "scalar", "gpsimd")


class Sem:
    __slots__ = ("h", "n", "name")

    def __init__(self, h, name):
        self.h = h
        self.n = 0
        self.name = name


class Prog:

    def __init__(self, nc, stack, max_ops=None):
        self.nc = nc
        self.stack = stack
        self.ops = {e: [] for e in ENGINES}
        self._sems = []
        self.n_ops = 0
        self.max_ops = max_ops

    def sem(self, name):
        s = Sem(self.stack.enter_context(self.nc.semaphore(name)), name)
        self._sems.append(s)
        return s

    def add(self, eng, fn, waits=(), incs=()):
        self.n_ops += 1
        if self.max_ops is not None and self.n_ops > self.max_ops:
            for s, k in incs:
                s.n += k
            return
        ws = [(s, int(v)) for (s, v) in waits if v > 0]
        self.ops[eng].append((fn, ws, list(incs)))
        for s, k in incs:
            s.n += k

    def emit(self, block):
        for eng in ENGINES:
            ops = self.ops[eng]
            if not ops:
                continue

            def body(handle, ops=ops):
                waited = {}
                for fn, ws, incs in ops:
                    for s, v in ws:
                        if waited.get(s, 0) < v:
                            handle.wait_ge(s.h, v)
                            waited[s] = v
                    instr = fn(handle)
                    for s, k in incs:
                        instr.then_inc(s.h, k)

            getattr(block, eng)(body)


PERTURB = 3


def build_program(max_ops=None):
    nc = bass.Bass("TRN2", target_bir_lowering=False)

    x_in = nc.dram_tensor("x", [T, D], F32, kind="ExternalInput").ap()
    wq_in = nc.dram_tensor("wq", [D, D], F32, kind="ExternalInput").ap()
    wk_in = nc.dram_tensor("wk", [D, D], F32, kind="ExternalInput").ap()
    wv_in = nc.dram_tensor("wv", [D, D], F32, kind="ExternalInput").ap()
    wo_in = nc.dram_tensor("wo", [D, D], F32, kind="ExternalInput").ap()
    wg_in = nc.dram_tensor("wg", [HID, D], I32, kind="ExternalInput").ap()
    wu_in = nc.dram_tensor("wu", [HID, D], I32, kind="ExternalInput").ap()
    wd_in = nc.dram_tensor("wd", [D, HID], I32, kind="ExternalInput").ap()
    id_in = nc.dram_tensor("ident", [128, 128], BF16, kind="ExternalInput").ap()
    out = nc.dram_tensor("out", [T, D], F32, kind="ExternalOutput").ap()

    x_bf = nc.dram_tensor("x_bf", [T, D], BF16)
    wq_bf = nc.dram_tensor("wq_bf", [D, D], BF16)
    wo_bf = nc.dram_tensor("wo_bf", [D, D], BF16)
    wg_bf = nc.dram_tensor("wg_bf", [HID, D], BF16)
    wu_bf = nc.dram_tensor("wu_bf", [HID, D], BF16)
    wd_bf = nc.dram_tensor("wd_bf", [D, HID], BF16)
    wdt_bf = nc.dram_tensor("wdt_bf", [HID, D], BF16)
    x1_bf = nc.dram_tensor("x1_bf", [T, D], BF16)
    cc_in = nc.dram_tensor("cc_in", [128, NPAIR * VW], F32)
    cc_out = nc.dram_tensor("cc_out", [128, NPAIR * VW], F32)

    def ap3(t_ap, offset, dims):
        return bass.AP(t_ap.tensor, t_ap.offset + offset,
                       [list(t_ap.ap[0])] + [list(d) for d in dims])

    with contextlib.ExitStack() as st0:
        e0 = st0.enter_context
        p = Prog(nc, st0, max_ops=max_ops)

        ident = e0(nc.sbuf_tensor("ident_sb", [128, 128], BF16))
        wgt = e0(nc.sbuf_tensor("wgt", [128, KC * HID], BF16))
        wut = e0(nc.sbuf_tensor("wut", [128, KC * HID], BF16))
        xt = [e0(nc.sbuf_tensor(f"xt{i}", [128, KC * 512], BF16,
                                is_transpose_dest=True)) for i in range(2)]
        bd = e0(nc.sbuf_tensor("bd", [128, NPAIR * VW], BF16))
        wdt = [e0(nc.sbuf_tensor(f"wdt{i}", [128, 2 * D], BF16,
                                 is_transpose_dest=True))
               for i in range(2)]
        smalls = e0(nc.sbuf_tensor("smalls", [128, 64], F32))
        acc_kv = e0(nc.sbuf_tensor("acc_kv", [128, NPAIR * VW], F32))
        kvr = e0(nc.sbuf_tensor("kvr", [128, NPAIR * VW], F32))
        pb = [e0(nc.psum_tensor(f"pb{i}", [128, 512], F32)) for i in range(7)]

        if PERTURB == 3:
            p.sem("pad0")
        ld_xc = [p.sem(f"ld_xc{i}") for i in range(NSUP)]
        ld_wkv = p.sem("ld_wkv")
        ld_wqo = p.sem("ld_wqo")
        ld_wf = p.sem("ld_wf")
        mk_t = [p.sem("mk_t0"), p.sem("mk_t1")]
        mk_s = [p.sem("mk_s0"), p.sem("mk_s1")]
        ld_wq = p.sem("ld_wq")
        ld_wgt = p.sem("ld_wgt")
        ld_wut = p.sem("ld_wut")
        xt_sp = [p.sem("xt_s0"), p.sem("xt_s1")]
        ld_xfp = [p.sem("ld_xf0"), p.sem("ld_xf1")]
        ld_wdtp = [p.sem("ld_wdt0"), p.sem("ld_wdt1")]
        ld_x1jq = [p.sem(f"ld_x1j{i}") for i in range(4)]
        st_x1p = [p.sem("st_x10"), p.sem("st_x11")]
        st_outp = [p.sem("st_out0"), p.sem("st_out1")]
        cc = p.sem("cc")
        cc2 = p.sem("cc2")
        cc3 = p.sem("cc3")
        pe_k = p.sem("pe_k")
        pe_v = p.sem("pe_v")
        pe_kv = p.sem("pe_kv")
        pe_q = p.sem("pe_q")
        pe_num = p.sem("pe_num")
        pe_t = p.sem("pe_t")
        pe_wo = p.sem("pe_wo")
        pe_g = p.sem("pe_g")
        pe_u = p.sem("pe_u")
        pe_d = p.sem("pe_d")
        ve_km = p.sem("ve_km")
        ve_kf = p.sem("ve_kf")
        ve_acc = p.sem("ve_acc")
        ve_bd = p.sem("ve_bd")
        ve_qm = p.sem("ve_qm")
        ve_qf = p.sem("ve_qf")
        ve_div = p.sem("ve_div")
        ve_r1 = p.sem("ve_r1")
        ve_ln = p.sem("ve_ln")
        ve_h2 = p.sem("ve_h2")
        ve_r2 = p.sem("ve_r2")
        ve_ln2 = p.sem("ve_ln2")
        ve_bn = p.sem("ve_bn")
        ve_bs = p.sem("ve_bs")
        ve_dv = p.sem("ve_dv")
        ve_rc = p.sem("ve_rc")
        pe_ds = p.sem("pe_ds")
        ac_e = p.sem("ac_e")
        ac_vh = p.sem("ac_vh")
        ac_qe = p.sem("ac_qe")
        ac_tc = p.sem("ac_tc")
        ac_x1 = p.sem("ac_x1")
        ac_sg = p.sem("ac_sg")
        ac_ub = p.sem("ac_ub")
        ac_std = p.sem("ac_std")
        bar_a = p.sem("bar_a")
        pe_endA = p.sem("pe_endA")
        bar_b = p.sem("bar_b")
        dma_misc = p.sem("dma_misc")

        p.add("sync", lambda h: h.dma_start(out=ident[:], in_=id_in),
              incs=[(dma_misc, 16)])
        stA = contextlib.ExitStack()
        eA = stA.enter_context
        wk_sb = eA(nc.sbuf_tensor("wk_sb", [128, KC * D], BF16))
        wv_sb = eA(nc.sbuf_tensor("wv_sb", [128, KC * D], BF16))

        def w_to_sb(h, dst, src):
            return h.dma_start(
                out=dst[:].rearrange("p (k d) -> p k d", k=KC),
                in_=src.rearrange("(k p) d -> p k d", p=128))

        def x_cast(s):
            p.add("gpsimd",
                  lambda h, s=s: h.dma_start(
                      out=x_bf.ap()[512 * s:512 * (s + 1), :],
                      in_=x_in[512 * s:512 * (s + 1), :]),
                  incs=[(ld_xc[s], 16)])

        p.add("gpsimd", lambda h: w_to_sb(h, wk_sb, wk_in), incs=[(ld_wkv, 16)])
        p.add("gpsimd", lambda h: w_to_sb(h, wv_sb, wv_in), incs=[(ld_wkv, 16)])
        for s in range(NSUP):
            x_cast(s)
        p.add("gpsimd", lambda h: h.dma_start(out=wq_bf.ap(), in_=wq_in),
              incs=[(ld_wqo, 16)])
        p.add("gpsimd", lambda h: h.dma_start(out=wo_bf.ap(), in_=wo_in),
              incs=[(ld_wqo, 16)])
        p.add("gpsimd", lambda h: h.dma_start(out=wg_bf.ap(), in_=wg_in),
              incs=[(ld_wf, 16)])
        p.add("gpsimd", lambda h: h.dma_start(out=wu_bf.ap(), in_=wu_in),
              incs=[(ld_wf, 16)])
        p.add("gpsimd", lambda h: h.dma_start(out=wd_bf.ap(), in_=wd_in),
              incs=[(ld_wf, 16)])
        xt_need_p = {}
        v_done = {}

        def add_xtA(s):
            for k in range(KC):
                p.add("sync",
                      lambda h, s=s, k=k: h.dma_start(
                          out=xt[s % 2][:, 512 * k:512 * (k + 1)],
                          in_=x_bf.ap()[512 * s:512 * (s + 1),
                                        128 * k:128 * (k + 1)],
                          transpose=True),
                      waits=[(ld_xc[s], 16),
                             (pe_v, v_done.get(s - 2, 0))],
                      incs=[(xt_sp[s % 2], 16)])
            xt_need_p[s] = xt_sp[s % 2].n

        add_xtA(0)
        add_xtA(1)

        deferred_tr = []
        for k in range(KC):
            deferred_tr.append(lambda k=k: p.add(
                "sync",
                lambda h: h.dma_start(
                    out=wgt[:, HID * k:HID * (k + 1)],
                    in_=wg_bf.ap()[:, 128 * k:128 * (k + 1)], transpose=True),
                waits=[(ld_wf, 48)], incs=[(ld_wgt, 16)]))
        for k in range(KC):
            deferred_tr.append(lambda k=k: p.add(
                "sync",
                lambda h: h.dma_start(
                    out=wut[:, HID * k:HID * (k + 1)],
                    in_=wu_bf.ap()[:, 128 * k:128 * (k + 1)], transpose=True),
                waits=[(ld_wf, 48)], incs=[(ld_wut, 16)]))

        def mat_strip(k):
            q = k % 2
            p.add("sync",
                  lambda h, k=k, q=q: h.dma_start(
                      out=wdt[q][:, 0:D],
                      in_=wd_bf.ap()[:, 128 * k:128 * (k + 1)], transpose=True),
                  waits=[(ld_wf, 48), (mk_s[q], mk_s[q].n)],
                  incs=[(mk_t[q], 16)])
            p.add("scalar",
                  lambda h, k=k, q=q: h.dma_start(
                      out=wdt_bf.ap()[128 * k:128 * (k + 1), :],
                      in_=wdt[q][:, 0:D]),
                  waits=[(mk_t[q], mk_t[q].n)], incs=[(mk_s[q], 16)])
        for k in range(MC):
            deferred_tr.append(lambda k=k: mat_strip(k))

        with stA:
            kf_sb = eA(nc.sbuf_tensor("kf_sb", [128, 8 * D], BF16))
            vh_sb = eA(nc.sbuf_tensor("vh_sb", [128, 8 * NPAIR * VW], BF16))
            ktmp = [eA(nc.sbuf_tensor(f"ktmp{i}", [128, 512], BF16))
                    for i in range(2)]
            etmp = [eA(nc.sbuf_tensor(f"etmp{i}", [128, 512], BF16))
                    for i in range(2)]

            if PERTURB == 4:
                p.add("vector", lambda h: h.memset(smalls[:, 34:35], 0.0))
            p.add("vector", lambda h: h.memset(smalls[:, 62:63], EPS_LN),
                  incs=[(ve_bn, 1)])
            p.add("vector", lambda h: h.memset(acc_kv[:], 0.0),
                  incs=[(ve_acc, 1)])
            p.add("vector", lambda h: h.memset(bd[:], 0.0), incs=[(ve_bd, 1)])
            p.add("vector",
                  lambda h: h.memset(
                      ap3(vh_sb[:], 128, [[VW * NPAIR, 8], [VW, NPAIR], [1, 1]]),
                      1.0),
                  incs=[(ve_acc, 1)])

            kf_done = {}
            vh_done = {}
            k_banks = [(pb[0], pb[1]), (pb[4], pb[5])]
            v_banks = (pb[2], pb[3])

            for j in range(NJ):
                s, jq = j // 4, j % 4
                jj = j % 8
                if jq == 0 and s >= 2:
                    add_xtA(s)
                if jq == 0 and s >= 3:
                    for _ in range(8):
                        if deferred_tr:
                            deferred_tr.pop(0)()
                kb = k_banks[j % 2]
                for k in range(KC):
                    for n in range(2):
                        p.add("tensor",
                              lambda h, j=j, k=k, n=n, kb=kb: h.matmul(
                                  kb[n][:],
                                  xt[(j // 4) % 2][:, 512 * k + 128 * (j % 4):
                                                   512 * k + 128 * (j % 4) + 128],
                                  wk_sb[:, D * k + 512 * n:D * k + 512 * (n + 1)],
                                  start=(k == 0), stop=(k == KC - 1)),
                              waits=[(xt_sp[s % 2], xt_need_p[s]),
                                     (ld_wkv, 32),
                                     (ve_kf, kf_done.get((j - 2, n), 0))],
                              incs=([(pe_k, 1)] if k == KC - 1 else []))
                for k in range(KC):
                    for n in range(2):
                        p.add("tensor",
                              lambda h, j=j, k=k, n=n: h.matmul(
                                  v_banks[n][:],
                                  xt[(j // 4) % 2][:, 512 * k + 128 * (j % 4):
                                                   512 * k + 128 * (j % 4) + 128],
                                  wv_sb[:, D * k + 512 * n:D * k + 512 * (n + 1)],
                                  start=(k == 0), stop=(k == KC - 1)),
                              waits=[(ld_wkv, 32),
                                     (ac_vh, vh_done.get((j - 1, n), 0))],
                              incs=([(pe_v, 1)] if k == KC - 1 else []))
                if jq == 3:
                    v_done[s] = pe_v.n
                for n in range(2):
                    p.add("vector",
                          lambda h, n=n, kb=kb: h.tensor_scalar_min(
                              ktmp[n][:], kb[n][:], 0.0),
                          waits=[(pe_k, pe_k.n - (1 - n)),
                                 (ac_e, ac_e.n - 2 + n if j > 0 or n > 0 else 0)],
                          incs=[(ve_km, 1)])
                    p.add("scalar",
                          lambda h, n=n: h.activation(
                              etmp[n][:], ktmp[n][:], AF.Exp),
                          waits=[(ve_km, ve_km.n),
                                 (ve_kf, ve_kf.n - 1 if (j, n) != (0, 0) else 0)],
                          incs=[(ac_e, 1)])
                    p.add("vector",
                          lambda h, jj=jj, n=n, kb=kb: h.scalar_tensor_tensor(
                              kf_sb[:, D * jj + 512 * n:D * jj + 512 * n + 512],
                              kb[n][:], 0.0, etmp[n][:], ALU.max, ALU.add),
                          waits=[(ac_e, ac_e.n)],
                          incs=[(ve_kf, 1)])
                    kf_done[(j, n)] = ve_kf.n
                    p.add("scalar",
                          lambda h, jj=jj, n=n: h.activation(
                              ap3(vh_sb[:], VW * NPAIR * jj + VW * 4 * n,
                                  [[VW, 4], [1, 128]]),
                              v_banks[n][:], AF.Copy),
                          waits=[(pe_v, pe_v.n - (1 - n)),
                                 (pe_kv, pe_kv.n)],
                          incs=[(ac_vh, 1)])
                    vh_done[(j, n)] = ac_vh.n

                if jj == 7:
                    kf_need = ve_kf.n
                    vh_need = ac_vh.n
                    for pr in range(NPAIR):
                        for t in range(8):
                            p.add("tensor",
                                  lambda h, pr=pr, t=t: h.matmul(
                                      pb[6][:, 0:129],
                                      kf_sb[:, D * t + 128 * pr:
                                            D * t + 128 * pr + 128],
                                      vh_sb[:, VW * NPAIR * t + VW * pr:
                                            VW * NPAIR * t + VW * pr + 129],
                                      start=(t == 0), stop=(t == 7)),
                                  waits=[(ve_kf, kf_need), (ac_vh, vh_need),
                                         (ve_acc, ve_acc.n)],
                                  incs=([(pe_kv, 1)] if t == 7 else []))
                        p.add("vector",
                              lambda h, pr=pr: h.tensor_tensor(
                                  acc_kv[:, VW * pr:VW * pr + 129],
                                  acc_kv[:, VW * pr:VW * pr + 129],
                                  pb[6][:, 0:129], ALU.add),
                              waits=[(pe_kv, pe_kv.n), (ve_acc, ve_acc.n)],
                              incs=[(ve_acc, 1)])

            p.add("gpsimd",
                  lambda h: h.dma_start(out=cc_in.ap(), in_=acc_kv[:]),
                  waits=[(ve_acc, ve_acc.n)], incs=[(cc, 16)])
            p.add("gpsimd",
                  lambda h: h.collective_compute(
                      "AllReduce", ALU.add,
                      replica_groups=[[0, 1], [2, 3], [4, 5], [6, 7]],
                      ins=[cc_in.ap().opt()], outs=[cc_out.ap().opt()]),
                  waits=[(cc, 16)], incs=[(cc2, 1)])
            p.add("gpsimd",
                  lambda h: h.dma_start(out=kvr[:], in_=cc_out.ap()),
                  waits=[(cc2, 1)], incs=[(cc3, 16)])
            for pr in range(NPAIR):
                o = VW * pr
                p.add("vector", lambda h, o=o: h.tensor_copy(
                    bd[0:64, o:o + 64], kvr[0:64, o:o + 64]),
                    waits=[(cc3, 16), (ve_bd, 1)])
                p.add("vector", lambda h, o=o: h.tensor_copy(
                    bd[0:64, o + 64:o + 65], kvr[0:64, o + 128:o + 129]))
                p.add("vector", lambda h, o=o: h.tensor_copy(
                    bd[64:128, o + 65:o + 129], kvr[64:128, o + 64:o + 128]))
                p.add("vector", lambda h, o=o: h.tensor_copy(
                    bd[64:128, o + 129:o + 130], kvr[64:128, o + 128:o + 129]),
                    incs=[(ve_bd, 1)])

            p.add("tensor", lambda h: h.nop())
            p.add("tensor", lambda h: h.nop(), incs=[(pe_endA, 1)])

        with contextlib.ExitStack() as stB:
            eB = stB.enter_context
            pt = eB(nc.psum_tensor("pt", [128, 1024], BF16))
            wq_sb = eB(nc.sbuf_tensor("wq_sb", [128, KC * D], BF16))
            wo_sb = eB(nc.sbuf_tensor("wo_sb", [128, KC * D], BF16))
            qf_sb = eB(nc.sbuf_tensor("qf_sb", [128, KC * 512], BF16))
            qtmp = [eB(nc.sbuf_tensor(f"qtmp{i}", [128, 512], BF16))
                    for i in range(2)]
            eqtmp = [eB(nc.sbuf_tensor(f"eqtmp{i}", [128, 512], BF16))
                     for i in range(2)]
            attn = [eB(nc.sbuf_tensor(f"attn{i}", [128, D], BF16))
                    for i in range(2)]
            attnT = [eB(nc.sbuf_tensor(f"attnT{i}", [128, D], BF16))
                     for i in range(2)]
            xf = [eB(nc.sbuf_tensor(f"xf{i}", [128, D], F32))
                  for i in range(2)]
            r1 = [eB(nc.sbuf_tensor(f"r1_{i}", [128, D], F32))
                  for i in range(2)]
            x1 = [eB(nc.sbuf_tensor(f"x1_{i}", [128, D], BF16))
                  for i in range(2)]

            BW = [(pe_endA, 1), (ve_kf, ve_kf.n), (ac_vh, ac_vh.n),
                  (ac_e, ac_e.n), (ve_km, ve_km.n), (ve_acc, ve_acc.n)]
            p.add("scalar", lambda h: w_to_sb(h, wq_sb, wq_bf.ap()),
                  waits=BW + [(ld_wqo, 32)], incs=[(ld_wq, 16)])
            p.add("scalar", lambda h: w_to_sb(h, wo_sb, wo_bf.ap()),
                  waits=BW + [(ld_wqo, 32)], incs=[(ld_wq, 16)])

            div_done = {}
            tc_done = {}
            wo_done = {}
            r1_done = {}
            x1_done = {}
            xf_need = {}
            num_banks = [(pb[2], 3, 0), (pb[3], 3, 3), (pb[4], 2, 6)]
            q_done = {}

            ln_state = {"i": 0, "applies": []}

            def add_ln(src_buf, dst_buf, src_sem, war_waits, store_fn):
                i = ln_state["i"]
                base = 46 + 6 * (i % 2)
                mv, std, inv, ng = base, base + 2, base + 3, base + 4
                guard = (ln_state["applies"][i - 2] if i >= 2 else 0)
                p.add("vector", lambda h: h.bn_stats(
                    smalls[:, 0:6], src_buf[:, 0:512]),
                    waits=[src_sem, (ve_bn, ve_bn.n)],
                    incs=[(ve_bs, 1)])
                p.add("vector", lambda h: h.bn_stats(
                    smalls[:, 6:12], src_buf[:, 512:1024]),
                    waits=[src_sem], incs=[(ve_bs, 1)])
                p.add("vector", lambda h: h.bn_aggr(
                    smalls[:, mv:mv + 2], smalls[:, 0:12]),
                    waits=[(ve_bs, ve_bs.n), (ac_x1, guard)],
                    incs=[(ve_bn, 1)])
                p.add("scalar", lambda h: h.activation(
                    smalls[:, std:std + 1], smalls[:, mv + 1:mv + 2], AF.Sqrt,
                    bias=smalls[:, 62:63]),
                    waits=[(ve_bn, ve_bn.n)], incs=[(ac_std, 1)])
                p.add("vector", lambda h: h.reciprocal(
                    smalls[:, inv:inv + 1], smalls[:, std:std + 1]),
                    waits=[(ac_std, ac_std.n)], incs=[(ve_rc, 1)])
                p.add("vector", lambda h: h.scalar_tensor_tensor(
                    smalls[:, ng:ng + 1], smalls[:, mv:mv + 1], -1.0,
                    smalls[:, inv:inv + 1], ALU.mult, ALU.mult),
                    waits=[(ve_rc, ve_rc.n), (ve_bn, ve_bn.n)],
                    incs=[(ve_ln, 1)])
                p.add("scalar",
                      lambda h: h.activation(
                          dst_buf[:], src_buf[:], AF.Identity,
                          scale=smalls[:, inv:inv + 1],
                          bias=smalls[:, ng:ng + 1]),
                      waits=[(ve_ln, ve_ln.n)] + war_waits,
                      incs=[(ac_x1, 1)])
                ln_state["applies"].append(ac_x1.n)
                ln_state["i"] = i + 1
                store_fn(ac_x1.n)

            for s in range(NSUP):
                for k in range(KC):
                    p.add("sync",
                          lambda h, s=s, k=k: h.dma_start(
                              out=xt[s % 2][:, 512 * k:512 * (k + 1)],
                              in_=x_bf.ap()[512 * s:512 * (s + 1),
                                            128 * k:128 * (k + 1)],
                              transpose=True),
                          waits=BW + [(ld_xc[s], 16),
                                      (pe_q, q_done.get(s - 2, 0))],
                          incs=[(xt_sp[s % 2], 16)])
                xt_need = xt_sp[s % 2].n
                for m in range(KC):
                    qb = pb[m % 2]
                    for k in range(KC):
                        p.add("tensor",
                              lambda h, m=m, k=k, s=s, qb=qb: h.matmul(
                                  qb[:],
                                  wq_sb[:, D * k + 128 * m:D * k + 128 * m + 128],
                                  xt[s % 2][:, 512 * k:512 * (k + 1)],
                                  start=(k == 0), stop=(k == KC - 1)),
                              waits=[(xt_sp[s % 2], xt_need), (ld_wq, 32),
                                     (ve_qf, max(0, ve_qf.n - 1))],
                              incs=([(pe_q, 1)] if k == KC - 1 else []))
                    if m == KC - 1:
                        q_done[s] = pe_q.n
                    p.add("vector",
                          lambda h, m=m, qb=qb: h.tensor_scalar_min(
                              qtmp[m % 2][:], qb[:], 0.0),
                          waits=[(pe_q, pe_q.n),
                                 (ac_qe, max(0, ac_qe.n - 1))],
                          incs=[(ve_qm, 1)])
                    p.add("scalar",
                          lambda h, m=m: h.activation(
                              eqtmp[m % 2][:], qtmp[m % 2][:], AF.Exp),
                          waits=[(ve_qm, ve_qm.n),
                                 (ve_qf, max(0, ve_qf.n - 1))],
                          incs=[(ac_qe, 1)])
                    p.add("vector",
                          lambda h, m=m, qb=qb: h.scalar_tensor_tensor(
                              qf_sb[:, 512 * m:512 * (m + 1)],
                              qb[:], 0.0, eqtmp[m % 2][:], ALU.max, ALU.add),
                          waits=[(ac_qe, ac_qe.n)],
                          incs=[(ve_qf, 1)])
                qf_need = ve_qf.n

                num_done = {}

                def stage_xf_num(jq, s=s):
                    prev = (s, jq - 2) if jq >= 2 else (s - 1, jq + 2)
                    p.add("gpsimd",
                          lambda h: h.dma_start(
                              out=xf[jq % 2][:],
                              in_=x_in[512 * s + 128 * jq:
                                       512 * s + 128 * jq + 128, :]),
                          waits=BW + [(ve_r1, r1_done.get(prev, 0))],
                          incs=[(ld_xfp[jq % 2], 16)])
                    xf_need[(s, jq)] = ld_xfp[jq % 2].n
                    for bank, cnt, p0 in num_banks:
                        for q in range(cnt):
                            pr = p0 + q
                            p.add("tensor",
                                  lambda h, bank=bank, q=q, pr=pr: h.matmul(
                                      bank[:, VW * q:VW * q + VW],
                                      qf_sb[:, 512 * pr + 128 * jq:
                                            512 * pr + 128 * jq + 128],
                                      bd[:, VW * pr:VW * (pr + 1)],
                                      start=True, stop=True),
                                  waits=[(ve_qf, qf_need), (ve_bd, 9),
                                         (ve_div, div_done.get((s, jq - 1), 0))],
                                  incs=([(pe_num, 1)] if q == cnt - 1 else []))
                    num_done[(s, jq)] = pe_num.n

                def stage_div(jq, s=s):
                    for bank, cnt, p0 in num_banks:
                        p.add("vector",
                              lambda h, bank=bank, cnt=cnt, p0=p0:
                              h.tensor_scalar(
                                  smalls[:, 12 + 2 * p0:12 + 2 * (p0 + cnt)],
                                  ap3(bank[:], 64, [[VW, cnt], [65, 2]]),
                                  EPS_ATTN, None, ALU.add),
                              waits=[(pe_num, num_done[(s, jq)]),
                                     (ve_rc, ve_rc.n)],
                              incs=[(ve_dv, 1)])
                    dv_now = ve_dv.n
                    for bank, cnt, p0 in num_banks:
                        p.add("vector",
                              lambda h, cnt=cnt, p0=p0: h.reciprocal(
                                  smalls[:, 28 + 2 * p0:28 + 2 * (p0 + cnt)],
                                  smalls[:, 12 + 2 * p0:12 + 2 * (p0 + cnt)]),
                              waits=[(ve_dv, dv_now),
                                     (ve_div, ve_div.n)],
                              incs=[(ve_rc, 1)])
                    rc_now = ve_rc.n
                    for bank, cnt, p0 in num_banks:
                        p.add("vector",
                              lambda h, bank=bank, cnt=cnt, p0=p0:
                              h.scalar_tensor_tensor(
                                  ap3(attn[jq % 2][:], 128 * p0,
                                      [[128, cnt], [64, 2], [1, 64]]),
                                  ap3(bank[:], 0, [[VW, cnt], [65, 2], [1, 64]]),
                                  1.0,
                                  ap3(smalls[:], 28 + 2 * p0,
                                      [[2, cnt], [1, 2], [0, 64]]),
                                  ALU.mult, ALU.mult),
                              waits=[(ve_rc, rc_now),
                                     (ac_tc, tc_done.get((s, jq - 2), 0))],
                              incs=[(ve_div, 1)])
                    div_done[(s, jq)] = ve_div.n

                def stage_tail(jq, s=s):
                    for k in range(KC):
                        p.add("tensor",
                              lambda h, k=k: h.transpose(
                                  pt[:, 128 * k:128 * (k + 1)],
                                  attn[jq % 2][:, 128 * k:128 * (k + 1)],
                                  ident[:]),
                              waits=[(ve_div, div_done[(s, jq)]),
                                     (ac_tc, tc_done.get((s, jq - 1), 0)),
                                     (dma_misc, 16)],
                              incs=([(pe_t, 1)] if k == KC - 1 else []))
                    if jq + 1 < 4:
                        stage_xf_num(jq + 1)
                        stage_div(jq + 1)
                    p.add("scalar",
                          lambda h: h.activation(
                              attnT[jq % 2][:], pt[:], AF.Copy),
                          waits=[(pe_t, pe_t.n),
                                 (pe_wo, wo_done.get((s, jq - 2), 0))],
                          incs=[(ac_tc, 1)])
                    tc_done[(s, jq)] = ac_tc.n
                    if jq >= 1:
                        stage_ln1(jq - 1)
                    for k in range(KC):
                        for n in range(2):
                            p.add("tensor",
                                  lambda h, k=k, n=n: h.matmul(
                                      pb[5 + n][:],
                                      attnT[jq % 2][:, 128 * k:128 * (k + 1)],
                                      wo_sb[:, D * k + 512 * n:
                                            D * k + 512 * (n + 1)],
                                      start=(k == 0), stop=(k == KC - 1)),
                                  waits=[(ac_tc, ac_tc.n), (ld_wq, 32),
                                         (ve_r1, r1_done.get((s, jq - 1), 0))],
                                  incs=([(pe_wo, 1)] if k == KC - 1 else []))
                    wo_done[(s, jq)] = pe_wo.n
                    for n in range(2):
                        p.add("vector",
                              lambda h, n=n: h.scalar_tensor_tensor(
                                  r1[jq % 2][:, 512 * n:512 * (n + 1)],
                                  pb[5 + n][:], 1.0,
                                  xf[jq % 2][:, 512 * n:512 * (n + 1)],
                                  ALU.mult, ALU.add),
                              waits=[(pe_wo, pe_wo.n),
                                     (ld_xfp[jq % 2], xf_need[(s, jq)]),
                                     (ac_x1, x1_done.get((s, jq - 2), 0))],
                              incs=[(ve_r1, 1)])
                    r1_done[(s, jq)] = ve_r1.n

                def stage_ln1(jq, s=s):
                    def store_x1(apply_n):
                        p.add("scalar",
                              lambda h: h.dma_start(
                                  out=x1_bf.ap()[512 * s + 128 * jq:
                                                 512 * s + 128 * jq + 128, :],
                                  in_=x1[jq % 2][:]),
                              waits=[(ac_x1, apply_n)],
                              incs=[(st_x1p[jq % 2], 16)])
                    add_ln(r1[jq % 2], x1[jq % 2],
                           (ve_r1, r1_done[(s, jq)]),
                           [(st_x1p[jq % 2], st_x1p[jq % 2].n)], store_x1)
                    x1_done[(s, jq)] = ac_x1.n

                stage_xf_num(0)
                stage_div(0)
                for jq in range(4):
                    stage_tail(jq)
                stage_ln1(3)
            p.add("tensor", lambda h: h.transpose(
                pt[:, 0:128], attn[1][:, 0:128], ident[:]),
                waits=[(ac_tc, ac_tc.n)], incs=[(bar_b, 1)])
            p.add("vector", lambda h: h.nop(),
                  waits=[(ac_x1, ac_x1.n)], incs=[(bar_b, 1)])
            p.add("scalar", lambda h: h.nop(),
                  incs=[(bar_b, 1)])

        with contextlib.ExitStack() as stC:
            eC = stC.enter_context
            pb7 = eC(nc.psum_tensor("pb7", [128, 512], F32))
            h2t = eC(nc.sbuf_tensor("h2t", [128, MC * 512], BF16))
            sg = [eC(nc.sbuf_tensor(f"sg{i}", [128, 512], BF16))
                  for i in range(2)]
            ub = [eC(nc.sbuf_tensor(f"ub{i}", [128, 512], BF16))
                  for i in range(2)]
            x1j = [eC(nc.sbuf_tensor(f"x1j{i}", [128, D], BF16))
                   for i in range(4)]
            dbanks = pb + [pb7]
            r2 = [eC(nc.sbuf_tensor(f"r2_{i}", [128, D], F32))
                  for i in range(2)]
            x2 = [eC(nc.sbuf_tensor(f"x2_{i}", [128, D], F32))
                  for i in range(2)]

            CW = [(bar_b, 3)]
            u_done = {}
            h2_done = {}
            h2_seq = []
            r2_done = {}
            x2_done = {}
            x1j_need = {}
            strip_done = []

            x1t_need_p = {}
            pe_d_bases = {}

            def do_r2_ln2(sp, jq):
                for n in range(2):
                    p.add("vector",
                          lambda h, sp=sp, jq=jq, n=n:
                          h.scalar_tensor_tensor(
                              r2[jq % 2][:, 512 * n:512 * (n + 1)],
                              dbanks[2 * jq + n][:], S127SQ,
                              x1j[jq][:, 512 * n:512 * (n + 1)],
                              ALU.mult, ALU.add),
                          waits=[(pe_d, pe_d_bases[sp] + jq + 1),
                                 (ld_x1jq[jq], x1j_need[(sp, jq)]),
                                 (ac_x1, x2_done.get((sp, jq - 2), 0))],
                          incs=[(ve_r2, 1)])
                r2_done[(sp, jq)] = ve_r2.n

                def store_out(apply_n, sp=sp, jq=jq):
                    p.add("scalar",
                          lambda h: h.dma_start(
                              out=out[512 * sp + 128 * jq:
                                      512 * sp + 128 * jq + 128, :],
                              in_=x2[jq % 2][:]),
                          waits=[(ac_x1, apply_n)],
                          incs=[(st_outp[jq % 2], 16)])
                add_ln(r2[jq % 2], x2[jq % 2], (ve_r2, ve_r2.n),
                       [(st_outp[jq % 2], st_outp[jq % 2].n)],
                       store_out)
                x2_done[(sp, jq)] = ac_x1.n

            def add_x1t(s):
                sn = [(st_x1p[0], 32 * (s + 1)), (st_x1p[1], 32 * (s + 1))]
                war = ([(pe_u, u_done[s - 2])] if s >= 2
                       else [(pe_q, q_done[6 + s])])
                for k in range(KC):
                    p.add("sync",
                          lambda h, s=s, k=k: h.dma_start(
                              out=xt[s % 2][:, 512 * k:512 * (k + 1)],
                              in_=x1_bf.ap()[512 * s:512 * (s + 1),
                                             128 * k:128 * (k + 1)],
                              transpose=True),
                          waits=sn + war,
                          incs=[(xt_sp[s % 2], 16)])
                x1t_need_p[s] = xt_sp[s % 2].n

            add_x1t(0)
            for s in range(NSUP):
                st_need = [(st_x1p[0], 32 * (s + 1)), (st_x1p[1], 32 * (s + 1))]
                x1t_need = x1t_need_p[s]
                for m in range(MC):
                    if s >= 1 and m < 4:
                        do_r2_ln2(s - 1, m)
                    gb, ubk = pb[m % 2 * 2], pb[m % 2 * 2 + 1]
                    gw = ([(ve_r2, r2_done.get((s - 1, m), 0))]
                          if m < 2 else [])
                    for k in range(KC):
                        p.add("tensor",
                              lambda h, m=m, k=k, s=s, gb=gb: h.matmul(
                                  gb[:],
                                  wgt[:, HID * k + 128 * m:HID * k + 128 * m + 128],
                                  xt[s % 2][:, 512 * k:512 * (k + 1)],
                                  start=(k == 0), stop=(k == KC - 1)),
                              waits=CW + [(xt_sp[s % 2], x1t_need),
                                          (ld_wgt, 128),
                                          (ac_sg, max(0, ac_sg.n - 1))] +
                              (gw if k == 0 else []),
                              incs=([(pe_g, 1)] if k == KC - 1 else []))
                    for k in range(KC):
                        p.add("tensor",
                              lambda h, m=m, k=k, s=s, ubk=ubk: h.matmul(
                                  ubk[:],
                                  wut[:, HID * k + 128 * m:HID * k + 128 * m + 128],
                                  xt[s % 2][:, 512 * k:512 * (k + 1)],
                                  start=(k == 0), stop=(k == KC - 1)),
                              waits=CW + [(ld_wut, 128),
                                          (ac_ub, max(0, ac_ub.n - 1))] +
                              (gw if k == 0 else []),
                              incs=([(pe_u, 1)] if k == KC - 1 else []))
                    h2_war = h2_seq[-2] if len(h2_seq) >= 2 else 0
                    p.add("scalar",
                          lambda h, m=m, gb=gb: h.activation(
                              sg[m % 2][:], gb[:], AF.Silu, scale=S127),
                          waits=[(pe_g, pe_g.n), (ve_h2, h2_war)],
                          incs=[(ac_sg, 1)])
                    p.add("scalar",
                          lambda h, m=m, ubk=ubk: h.activation(
                              ub[m % 2][:], ubk[:], AF.Copy),
                          waits=[(pe_u, pe_u.n), (ve_h2, h2_war)],
                          incs=[(ac_ub, 1)])
                    p.add("vector",
                          lambda h, m=m: h.tensor_tensor(
                              h2t[:, 512 * m:512 * (m + 1)],
                              sg[m % 2][:], ub[m % 2][:], ALU.mult),
                          waits=[(ac_sg, ac_sg.n), (ac_ub, ac_ub.n),
                                 (pe_d, 4 * s)],
                          incs=[(ve_h2, 1)])
                    h2_done[(s, m)] = ve_h2.n
                    h2_seq.append(ve_h2.n)
                u_done[s] = pe_u.n
                if s + 1 < NSUP:
                    add_x1t(s + 1)
                for jq in range(4):
                    p.add("scalar",
                          lambda h, s=s, jq=jq: h.dma_start(
                              out=x1j[jq][:],
                              in_=x1_bf.ap()[512 * s + 128 * jq:
                                             512 * s + 128 * jq + 128, :]),
                          waits=CW + st_need +
                          [(ve_r2, r2_done.get((s - 1, jq), 0))],
                          incs=[(ld_x1jq[jq], 16)])
                    x1j_need[(s, jq)] = ld_x1jq[jq].n
                pe_d_base = pe_d.n
                pe_d_bases[s] = pe_d_base
                for k in range(MC):
                    kp = k // 2
                    if k % 2 == 0:
                        g = len(strip_done)
                        strip_war = (strip_done[g - 2] if g >= 2
                                     else (pe_ds, 0))
                        p.add("sync",
                              lambda h, k=k: h.dma_start(
                                  out=wdt[(k // 2) % 2][:]
                                  .rearrange("p (b d) -> p b d", b=2),
                                  in_=wdt_bf.ap()[128 * k:128 * (k + 2), :]
                                  .rearrange("(b p) d -> p b d", p=128)),
                              waits=[(mk_s[0], 16 * (MC // 2)),
                                     (mk_s[1], 16 * (MC // 2)),
                                     strip_war],
                              incs=[(ld_wdtp[(k // 2) % 2], 16)])
                        wdt_need = ld_wdtp[kp % 2].n
                    for n in range(2):
                        for jq in range(4):
                            xw = []
                            if k == 0:
                                xw = [(ac_sg, ac_sg.n), (ac_ub, ac_ub.n),
                                      (ve_r2, r2_done.get((s - 1, 3), 0))]
                            incs = []
                            if k == MC - 1 and n == 1:
                                incs = [(pe_d, 1)]
                            elif n == 1 and jq == 3 and k % 2 == 1:
                                incs = [(pe_ds, 1)]
                            elif k == MC - 1 and n == 0 and jq == 3:
                                incs = [(pe_ds, 1)]
                            p.add("tensor",
                                  lambda h, k=k, jq=jq, n=n: h.matmul(
                                      dbanks[2 * jq + n][:],
                                      h2t[:, 512 * k + 128 * jq:
                                           512 * k + 128 * jq + 128],
                                      wdt[(k // 2) % 2][:, 1024 * (k % 2) +
                                                        512 * n:
                                                        1024 * (k % 2) +
                                                        512 * (n + 1)],
                                      start=(k == 0), stop=(k == MC - 1)),
                                  waits=[(ve_h2, h2_done[(s, k)]),
                                         (ld_wdtp[kp % 2], wdt_need)] + xw,
                                  incs=incs)
                    if k % 2 == 1:
                        strip_done.append((pe_d, pe_d_base + 4)
                                          if k == MC - 1
                                          else (pe_ds, pe_ds.n))

                if s == NSUP - 1:
                    for jq in range(4):
                        do_r2_ln2(s, jq)
            p.add("scalar", lambda h: h.nop(),
                  waits=[(st_outp[0], 16 * NJ // 2),
                         (st_outp[1], 16 * NJ // 2)])
            if PERTURB == 1:
                p.add("scalar", lambda h: h.nop())
            elif PERTURB == 2:
                p.add("scalar", lambda h: h.nop())
                p.add("vector", lambda h: h.nop())
            elif PERTURB == 5:
                p.add("tensor", lambda h: h.nop())
            elif PERTURB == 6:
                for _ in range(3):
                    p.add("scalar", lambda h: h.nop())
                    p.add("vector", lambda h: h.nop())
                    p.add("tensor", lambda h: h.nop())

        with nc.Block() as block:
            p.emit(block)

    nc._prog_n_ops = p.n_ops
    return nc


_NC_CACHE = {}


def kernel(L0_attn_x, L0_attn_Wq, L0_attn_Wk, L0_attn_Wv, L0_attn_Wo,
           L0_ffn_w_gate, L0_ffn_w_up, L0_ffn_w_down):
    x = np.ascontiguousarray(np.asarray(L0_attn_x, dtype=np.float32))
    wq = np.ascontiguousarray(np.asarray(L0_attn_Wq, dtype=np.float32))
    wk = np.ascontiguousarray(np.asarray(L0_attn_Wk, dtype=np.float32))
    wv = np.ascontiguousarray(np.asarray(L0_attn_Wv, dtype=np.float32))
    wo = np.ascontiguousarray(np.asarray(L0_attn_Wo, dtype=np.float32))
    wg = np.ascontiguousarray(np.asarray(L0_ffn_w_gate, dtype=np.int32))
    wu = np.ascontiguousarray(np.asarray(L0_ffn_w_up, dtype=np.int32))
    wd = np.ascontiguousarray(np.asarray(L0_ffn_w_down, dtype=np.int32))
    ident = np.eye(128, dtype=ml_dtypes.bfloat16)

    if "nc" not in _NC_CACHE:
        _NC_CACHE["nc"] = build_program()
    nc = _NC_CACHE["nc"]

    in_maps = []
    for c in range(8):
        b, half = c // 2, c % 2
        xo = np.ascontiguousarray(x[b, half * T:(half + 1) * T, :])
        in_maps.append(dict(x=xo, wq=wq, wk=wk, wv=wv, wo=wo,
                            wg=wg, wu=wu, wd=wd, ident=ident))

    res = run_bass_kernel_spmd(nc, in_maps, core_ids=list(range(8)))

    outp = np.empty((4, 8192, 1024), dtype=np.float32)
    for c in range(8):
        b, half = c // 2, c % 2
        outp[b, half * T:(half + 1) * T, :] = np.asarray(res.results[c]["out"])
    return outp


if __name__ == "__main__":
    nc = build_program()
    print("built OK; instructions:", nc.n_instructions())


# version 19
# speedup vs baseline: 1.1284x; 1.0188x over previous
import contextlib
import numpy as np
import ml_dtypes
import concourse.bass as bass
import concourse.mybir as mybir
from concourse.bass_utils import run_bass_kernel_spmd

F32 = mybir.dt.float32
BF16 = mybir.dt.bfloat16
I32 = mybir.dt.int32
AF = mybir.ActivationFunctionType
ALU = mybir.AluOpType

T = 4096
D = 1024
HID = 3072
NSUP = 8
NJ = 32
KC = D // 128
MC = HID // 128
NPAIR = 8
VW = 130
EPS_ATTN = 1e-6
EPS_LN = 1e-5
S127 = 1.0 / 127.0
S127SQ = float(np.float32(1.0) / np.float32(127.0)) ** 2

ENGINES = ("sync", "tensor", "vector", "scalar", "gpsimd")


class Sem:
    __slots__ = ("h", "n", "name")

    def __init__(self, h, name):
        self.h = h
        self.n = 0
        self.name = name


class Prog:

    def __init__(self, nc, stack, max_ops=None):
        self.nc = nc
        self.stack = stack
        self.ops = {e: [] for e in ENGINES}
        self._sems = []
        self.n_ops = 0
        self.max_ops = max_ops

    def sem(self, name):
        s = Sem(self.stack.enter_context(self.nc.semaphore(name)), name)
        self._sems.append(s)
        return s

    def add(self, eng, fn, waits=(), incs=()):
        self.n_ops += 1
        if self.max_ops is not None and self.n_ops > self.max_ops:
            for s, k in incs:
                s.n += k
            return
        ws = [(s, int(v)) for (s, v) in waits if v > 0]
        self.ops[eng].append((fn, ws, list(incs)))
        for s, k in incs:
            s.n += k

    def emit(self, block):
        for eng in ENGINES:
            ops = self.ops[eng]
            if not ops:
                continue

            def body(handle, ops=ops):
                waited = {}
                for fn, ws, incs in ops:
                    for s, v in ws:
                        if waited.get(s, 0) < v:
                            handle.wait_ge(s.h, v)
                            waited[s] = v
                    instr = fn(handle)
                    for s, k in incs:
                        instr.then_inc(s.h, k)

            getattr(block, eng)(body)


PERTURB = 3


def build_program(max_ops=None):
    nc = bass.Bass("TRN2", target_bir_lowering=False)

    x_in = nc.dram_tensor("x", [T, D], F32, kind="ExternalInput").ap()
    wq_in = nc.dram_tensor("wq", [D, D], F32, kind="ExternalInput").ap()
    wk_in = nc.dram_tensor("wk", [D, D], F32, kind="ExternalInput").ap()
    wv_in = nc.dram_tensor("wv", [D, D], F32, kind="ExternalInput").ap()
    wo_in = nc.dram_tensor("wo", [D, D], F32, kind="ExternalInput").ap()
    wg_in = nc.dram_tensor("wg", [HID, D], I32, kind="ExternalInput").ap()
    wu_in = nc.dram_tensor("wu", [HID, D], I32, kind="ExternalInput").ap()
    wd_in = nc.dram_tensor("wd", [D, HID], I32, kind="ExternalInput").ap()
    id_in = nc.dram_tensor("ident", [128, 128], BF16, kind="ExternalInput").ap()
    out = nc.dram_tensor("out", [T, D], F32, kind="ExternalOutput").ap()

    x_bf = nc.dram_tensor("x_bf", [T, D], BF16)
    wq_bf = nc.dram_tensor("wq_bf", [128, KC * D], BF16)
    wo_bf = nc.dram_tensor("wo_bf", [128, KC * D], BF16)
    wg_bf = nc.dram_tensor("wg_bf", [HID, D], BF16)
    wu_bf = nc.dram_tensor("wu_bf", [HID, D], BF16)
    wd_bf = nc.dram_tensor("wd_bf", [D, HID], BF16)
    wdt_bf = nc.dram_tensor("wdt_bf", [HID, D], BF16)
    x1_bf = nc.dram_tensor("x1_bf", [T, D], BF16)
    cc_in = nc.dram_tensor("cc_in", [128, NPAIR * VW], F32)
    cc_out = nc.dram_tensor("cc_out", [128, NPAIR * VW], F32)

    def ap3(t_ap, offset, dims):
        return bass.AP(t_ap.tensor, t_ap.offset + offset,
                       [list(t_ap.ap[0])] + [list(d) for d in dims])

    with contextlib.ExitStack() as st0:
        e0 = st0.enter_context
        p = Prog(nc, st0, max_ops=max_ops)

        ident = e0(nc.sbuf_tensor("ident_sb", [128, 128], BF16))
        wgt = e0(nc.sbuf_tensor("wgt", [128, KC * HID], BF16))
        wut = e0(nc.sbuf_tensor("wut", [128, KC * HID], BF16))
        xt = [e0(nc.sbuf_tensor(f"xt{i}", [128, KC * 512], BF16,
                                is_transpose_dest=True)) for i in range(2)]
        bd = e0(nc.sbuf_tensor("bd", [128, NPAIR * VW], BF16))
        wdt = [e0(nc.sbuf_tensor(f"wdt{i}", [128, 2 * D], BF16,
                                 is_transpose_dest=True))
               for i in range(2)]
        smalls = e0(nc.sbuf_tensor("smalls", [128, 64], F32))
        acc_kv = e0(nc.sbuf_tensor("acc_kv", [128, NPAIR * VW], F32))
        kvr = e0(nc.sbuf_tensor("kvr", [128, NPAIR * VW], F32))
        pb = [e0(nc.psum_tensor(f"pb{i}", [128, 512], F32)) for i in range(7)]

        if PERTURB == 3:
            p.sem("pad0")
        ld_xc = [p.sem(f"ld_xc{i}") for i in range(NSUP)]
        ld_wkv = p.sem("ld_wkv")
        ld_wqo = p.sem("ld_wqo")
        ld_wf = p.sem("ld_wf")
        mk_t = [p.sem(f"mk_t{i}") for i in range(4)]
        mk_s = [p.sem(f"mk_s{i}") for i in range(4)]
        ld_wq = p.sem("ld_wq")
        ld_wgt = p.sem("ld_wgt")
        ld_wut = p.sem("ld_wut")
        xt_sp = [p.sem("xt_s0"), p.sem("xt_s1")]
        ld_xfp = [p.sem("ld_xf0"), p.sem("ld_xf1")]
        ld_wdtp = [p.sem("ld_wdt0"), p.sem("ld_wdt1")]
        ld_x1jq = [p.sem(f"ld_x1j{i}") for i in range(4)]
        st_x1p = [p.sem("st_x10"), p.sem("st_x11")]
        st_outp = [p.sem("st_out0"), p.sem("st_out1")]
        cc = p.sem("cc")
        cc2 = p.sem("cc2")
        cc3 = p.sem("cc3")
        pe_k = p.sem("pe_k")
        pe_v = p.sem("pe_v")
        pe_kv = p.sem("pe_kv")
        pe_q = p.sem("pe_q")
        pe_num = p.sem("pe_num")
        pe_t = p.sem("pe_t")
        pe_wo = p.sem("pe_wo")
        pe_g = p.sem("pe_g")
        pe_u = p.sem("pe_u")
        pe_d = p.sem("pe_d")
        ve_km = p.sem("ve_km")
        ve_kf = p.sem("ve_kf")
        ve_acc = p.sem("ve_acc")
        ve_bd = p.sem("ve_bd")
        ve_qm = p.sem("ve_qm")
        ve_qf = p.sem("ve_qf")
        ve_div = p.sem("ve_div")
        ve_r1 = p.sem("ve_r1")
        ve_ln = p.sem("ve_ln")
        ve_h2 = p.sem("ve_h2")
        ve_r2 = p.sem("ve_r2")
        ve_ln2 = p.sem("ve_ln2")
        ve_bn = p.sem("ve_bn")
        ve_bs = p.sem("ve_bs")
        ve_dv = p.sem("ve_dv")
        ve_rc = p.sem("ve_rc")
        pe_ds = p.sem("pe_ds")
        ac_e = p.sem("ac_e")
        ac_vh = p.sem("ac_vh")
        ac_qe = p.sem("ac_qe")
        ac_tc = p.sem("ac_tc")
        ac_x1 = p.sem("ac_x1")
        ac_sg = p.sem("ac_sg")
        ac_ub = p.sem("ac_ub")
        ac_std = p.sem("ac_std")
        bar_a = p.sem("bar_a")
        pe_endA = p.sem("pe_endA")
        bar_b = p.sem("bar_b")
        dma_misc = p.sem("dma_misc")

        p.add("sync", lambda h: h.dma_start(out=ident[:], in_=id_in),
              incs=[(dma_misc, 16)])
        stA = contextlib.ExitStack()
        eA = stA.enter_context
        wk_sb = eA(nc.sbuf_tensor("wk_sb", [128, KC * D], BF16))
        wv_sb = eA(nc.sbuf_tensor("wv_sb", [128, KC * D], BF16))

        def w_to_sb(h, dst, src):
            return h.dma_start(
                out=dst[:].rearrange("p (k d) -> p k d", k=KC),
                in_=src.rearrange("(k p) d -> p k d", p=128))

        def x_cast(s):
            p.add("gpsimd",
                  lambda h, s=s: h.dma_start(
                      out=x_bf.ap()[512 * s:512 * (s + 1), :],
                      in_=x_in[512 * s:512 * (s + 1), :]),
                  incs=[(ld_xc[s], 16)])

        p.add("gpsimd", lambda h: w_to_sb(h, wk_sb, wk_in), incs=[(ld_wkv, 16)])
        p.add("gpsimd", lambda h: w_to_sb(h, wv_sb, wv_in), incs=[(ld_wkv, 16)])
        for s in range(NSUP):
            x_cast(s)
        p.add("gpsimd", lambda h: h.dma_start(
            out=wq_bf.ap().rearrange("p (k d) -> p k d", k=KC),
            in_=wq_in.rearrange("(k p) d -> p k d", p=128)),
            incs=[(ld_wqo, 16)])
        p.add("gpsimd", lambda h: h.dma_start(
            out=wo_bf.ap().rearrange("p (k d) -> p k d", k=KC),
            in_=wo_in.rearrange("(k p) d -> p k d", p=128)),
            incs=[(ld_wqo, 16)])
        p.add("gpsimd", lambda h: h.dma_start(out=wg_bf.ap(), in_=wg_in),
              incs=[(ld_wf, 16)])
        p.add("gpsimd", lambda h: h.dma_start(out=wu_bf.ap(), in_=wu_in),
              incs=[(ld_wf, 16)])
        p.add("gpsimd", lambda h: h.dma_start(out=wd_bf.ap(), in_=wd_in),
              incs=[(ld_wf, 16)])
        xt_need_p = {}
        v_done = {}

        def add_xtA(s):
            for k in range(KC):
                p.add("sync",
                      lambda h, s=s, k=k: h.dma_start(
                          out=xt[s % 2][:, 512 * k:512 * (k + 1)],
                          in_=x_bf.ap()[512 * s:512 * (s + 1),
                                        128 * k:128 * (k + 1)],
                          transpose=True),
                      waits=[(ld_xc[s], 16),
                             (pe_v, v_done.get(s - 2, 0))],
                      incs=[(xt_sp[s % 2], 16)])
            xt_need_p[s] = xt_sp[s % 2].n

        add_xtA(0)
        add_xtA(1)

        deferred_tr = []
        for k in range(KC):
            deferred_tr.append(lambda k=k: p.add(
                "sync",
                lambda h: h.dma_start(
                    out=wgt[:, HID * k:HID * (k + 1)],
                    in_=wg_bf.ap()[:, 128 * k:128 * (k + 1)], transpose=True),
                waits=[(ld_wf, 48)], incs=[(ld_wgt, 16)]))
        for k in range(KC):
            deferred_tr.append(lambda k=k: p.add(
                "sync",
                lambda h: h.dma_start(
                    out=wut[:, HID * k:HID * (k + 1)],
                    in_=wu_bf.ap()[:, 128 * k:128 * (k + 1)], transpose=True),
                waits=[(ld_wf, 48)], incs=[(ld_wut, 16)]))

        def mat_strip(k):
            q = k % 4
            sl_lo = D * (q % 2)
            p.add("sync",
                  lambda h, k=k, q=q, sl_lo=sl_lo: h.dma_start(
                      out=wdt[q // 2][:, sl_lo:sl_lo + D],
                      in_=wd_bf.ap()[:, 128 * k:128 * (k + 1)], transpose=True),
                  waits=[(ld_wf, 48), (mk_s[q], mk_s[q].n)],
                  incs=[(mk_t[q], 16)])
            p.add("scalar",
                  lambda h, k=k, q=q, sl_lo=sl_lo: h.dma_start(
                      out=wdt_bf.ap()[128 * k:128 * (k + 1), :],
                      in_=wdt[q // 2][:, sl_lo:sl_lo + D]),
                  waits=[(mk_t[q], mk_t[q].n)], incs=[(mk_s[q], 16)])
        for k in range(MC):
            deferred_tr.append(lambda k=k: mat_strip(k))

        with stA:
            kf_sb = eA(nc.sbuf_tensor("kf_sb", [128, 8 * D], BF16))
            vh_sb = eA(nc.sbuf_tensor("vh_sb", [128, 8 * NPAIR * VW], BF16))
            ktmp = [eA(nc.sbuf_tensor(f"ktmp{i}", [128, 512], BF16))
                    for i in range(2)]
            etmp = [eA(nc.sbuf_tensor(f"etmp{i}", [128, 512], BF16))
                    for i in range(2)]

            if PERTURB == 4:
                p.add("vector", lambda h: h.memset(smalls[:, 34:35], 0.0))
            p.add("vector", lambda h: h.memset(smalls[:, 62:63], EPS_LN),
                  incs=[(ve_bn, 1)])
            p.add("vector", lambda h: h.memset(acc_kv[:], 0.0),
                  incs=[(ve_acc, 1)])
            p.add("vector", lambda h: h.memset(bd[:], 0.0), incs=[(ve_bd, 1)])
            p.add("vector",
                  lambda h: h.memset(
                      ap3(vh_sb[:], 128, [[VW * NPAIR, 8], [VW, NPAIR], [1, 1]]),
                      1.0),
                  incs=[(ve_acc, 1)])

            kf_done = {}
            vh_done = {}
            k_banks = [(pb[0], pb[1]), (pb[4], pb[5])]
            v_banks = (pb[2], pb[3])

            for j in range(NJ):
                s, jq = j // 4, j % 4
                jj = j % 8
                if jq == 0 and s >= 2:
                    add_xtA(s)
                if jq == 0 and s >= 3:
                    for _ in range(8):
                        if deferred_tr:
                            deferred_tr.pop(0)()
                kb = k_banks[j % 2]
                for k in range(KC):
                    for n in range(2):
                        p.add("tensor",
                              lambda h, j=j, k=k, n=n, kb=kb: h.matmul(
                                  kb[n][:],
                                  xt[(j // 4) % 2][:, 512 * k + 128 * (j % 4):
                                                   512 * k + 128 * (j % 4) + 128],
                                  wk_sb[:, D * k + 512 * n:D * k + 512 * (n + 1)],
                                  start=(k == 0), stop=(k == KC - 1)),
                              waits=[(xt_sp[s % 2], xt_need_p[s]),
                                     (ld_wkv, 32),
                                     (ve_kf, kf_done.get((j - 2, n), 0))],
                              incs=([(pe_k, 1)] if k == KC - 1 else []))
                for k in range(KC):
                    for n in range(2):
                        p.add("tensor",
                              lambda h, j=j, k=k, n=n: h.matmul(
                                  v_banks[n][:],
                                  xt[(j // 4) % 2][:, 512 * k + 128 * (j % 4):
                                                   512 * k + 128 * (j % 4) + 128],
                                  wv_sb[:, D * k + 512 * n:D * k + 512 * (n + 1)],
                                  start=(k == 0), stop=(k == KC - 1)),
                              waits=[(ld_wkv, 32),
                                     (ac_vh, vh_done.get((j - 1, n), 0))],
                              incs=([(pe_v, 1)] if k == KC - 1 else []))
                if jq == 3:
                    v_done[s] = pe_v.n
                for n in range(2):
                    p.add("vector",
                          lambda h, n=n, kb=kb: h.tensor_scalar_min(
                              ktmp[n][:], kb[n][:], 0.0),
                          waits=[(pe_k, pe_k.n - (1 - n)),
                                 (ac_e, ac_e.n - 2 + n if j > 0 or n > 0 else 0)],
                          incs=[(ve_km, 1)])
                    p.add("scalar",
                          lambda h, n=n: h.activation(
                              etmp[n][:], ktmp[n][:], AF.Exp),
                          waits=[(ve_km, ve_km.n),
                                 (ve_kf, ve_kf.n - 1 if (j, n) != (0, 0) else 0)],
                          incs=[(ac_e, 1)])
                    p.add("vector",
                          lambda h, jj=jj, n=n, kb=kb: h.scalar_tensor_tensor(
                              kf_sb[:, D * jj + 512 * n:D * jj + 512 * n + 512],
                              kb[n][:], 0.0, etmp[n][:], ALU.max, ALU.add),
                          waits=[(ac_e, ac_e.n)],
                          incs=[(ve_kf, 1)])
                    kf_done[(j, n)] = ve_kf.n
                    p.add("scalar",
                          lambda h, jj=jj, n=n: h.activation(
                              ap3(vh_sb[:], VW * NPAIR * jj + VW * 4 * n,
                                  [[VW, 4], [1, 128]]),
                              v_banks[n][:], AF.Copy),
                          waits=[(pe_v, pe_v.n - (1 - n)),
                                 (pe_kv, pe_kv.n)],
                          incs=[(ac_vh, 1)])
                    vh_done[(j, n)] = ac_vh.n

                if jj == 7:
                    kf_need = ve_kf.n
                    vh_need = ac_vh.n
                    for pr in range(NPAIR):
                        for t in range(8):
                            p.add("tensor",
                                  lambda h, pr=pr, t=t: h.matmul(
                                      pb[6][:, 0:129],
                                      kf_sb[:, D * t + 128 * pr:
                                            D * t + 128 * pr + 128],
                                      vh_sb[:, VW * NPAIR * t + VW * pr:
                                            VW * NPAIR * t + VW * pr + 129],
                                      start=(t == 0), stop=(t == 7)),
                                  waits=[(ve_kf, kf_need), (ac_vh, vh_need),
                                         (ve_acc, ve_acc.n)],
                                  incs=([(pe_kv, 1)] if t == 7 else []))
                        p.add("vector",
                              lambda h, pr=pr: h.tensor_tensor(
                                  acc_kv[:, VW * pr:VW * pr + 129],
                                  acc_kv[:, VW * pr:VW * pr + 129],
                                  pb[6][:, 0:129], ALU.add),
                              waits=[(pe_kv, pe_kv.n), (ve_acc, ve_acc.n)],
                              incs=[(ve_acc, 1)])

            p.add("gpsimd",
                  lambda h: h.dma_start(out=cc_in.ap(), in_=acc_kv[:]),
                  waits=[(ve_acc, ve_acc.n)], incs=[(cc, 16)])
            p.add("gpsimd",
                  lambda h: h.collective_compute(
                      "AllReduce", ALU.add,
                      replica_groups=[[0, 1], [2, 3], [4, 5], [6, 7]],
                      ins=[cc_in.ap().opt()], outs=[cc_out.ap().opt()]),
                  waits=[(cc, 16)], incs=[(cc2, 1)])
            p.add("gpsimd",
                  lambda h: h.dma_start(out=kvr[:], in_=cc_out.ap()),
                  waits=[(cc2, 1)], incs=[(cc3, 16)])
            for pr in range(NPAIR):
                o = VW * pr
                p.add("vector", lambda h, o=o: h.tensor_copy(
                    bd[0:64, o:o + 64], kvr[0:64, o:o + 64]),
                    waits=[(cc3, 16), (ve_bd, 1)])
                p.add("vector", lambda h, o=o: h.tensor_copy(
                    bd[0:64, o + 64:o + 65], kvr[0:64, o + 128:o + 129]))
                p.add("vector", lambda h, o=o: h.tensor_copy(
                    bd[64:128, o + 65:o + 129], kvr[64:128, o + 64:o + 128]))
                p.add("vector", lambda h, o=o: h.tensor_copy(
                    bd[64:128, o + 129:o + 130], kvr[64:128, o + 128:o + 129]),
                    incs=[(ve_bd, 1)])

            p.add("tensor", lambda h: h.nop())
            p.add("tensor", lambda h: h.nop(), incs=[(pe_endA, 1)])

        with contextlib.ExitStack() as stB:
            eB = stB.enter_context
            pt = eB(nc.psum_tensor("pt", [128, 1024], BF16))
            wq_sb = eB(nc.sbuf_tensor("wq_sb", [128, KC * D], BF16))
            wo_sb = eB(nc.sbuf_tensor("wo_sb", [128, KC * D], BF16))
            qf_sb = eB(nc.sbuf_tensor("qf_sb", [128, KC * 512], BF16))
            qtmp = [eB(nc.sbuf_tensor(f"qtmp{i}", [128, 512], BF16))
                    for i in range(2)]
            eqtmp = [eB(nc.sbuf_tensor(f"eqtmp{i}", [128, 512], BF16))
                     for i in range(2)]
            attn = [eB(nc.sbuf_tensor(f"attn{i}", [128, D], BF16))
                    for i in range(2)]
            attnT = [eB(nc.sbuf_tensor(f"attnT{i}", [128, D], BF16))
                     for i in range(2)]
            xf = [eB(nc.sbuf_tensor(f"xf{i}", [128, D], F32))
                  for i in range(2)]
            r1 = [eB(nc.sbuf_tensor(f"r1_{i}", [128, D], F32))
                  for i in range(2)]
            x1 = [eB(nc.sbuf_tensor(f"x1_{i}", [128, D], BF16))
                  for i in range(2)]

            BW = [(pe_endA, 1), (ve_kf, ve_kf.n), (ac_vh, ac_vh.n),
                  (ac_e, ac_e.n), (ve_km, ve_km.n), (ve_acc, ve_acc.n)]
            p.add("scalar", lambda h: h.dma_start(out=wq_sb[:],
                                                   in_=wq_bf.ap()),
                  waits=BW + [(ld_wqo, 32)], incs=[(ld_wq, 16)])
            p.add("scalar", lambda h: h.dma_start(out=wo_sb[:],
                                                  in_=wo_bf.ap()),
                  waits=BW + [(ld_wqo, 32)], incs=[(ld_wq, 16)])

            div_done = {}
            tc_done = {}
            wo_done = {}
            r1_done = {}
            x1_done = {}
            xf_need = {}
            num_banks = [(pb[2], 3, 0), (pb[3], 3, 3), (pb[4], 2, 6)]
            q_done = {}

            ln_state = {"i": 0, "applies": []}

            def add_ln(src_buf, dst_buf, src_sem, war_waits, store_fn):
                i = ln_state["i"]
                base = 46 + 6 * (i % 2)
                mv, std, inv, ng = base, base + 2, base + 3, base + 4
                guard = (ln_state["applies"][i - 2] if i >= 2 else 0)
                p.add("vector", lambda h: h.bn_stats(
                    smalls[:, 0:6], src_buf[:, 0:512]),
                    waits=[src_sem, (ve_bn, ve_bn.n)],
                    incs=[(ve_bs, 1)])
                p.add("vector", lambda h: h.bn_stats(
                    smalls[:, 6:12], src_buf[:, 512:1024]),
                    waits=[src_sem], incs=[(ve_bs, 1)])
                p.add("vector", lambda h: h.bn_aggr(
                    smalls[:, mv:mv + 2], smalls[:, 0:12]),
                    waits=[(ve_bs, ve_bs.n), (ac_x1, guard)],
                    incs=[(ve_bn, 1)])
                p.add("scalar", lambda h: h.activation(
                    smalls[:, std:std + 1], smalls[:, mv + 1:mv + 2], AF.Sqrt,
                    bias=smalls[:, 62:63]),
                    waits=[(ve_bn, ve_bn.n)], incs=[(ac_std, 1)])
                p.add("vector", lambda h: h.reciprocal(
                    smalls[:, inv:inv + 1], smalls[:, std:std + 1]),
                    waits=[(ac_std, ac_std.n)], incs=[(ve_rc, 1)])
                p.add("vector", lambda h: h.scalar_tensor_tensor(
                    smalls[:, ng:ng + 1], smalls[:, mv:mv + 1], -1.0,
                    smalls[:, inv:inv + 1], ALU.mult, ALU.mult),
                    waits=[(ve_rc, ve_rc.n), (ve_bn, ve_bn.n)],
                    incs=[(ve_ln, 1)])
                p.add("scalar",
                      lambda h: h.activation(
                          dst_buf[:], src_buf[:], AF.Identity,
                          scale=smalls[:, inv:inv + 1],
                          bias=smalls[:, ng:ng + 1]),
                      waits=[(ve_ln, ve_ln.n)] + war_waits,
                      incs=[(ac_x1, 1)])
                ln_state["applies"].append(ac_x1.n)
                ln_state["i"] = i + 1
                store_fn(ac_x1.n)

            for s in range(NSUP):
                for k in range(KC):
                    p.add("sync",
                          lambda h, s=s, k=k: h.dma_start(
                              out=xt[s % 2][:, 512 * k:512 * (k + 1)],
                              in_=x_bf.ap()[512 * s:512 * (s + 1),
                                            128 * k:128 * (k + 1)],
                              transpose=True),
                          waits=BW + [(ld_xc[s], 16),
                                      (pe_q, q_done.get(s - 2, 0))],
                          incs=[(xt_sp[s % 2], 16)])
                xt_need = xt_sp[s % 2].n
                for m in range(KC):
                    qb = pb[m % 2]
                    for k in range(KC):
                        p.add("tensor",
                              lambda h, m=m, k=k, s=s, qb=qb: h.matmul(
                                  qb[:],
                                  wq_sb[:, D * k + 128 * m:D * k + 128 * m + 128],
                                  xt[s % 2][:, 512 * k:512 * (k + 1)],
                                  start=(k == 0), stop=(k == KC - 1)),
                              waits=[(xt_sp[s % 2], xt_need), (ld_wq, 32),
                                     (ve_qf, max(0, ve_qf.n - 1))],
                              incs=([(pe_q, 1)] if k == KC - 1 else []))
                    if m == KC - 1:
                        q_done[s] = pe_q.n
                    p.add("vector",
                          lambda h, m=m, qb=qb: h.tensor_scalar_min(
                              qtmp[m % 2][:], qb[:], 0.0),
                          waits=[(pe_q, pe_q.n),
                                 (ac_qe, max(0, ac_qe.n - 1))],
                          incs=[(ve_qm, 1)])
                    p.add("scalar",
                          lambda h, m=m: h.activation(
                              eqtmp[m % 2][:], qtmp[m % 2][:], AF.Exp),
                          waits=[(ve_qm, ve_qm.n),
                                 (ve_qf, max(0, ve_qf.n - 1))],
                          incs=[(ac_qe, 1)])
                    p.add("vector",
                          lambda h, m=m, qb=qb: h.scalar_tensor_tensor(
                              qf_sb[:, 512 * m:512 * (m + 1)],
                              qb[:], 0.0, eqtmp[m % 2][:], ALU.max, ALU.add),
                          waits=[(ac_qe, ac_qe.n)],
                          incs=[(ve_qf, 1)])
                qf_need = ve_qf.n

                num_done = {}

                def stage_xf_num(jq, s=s):
                    prev = (s, jq - 2) if jq >= 2 else (s - 1, jq + 2)
                    p.add("gpsimd",
                          lambda h: h.dma_start(
                              out=xf[jq % 2][:],
                              in_=x_in[512 * s + 128 * jq:
                                       512 * s + 128 * jq + 128, :]),
                          waits=BW + [(ve_r1, r1_done.get(prev, 0))],
                          incs=[(ld_xfp[jq % 2], 16)])
                    xf_need[(s, jq)] = ld_xfp[jq % 2].n
                    for bank, cnt, p0 in num_banks:
                        for q in range(cnt):
                            pr = p0 + q
                            p.add("tensor",
                                  lambda h, bank=bank, q=q, pr=pr: h.matmul(
                                      bank[:, VW * q:VW * q + VW],
                                      qf_sb[:, 512 * pr + 128 * jq:
                                            512 * pr + 128 * jq + 128],
                                      bd[:, VW * pr:VW * (pr + 1)],
                                      start=True, stop=True),
                                  waits=[(ve_qf, qf_need), (ve_bd, 9),
                                         (ve_div, div_done.get((s, jq - 1), 0))],
                                  incs=([(pe_num, 1)] if q == cnt - 1 else []))
                    num_done[(s, jq)] = pe_num.n

                def stage_div(jq, s=s):
                    for bank, cnt, p0 in num_banks:
                        p.add("vector",
                              lambda h, bank=bank, cnt=cnt, p0=p0:
                              h.tensor_scalar(
                                  smalls[:, 12 + 2 * p0:12 + 2 * (p0 + cnt)],
                                  ap3(bank[:], 64, [[VW, cnt], [65, 2]]),
                                  EPS_ATTN, None, ALU.add),
                              waits=[(pe_num, num_done[(s, jq)]),
                                     (ve_rc, ve_rc.n)],
                              incs=[(ve_dv, 1)])
                    dv_now = ve_dv.n
                    for bank, cnt, p0 in num_banks:
                        p.add("vector",
                              lambda h, cnt=cnt, p0=p0: h.reciprocal(
                                  smalls[:, 28 + 2 * p0:28 + 2 * (p0 + cnt)],
                                  smalls[:, 12 + 2 * p0:12 + 2 * (p0 + cnt)]),
                              waits=[(ve_dv, dv_now),
                                     (ve_div, ve_div.n)],
                              incs=[(ve_rc, 1)])
                    rc_now = ve_rc.n
                    for bank, cnt, p0 in num_banks:
                        p.add("vector",
                              lambda h, bank=bank, cnt=cnt, p0=p0:
                              h.scalar_tensor_tensor(
                                  ap3(attn[jq % 2][:], 128 * p0,
                                      [[128, cnt], [64, 2], [1, 64]]),
                                  ap3(bank[:], 0, [[VW, cnt], [65, 2], [1, 64]]),
                                  1.0,
                                  ap3(smalls[:], 28 + 2 * p0,
                                      [[2, cnt], [1, 2], [0, 64]]),
                                  ALU.mult, ALU.mult),
                              waits=[(ve_rc, rc_now),
                                     (ac_tc, tc_done.get((s, jq - 2), 0))],
                              incs=[(ve_div, 1)])
                    div_done[(s, jq)] = ve_div.n

                def stage_tail(jq, s=s):
                    for k in range(KC):
                        p.add("tensor",
                              lambda h, k=k: h.transpose(
                                  pt[:, 128 * k:128 * (k + 1)],
                                  attn[jq % 2][:, 128 * k:128 * (k + 1)],
                                  ident[:]),
                              waits=[(ve_div, div_done[(s, jq)]),
                                     (ac_tc, tc_done.get((s, jq - 1), 0)),
                                     (dma_misc, 16)],
                              incs=([(pe_t, 1)] if k == KC - 1 else []))
                    if jq + 1 < 4:
                        stage_xf_num(jq + 1)
                        stage_div(jq + 1)
                    p.add("scalar",
                          lambda h: h.activation(
                              attnT[jq % 2][:], pt[:], AF.Copy),
                          waits=[(pe_t, pe_t.n),
                                 (pe_wo, wo_done.get((s, jq - 2), 0))],
                          incs=[(ac_tc, 1)])
                    tc_done[(s, jq)] = ac_tc.n
                    if jq >= 1:
                        stage_ln1(jq - 1)
                    for k in range(KC):
                        for n in range(2):
                            p.add("tensor",
                                  lambda h, k=k, n=n: h.matmul(
                                      pb[5 + n][:],
                                      attnT[jq % 2][:, 128 * k:128 * (k + 1)],
                                      wo_sb[:, D * k + 512 * n:
                                            D * k + 512 * (n + 1)],
                                      start=(k == 0), stop=(k == KC - 1)),
                                  waits=[(ac_tc, ac_tc.n), (ld_wq, 32),
                                         (ve_r1, r1_done.get((s, jq - 1), 0))],
                                  incs=([(pe_wo, 1)] if k == KC - 1 else []))
                    wo_done[(s, jq)] = pe_wo.n
                    for n in range(2):
                        p.add("vector",
                              lambda h, n=n: h.scalar_tensor_tensor(
                                  r1[jq % 2][:, 512 * n:512 * (n + 1)],
                                  pb[5 + n][:], 1.0,
                                  xf[jq % 2][:, 512 * n:512 * (n + 1)],
                                  ALU.mult, ALU.add),
                              waits=[(pe_wo, pe_wo.n),
                                     (ld_xfp[jq % 2], xf_need[(s, jq)]),
                                     (ac_x1, x1_done.get((s, jq - 2), 0))],
                              incs=[(ve_r1, 1)])
                    r1_done[(s, jq)] = ve_r1.n

                def stage_ln1(jq, s=s):
                    def store_x1(apply_n):
                        p.add("scalar",
                              lambda h: h.dma_start(
                                  out=x1_bf.ap()[512 * s + 128 * jq:
                                                 512 * s + 128 * jq + 128, :],
                                  in_=x1[jq % 2][:]),
                              waits=[(ac_x1, apply_n)],
                              incs=[(st_x1p[jq % 2], 16)])
                    add_ln(r1[jq % 2], x1[jq % 2],
                           (ve_r1, r1_done[(s, jq)]),
                           [(st_x1p[jq % 2], st_x1p[jq % 2].n)], store_x1)
                    x1_done[(s, jq)] = ac_x1.n

                stage_xf_num(0)
                stage_div(0)
                for jq in range(4):
                    stage_tail(jq)
                stage_ln1(3)
            p.add("tensor", lambda h: h.transpose(
                pt[:, 0:128], attn[1][:, 0:128], ident[:]),
                waits=[(ac_tc, ac_tc.n)], incs=[(bar_b, 1)])
            p.add("vector", lambda h: h.nop(),
                  waits=[(ac_x1, ac_x1.n)], incs=[(bar_b, 1)])
            p.add("scalar", lambda h: h.nop(),
                  incs=[(bar_b, 1)])

        with contextlib.ExitStack() as stC:
            eC = stC.enter_context
            pb7 = eC(nc.psum_tensor("pb7", [128, 512], F32))
            h2t = eC(nc.sbuf_tensor("h2t", [128, MC * 512], BF16))
            sg = [eC(nc.sbuf_tensor(f"sg{i}", [128, 512], BF16))
                  for i in range(2)]
            ub = [eC(nc.sbuf_tensor(f"ub{i}", [128, 512], BF16))
                  for i in range(2)]
            x1j = [eC(nc.sbuf_tensor(f"x1j{i}", [128, D], BF16))
                   for i in range(4)]
            dbanks = pb + [pb7]
            r2 = [eC(nc.sbuf_tensor(f"r2_{i}", [128, D], F32))
                  for i in range(2)]
            x2 = [eC(nc.sbuf_tensor(f"x2_{i}", [128, D], F32))
                  for i in range(2)]

            CW = [(bar_b, 3)]
            u_done = {}
            h2_done = {}
            h2_seq = []
            r2_done = {}
            x2_done = {}
            x1j_need = {}
            strip_done = []

            x1t_need_p = {}
            pe_d_bases = {}

            def do_r2_ln2(sp, jq):
                for n in range(2):
                    p.add("vector",
                          lambda h, sp=sp, jq=jq, n=n:
                          h.scalar_tensor_tensor(
                              r2[jq % 2][:, 512 * n:512 * (n + 1)],
                              dbanks[2 * jq + n][:], S127SQ,
                              x1j[jq][:, 512 * n:512 * (n + 1)],
                              ALU.mult, ALU.add),
                          waits=[(pe_d, pe_d_bases[sp] + jq + 1),
                                 (ld_x1jq[jq], x1j_need[(sp, jq)]),
                                 (ac_x1, x2_done.get((sp, jq - 2), 0))],
                          incs=[(ve_r2, 1)])
                r2_done[(sp, jq)] = ve_r2.n

                def store_out(apply_n, sp=sp, jq=jq):
                    p.add("scalar",
                          lambda h: h.dma_start(
                              out=out[512 * sp + 128 * jq:
                                      512 * sp + 128 * jq + 128, :],
                              in_=x2[jq % 2][:]),
                          waits=[(ac_x1, apply_n)],
                          incs=[(st_outp[jq % 2], 16)])
                add_ln(r2[jq % 2], x2[jq % 2], (ve_r2, ve_r2.n),
                       [(st_outp[jq % 2], st_outp[jq % 2].n)],
                       store_out)
                x2_done[(sp, jq)] = ac_x1.n

            def add_x1t(s):
                sn = [(st_x1p[0], 32 * (s + 1)), (st_x1p[1], 32 * (s + 1))]
                war = ([(pe_u, u_done[s - 2])] if s >= 2
                       else [(pe_q, q_done[6 + s])])
                for k in range(KC):
                    p.add("sync",
                          lambda h, s=s, k=k: h.dma_start(
                              out=xt[s % 2][:, 512 * k:512 * (k + 1)],
                              in_=x1_bf.ap()[512 * s:512 * (s + 1),
                                             128 * k:128 * (k + 1)],
                              transpose=True),
                          waits=sn + war,
                          incs=[(xt_sp[s % 2], 16)])
                x1t_need_p[s] = xt_sp[s % 2].n

            add_x1t(0)
            for s in range(NSUP):
                st_need = [(st_x1p[0], 32 * (s + 1)), (st_x1p[1], 32 * (s + 1))]
                x1t_need = x1t_need_p[s]
                for m in range(MC):
                    if s >= 1 and m < 4:
                        do_r2_ln2(s - 1, m)
                    gb, ubk = pb[m % 2 * 2], pb[m % 2 * 2 + 1]
                    gw = ([(ve_r2, r2_done.get((s - 1, m), 0))]
                          if m < 2 else [])
                    for k in range(KC):
                        p.add("tensor",
                              lambda h, m=m, k=k, s=s, gb=gb: h.matmul(
                                  gb[:],
                                  wgt[:, HID * k + 128 * m:HID * k + 128 * m + 128],
                                  xt[s % 2][:, 512 * k:512 * (k + 1)],
                                  start=(k == 0), stop=(k == KC - 1)),
                              waits=CW + [(xt_sp[s % 2], x1t_need),
                                          (ld_wgt, 128),
                                          (ac_sg, max(0, ac_sg.n - 1))] +
                              (gw if k == 0 else []),
                              incs=([(pe_g, 1)] if k == KC - 1 else []))
                    for k in range(KC):
                        p.add("tensor",
                              lambda h, m=m, k=k, s=s, ubk=ubk: h.matmul(
                                  ubk[:],
                                  wut[:, HID * k + 128 * m:HID * k + 128 * m + 128],
                                  xt[s % 2][:, 512 * k:512 * (k + 1)],
                                  start=(k == 0), stop=(k == KC - 1)),
                              waits=CW + [(ld_wut, 128),
                                          (ac_ub, max(0, ac_ub.n - 1))] +
                              (gw if k == 0 else []),
                              incs=([(pe_u, 1)] if k == KC - 1 else []))
                    h2_war = h2_seq[-2] if len(h2_seq) >= 2 else 0
                    p.add("scalar",
                          lambda h, m=m, gb=gb: h.activation(
                              sg[m % 2][:], gb[:], AF.Silu, scale=S127),
                          waits=[(pe_g, pe_g.n), (ve_h2, h2_war)],
                          incs=[(ac_sg, 1)])
                    p.add("scalar",
                          lambda h, m=m, ubk=ubk: h.activation(
                              ub[m % 2][:], ubk[:], AF.Copy),
                          waits=[(pe_u, pe_u.n), (ve_h2, h2_war)],
                          incs=[(ac_ub, 1)])
                    p.add("vector",
                          lambda h, m=m: h.tensor_tensor(
                              h2t[:, 512 * m:512 * (m + 1)],
                              sg[m % 2][:], ub[m % 2][:], ALU.mult),
                          waits=[(ac_sg, ac_sg.n), (ac_ub, ac_ub.n),
                                 (pe_d, 4 * s)],
                          incs=[(ve_h2, 1)])
                    h2_done[(s, m)] = ve_h2.n
                    h2_seq.append(ve_h2.n)
                u_done[s] = pe_u.n
                if s + 1 < NSUP:
                    add_x1t(s + 1)
                for jq in range(4):
                    p.add("scalar",
                          lambda h, s=s, jq=jq: h.dma_start(
                              out=x1j[jq][:],
                              in_=x1_bf.ap()[512 * s + 128 * jq:
                                             512 * s + 128 * jq + 128, :]),
                          waits=CW + st_need +
                          [(ve_r2, r2_done.get((s - 1, jq), 0))],
                          incs=[(ld_x1jq[jq], 16)])
                    x1j_need[(s, jq)] = ld_x1jq[jq].n
                pe_d_base = pe_d.n
                pe_d_bases[s] = pe_d_base
                for k in range(MC):
                    kp = k // 2
                    if k % 2 == 0:
                        g = len(strip_done)
                        strip_war = (strip_done[g - 2] if g >= 2
                                     else (pe_ds, 0))
                        p.add("sync",
                              lambda h, k=k: h.dma_start(
                                  out=wdt[(k // 2) % 2][:]
                                  .rearrange("p (b d) -> p b d", b=2),
                                  in_=wdt_bf.ap()[128 * k:128 * (k + 2), :]
                                  .rearrange("(b p) d -> p b d", p=128)),
                              waits=[(mk_s[0], 16 * (MC // 4)),
                                     (mk_s[1], 16 * (MC // 4)),
                                     (mk_s[2], 16 * (MC // 4)),
                                     (mk_s[3], 16 * (MC // 4)),
                                     strip_war],
                              incs=[(ld_wdtp[(k // 2) % 2], 16)])
                        wdt_need = ld_wdtp[kp % 2].n
                    for n in range(2):
                        for jq in range(4):
                            xw = []
                            if k == 0:
                                xw = [(ac_sg, ac_sg.n), (ac_ub, ac_ub.n),
                                      (ve_r2, r2_done.get((s - 1, 3), 0))]
                            incs = []
                            if k == MC - 1 and n == 1:
                                incs = [(pe_d, 1)]
                            elif n == 1 and jq == 3 and k % 2 == 1:
                                incs = [(pe_ds, 1)]
                            elif k == MC - 1 and n == 0 and jq == 3:
                                incs = [(pe_ds, 1)]
                            p.add("tensor",
                                  lambda h, k=k, jq=jq, n=n: h.matmul(
                                      dbanks[2 * jq + n][:],
                                      h2t[:, 512 * k + 128 * jq:
                                           512 * k + 128 * jq + 128],
                                      wdt[(k // 2) % 2][:, 1024 * (k % 2) +
                                                        512 * n:
                                                        1024 * (k % 2) +
                                                        512 * (n + 1)],
                                      start=(k == 0), stop=(k == MC - 1)),
                                  waits=[(ve_h2, h2_done[(s, k)]),
                                         (ld_wdtp[kp % 2], wdt_need)] + xw,
                                  incs=incs)
                    if k % 2 == 1:
                        strip_done.append((pe_d, pe_d_base + 4)
                                          if k == MC - 1
                                          else (pe_ds, pe_ds.n))

                if s == NSUP - 1:
                    for jq in range(4):
                        do_r2_ln2(s, jq)
            p.add("scalar", lambda h: h.nop(),
                  waits=[(st_outp[0], 16 * NJ // 2),
                         (st_outp[1], 16 * NJ // 2)])
            if PERTURB == 1:
                p.add("scalar", lambda h: h.nop())
            elif PERTURB == 2:
                p.add("scalar", lambda h: h.nop())
                p.add("vector", lambda h: h.nop())
            elif PERTURB == 5:
                p.add("tensor", lambda h: h.nop())
            elif PERTURB == 6:
                for _ in range(3):
                    p.add("scalar", lambda h: h.nop())
                    p.add("vector", lambda h: h.nop())
                    p.add("tensor", lambda h: h.nop())

        with nc.Block() as block:
            p.emit(block)

    nc._prog_n_ops = p.n_ops
    return nc


_NC_CACHE = {}


def kernel(L0_attn_x, L0_attn_Wq, L0_attn_Wk, L0_attn_Wv, L0_attn_Wo,
           L0_ffn_w_gate, L0_ffn_w_up, L0_ffn_w_down):
    x = np.ascontiguousarray(np.asarray(L0_attn_x, dtype=np.float32))
    wq = np.ascontiguousarray(np.asarray(L0_attn_Wq, dtype=np.float32))
    wk = np.ascontiguousarray(np.asarray(L0_attn_Wk, dtype=np.float32))
    wv = np.ascontiguousarray(np.asarray(L0_attn_Wv, dtype=np.float32))
    wo = np.ascontiguousarray(np.asarray(L0_attn_Wo, dtype=np.float32))
    wg = np.ascontiguousarray(np.asarray(L0_ffn_w_gate, dtype=np.int32))
    wu = np.ascontiguousarray(np.asarray(L0_ffn_w_up, dtype=np.int32))
    wd = np.ascontiguousarray(np.asarray(L0_ffn_w_down, dtype=np.int32))
    ident = np.eye(128, dtype=ml_dtypes.bfloat16)

    if "nc" not in _NC_CACHE:
        _NC_CACHE["nc"] = build_program()
    nc = _NC_CACHE["nc"]

    in_maps = []
    for c in range(8):
        b, half = c // 2, c % 2
        xo = np.ascontiguousarray(x[b, half * T:(half + 1) * T, :])
        in_maps.append(dict(x=xo, wq=wq, wk=wk, wv=wv, wo=wo,
                            wg=wg, wu=wu, wd=wd, ident=ident))

    res = run_bass_kernel_spmd(nc, in_maps, core_ids=list(range(8)))

    outp = np.empty((4, 8192, 1024), dtype=np.float32)
    for c in range(8):
        b, half = c // 2, c % 2
        outp[b, half * T:(half + 1) * T, :] = np.asarray(res.results[c]["out"])
    return outp


if __name__ == "__main__":
    nc = build_program()
    print("built OK; instructions:", nc.n_instructions())


# version 20
# speedup vs baseline: 1.1479x; 1.0173x over previous
import contextlib
import numpy as np
import ml_dtypes
import concourse.bass as bass
import concourse.mybir as mybir
from concourse.bass_utils import run_bass_kernel_spmd

F32 = mybir.dt.float32
BF16 = mybir.dt.bfloat16
I32 = mybir.dt.int32
AF = mybir.ActivationFunctionType
ALU = mybir.AluOpType

T = 4096
D = 1024
HID = 3072
NSUP = 8
NJ = 32
KC = D // 128
MC = HID // 128
NPAIR = 8
VW = 130
EPS_ATTN = 1e-6
EPS_LN = 1e-5
S127 = 1.0 / 127.0
S127SQ = float(np.float32(1.0) / np.float32(127.0)) ** 2

ENGINES = ("sync", "tensor", "vector", "scalar", "gpsimd")


class Sem:
    __slots__ = ("h", "n", "name")

    def __init__(self, h, name):
        self.h = h
        self.n = 0
        self.name = name


class Prog:

    def __init__(self, nc, stack, max_ops=None):
        self.nc = nc
        self.stack = stack
        self.ops = {e: [] for e in ENGINES}
        self._sems = []
        self.n_ops = 0
        self.max_ops = max_ops

    def sem(self, name):
        s = Sem(self.stack.enter_context(self.nc.semaphore(name)), name)
        self._sems.append(s)
        return s

    def add(self, eng, fn, waits=(), incs=()):
        self.n_ops += 1
        if self.max_ops is not None and self.n_ops > self.max_ops:
            for s, k in incs:
                s.n += k
            return
        ws = [(s, int(v)) for (s, v) in waits if v > 0]
        self.ops[eng].append((fn, ws, list(incs)))
        for s, k in incs:
            s.n += k

    def emit(self, block):
        for eng in ENGINES:
            ops = self.ops[eng]
            if not ops:
                continue

            def body(handle, ops=ops):
                waited = {}
                for fn, ws, incs in ops:
                    for s, v in ws:
                        if waited.get(s, 0) < v:
                            handle.wait_ge(s.h, v)
                            waited[s] = v
                    instr = fn(handle)
                    for s, k in incs:
                        instr.then_inc(s.h, k)

            getattr(block, eng)(body)


PERTURB = 3


def build_program(max_ops=None):
    nc = bass.Bass("TRN2", target_bir_lowering=False)

    x_in = nc.dram_tensor("x", [T, D], F32, kind="ExternalInput").ap()
    wq_in = nc.dram_tensor("wq", [D, D], F32, kind="ExternalInput").ap()
    wk_in = nc.dram_tensor("wk", [D, D], F32, kind="ExternalInput").ap()
    wv_in = nc.dram_tensor("wv", [D, D], F32, kind="ExternalInput").ap()
    wo_in = nc.dram_tensor("wo", [D, D], F32, kind="ExternalInput").ap()
    wg_in = nc.dram_tensor("wg", [HID, D], I32, kind="ExternalInput").ap()
    wu_in = nc.dram_tensor("wu", [HID, D], I32, kind="ExternalInput").ap()
    wd_in = nc.dram_tensor("wd", [D, HID], I32, kind="ExternalInput").ap()
    id_in = nc.dram_tensor("ident", [128, 128], BF16, kind="ExternalInput").ap()
    out = nc.dram_tensor("out", [T, D], F32, kind="ExternalOutput").ap()

    x_bf = nc.dram_tensor("x_bf", [T, D], BF16)
    wq_bf = nc.dram_tensor("wq_bf", [128, KC * D], BF16)
    wo_bf = nc.dram_tensor("wo_bf", [128, KC * D], BF16)
    wg_bf = nc.dram_tensor("wg_bf", [HID, D], BF16)
    wu_bf = nc.dram_tensor("wu_bf", [HID, D], BF16)
    wd_bf = nc.dram_tensor("wd_bf", [D, HID], BF16)
    wdt_bf = nc.dram_tensor("wdt_bf", [HID, D], BF16)
    x1_bf = nc.dram_tensor("x1_bf", [T, D], BF16)
    cc_in = nc.dram_tensor("cc_in", [128, NPAIR * VW], F32)
    cc_out = nc.dram_tensor("cc_out", [128, NPAIR * VW], F32)

    def ap3(t_ap, offset, dims):
        return bass.AP(t_ap.tensor, t_ap.offset + offset,
                       [list(t_ap.ap[0])] + [list(d) for d in dims])

    with contextlib.ExitStack() as st0:
        e0 = st0.enter_context
        p = Prog(nc, st0, max_ops=max_ops)

        ident = e0(nc.sbuf_tensor("ident_sb", [128, 128], BF16))
        wgt = e0(nc.sbuf_tensor("wgt", [128, KC * HID], BF16))
        wut = e0(nc.sbuf_tensor("wut", [128, KC * HID], BF16))
        xt = [e0(nc.sbuf_tensor(f"xt{i}", [128, KC * 512], BF16,
                                is_transpose_dest=True)) for i in range(2)]
        bd = e0(nc.sbuf_tensor("bd", [128, NPAIR * VW], BF16))
        wdt = [e0(nc.sbuf_tensor(f"wdt{i}", [128, 2 * D], BF16,
                                 is_transpose_dest=True))
               for i in range(2)]
        smalls = e0(nc.sbuf_tensor("smalls", [128, 64], F32))
        acc_kv = e0(nc.sbuf_tensor("acc_kv", [128, NPAIR * VW], F32))
        kvr = e0(nc.sbuf_tensor("kvr", [128, NPAIR * VW], F32))
        pb = [e0(nc.psum_tensor(f"pb{i}", [128, 512], F32)) for i in range(7)]

        if PERTURB == 3:
            p.sem("pad0")
        ld_xc = [p.sem(f"ld_xc{i}") for i in range(NSUP)]
        ld_wkv = p.sem("ld_wkv")
        ld_wqo = p.sem("ld_wqo")
        ld_wf = p.sem("ld_wf")
        mk_t = [p.sem(f"mk_t{i}") for i in range(4)]
        mk_s = [p.sem(f"mk_s{i}") for i in range(4)]
        ld_wq = p.sem("ld_wq")
        ld_wgt = p.sem("ld_wgt")
        ld_wut = p.sem("ld_wut")
        xt_sp = [p.sem("xt_s0"), p.sem("xt_s1")]
        ld_xfp = [p.sem("ld_xf0"), p.sem("ld_xf1")]
        ld_wdtp = [p.sem("ld_wdt0"), p.sem("ld_wdt1")]
        ld_x1jq = [p.sem(f"ld_x1j{i}") for i in range(4)]
        st_x1p = [p.sem("st_x10"), p.sem("st_x11")]
        st_outp = [p.sem("st_out0"), p.sem("st_out1")]
        cc = p.sem("cc")
        cc2 = p.sem("cc2")
        cc3 = p.sem("cc3")
        pe_k = p.sem("pe_k")
        pe_v = p.sem("pe_v")
        pe_kv = p.sem("pe_kv")
        pe_q = p.sem("pe_q")
        pe_num = p.sem("pe_num")
        pe_t = p.sem("pe_t")
        pe_wo = p.sem("pe_wo")
        pe_g = p.sem("pe_g")
        pe_u = p.sem("pe_u")
        pe_d = p.sem("pe_d")
        ve_km = p.sem("ve_km")
        ve_kf = p.sem("ve_kf")
        ve_acc = p.sem("ve_acc")
        ve_bd = p.sem("ve_bd")
        ve_qm = p.sem("ve_qm")
        ve_qf = p.sem("ve_qf")
        ve_div = p.sem("ve_div")
        ve_r1 = p.sem("ve_r1")
        ve_ln = p.sem("ve_ln")
        ve_h2 = p.sem("ve_h2")
        ve_r2 = p.sem("ve_r2")
        ve_ln2 = p.sem("ve_ln2")
        ve_bn = p.sem("ve_bn")
        ve_bs = p.sem("ve_bs")
        ve_dv = p.sem("ve_dv")
        ve_rc = p.sem("ve_rc")
        pe_ds = p.sem("pe_ds")
        ac_e = p.sem("ac_e")
        ac_vh = p.sem("ac_vh")
        ac_qe = p.sem("ac_qe")
        ac_tc = p.sem("ac_tc")
        ac_x1 = p.sem("ac_x1")
        ac_sg = p.sem("ac_sg")
        ac_ub = p.sem("ac_ub")
        ac_std = p.sem("ac_std")
        bar_a = p.sem("bar_a")
        pe_endA = p.sem("pe_endA")
        bar_b = p.sem("bar_b")
        dma_misc = p.sem("dma_misc")

        p.add("sync", lambda h: h.dma_start(out=ident[:], in_=id_in),
              incs=[(dma_misc, 16)])
        stA = contextlib.ExitStack()
        eA = stA.enter_context
        wk_sb = eA(nc.sbuf_tensor("wk_sb", [128, KC * D], BF16))
        wv_sb = eA(nc.sbuf_tensor("wv_sb", [128, KC * D], BF16))

        def w_to_sb(h, dst, src):
            return h.dma_start(
                out=dst[:].rearrange("p (k d) -> p k d", k=KC),
                in_=src.rearrange("(k p) d -> p k d", p=128))

        def x_cast(s):
            p.add("gpsimd",
                  lambda h, s=s: h.dma_start(
                      out=x_bf.ap()[512 * s:512 * (s + 1), :],
                      in_=x_in[512 * s:512 * (s + 1), :]),
                  incs=[(ld_xc[s], 16)])

        p.add("gpsimd", lambda h: w_to_sb(h, wk_sb, wk_in), incs=[(ld_wkv, 16)])
        p.add("gpsimd", lambda h: w_to_sb(h, wv_sb, wv_in), incs=[(ld_wkv, 16)])
        for s in range(NSUP):
            x_cast(s)
        p.add("gpsimd", lambda h: h.dma_start(
            out=wq_bf.ap().rearrange("p (k d) -> p k d", k=KC),
            in_=wq_in.rearrange("(k p) d -> p k d", p=128)),
            incs=[(ld_wqo, 16)])
        p.add("gpsimd", lambda h: h.dma_start(
            out=wo_bf.ap().rearrange("p (k d) -> p k d", k=KC),
            in_=wo_in.rearrange("(k p) d -> p k d", p=128)),
            incs=[(ld_wqo, 16)])
        p.add("gpsimd", lambda h: h.dma_start(out=wg_bf.ap(), in_=wg_in),
              incs=[(ld_wf, 16)])
        p.add("gpsimd", lambda h: h.dma_start(out=wu_bf.ap(), in_=wu_in),
              incs=[(ld_wf, 16)])
        p.add("gpsimd", lambda h: h.dma_start(out=wd_bf.ap(), in_=wd_in),
              incs=[(ld_wf, 16)])
        xt_need_p = {}
        v_done = {}

        def add_xtA(s):
            for k in range(KC):
                p.add("sync",
                      lambda h, s=s, k=k: h.dma_start(
                          out=xt[s % 2][:, 512 * k:512 * (k + 1)],
                          in_=x_bf.ap()[512 * s:512 * (s + 1),
                                        128 * k:128 * (k + 1)],
                          transpose=True),
                      waits=[(ld_xc[s], 16),
                             (pe_v, v_done.get(s - 2, 0))],
                      incs=[(xt_sp[s % 2], 16)])
            xt_need_p[s] = xt_sp[s % 2].n

        add_xtA(0)
        add_xtA(1)

        deferred_tr = []
        for k in range(KC):
            deferred_tr.append(lambda k=k: p.add(
                "sync",
                lambda h: h.dma_start(
                    out=wgt[:, HID * k:HID * (k + 1)],
                    in_=wg_bf.ap()[:, 128 * k:128 * (k + 1)], transpose=True),
                waits=[(ld_wf, 48)], incs=[(ld_wgt, 16)]))
        for k in range(KC):
            deferred_tr.append(lambda k=k: p.add(
                "sync",
                lambda h: h.dma_start(
                    out=wut[:, HID * k:HID * (k + 1)],
                    in_=wu_bf.ap()[:, 128 * k:128 * (k + 1)], transpose=True),
                waits=[(ld_wf, 48)], incs=[(ld_wut, 16)]))

        def mat_strip(k):
            q = k % 4
            sl_lo = D * (q % 2)
            p.add("sync",
                  lambda h, k=k, q=q, sl_lo=sl_lo: h.dma_start(
                      out=wdt[q // 2][:, sl_lo:sl_lo + D],
                      in_=wd_bf.ap()[:, 128 * k:128 * (k + 1)], transpose=True),
                  waits=[(ld_wf, 48), (mk_s[q], mk_s[q].n)],
                  incs=[(mk_t[q], 16)])
            p.add("scalar",
                  lambda h, k=k, q=q, sl_lo=sl_lo: h.dma_start(
                      out=wdt_bf.ap()[128 * k:128 * (k + 1), :],
                      in_=wdt[q // 2][:, sl_lo:sl_lo + D]),
                  waits=[(mk_t[q], mk_t[q].n)], incs=[(mk_s[q], 16)])
        for k in range(MC):
            deferred_tr.append(lambda k=k: mat_strip(k))

        with stA:
            kf_sb = eA(nc.sbuf_tensor("kf_sb", [128, 8 * D], BF16))
            vh_sb = eA(nc.sbuf_tensor("vh_sb", [128, 8 * NPAIR * VW], BF16))
            ktmp = [eA(nc.sbuf_tensor(f"ktmp{i}", [128, 512], BF16))
                    for i in range(2)]
            etmp = [eA(nc.sbuf_tensor(f"etmp{i}", [128, 512], BF16))
                    for i in range(2)]

            if PERTURB == 4:
                p.add("vector", lambda h: h.memset(smalls[:, 34:35], 0.0))
            p.add("vector", lambda h: h.memset(smalls[:, 62:63], EPS_LN),
                  incs=[(ve_bn, 1)])
            p.add("vector", lambda h: h.memset(acc_kv[:], 0.0),
                  incs=[(ve_acc, 1)])
            p.add("vector", lambda h: h.memset(bd[:], 0.0), incs=[(ve_bd, 1)])
            p.add("vector",
                  lambda h: h.memset(
                      ap3(vh_sb[:], 128, [[VW * NPAIR, 8], [VW, NPAIR], [1, 1]]),
                      1.0),
                  incs=[(ve_acc, 1)])

            kf_done = {}
            vh_done = {}
            k_banks = [(pb[0], pb[1]), (pb[4], pb[5])]
            v_banks = (pb[2], pb[3])

            for j in range(NJ):
                s, jq = j // 4, j % 4
                jj = j % 8
                if jq == 0 and s >= 2:
                    add_xtA(s)
                if jq == 0 and s >= 3:
                    for _ in range(8):
                        if deferred_tr:
                            deferred_tr.pop(0)()
                kb = k_banks[j % 2]
                for k in range(KC):
                    for n in range(2):
                        p.add("tensor",
                              lambda h, j=j, k=k, n=n, kb=kb: h.matmul(
                                  kb[n][:],
                                  xt[(j // 4) % 2][:, 512 * k + 128 * (j % 4):
                                                   512 * k + 128 * (j % 4) + 128],
                                  wk_sb[:, D * k + 512 * n:D * k + 512 * (n + 1)],
                                  start=(k == 0), stop=(k == KC - 1)),
                              waits=[(xt_sp[s % 2], xt_need_p[s]),
                                     (ld_wkv, 32),
                                     (ve_kf, kf_done.get((j - 2, n), 0))],
                              incs=([(pe_k, 1)] if k == KC - 1 else []))
                for k in range(KC):
                    for n in range(2):
                        p.add("tensor",
                              lambda h, j=j, k=k, n=n: h.matmul(
                                  v_banks[n][:],
                                  xt[(j // 4) % 2][:, 512 * k + 128 * (j % 4):
                                                   512 * k + 128 * (j % 4) + 128],
                                  wv_sb[:, D * k + 512 * n:D * k + 512 * (n + 1)],
                                  start=(k == 0), stop=(k == KC - 1)),
                              waits=[(ld_wkv, 32),
                                     (ac_vh, vh_done.get((j - 1, n), 0))],
                              incs=([(pe_v, 1)] if k == KC - 1 else []))
                if jq == 3:
                    v_done[s] = pe_v.n
                for n in range(2):
                    p.add("vector",
                          lambda h, n=n, kb=kb: h.tensor_scalar_min(
                              ktmp[n][:], kb[n][:], 0.0),
                          waits=[(pe_k, pe_k.n - (1 - n)),
                                 (ac_e, ac_e.n - 2 + n if j > 0 or n > 0 else 0)],
                          incs=[(ve_km, 1)])
                    p.add("scalar",
                          lambda h, n=n: h.activation(
                              etmp[n][:], ktmp[n][:], AF.Exp),
                          waits=[(ve_km, ve_km.n),
                                 (ve_kf, ve_kf.n - 1 if (j, n) != (0, 0) else 0)],
                          incs=[(ac_e, 1)])
                    p.add("vector",
                          lambda h, jj=jj, n=n, kb=kb: h.scalar_tensor_tensor(
                              kf_sb[:, D * jj + 512 * n:D * jj + 512 * n + 512],
                              kb[n][:], 0.0, etmp[n][:], ALU.max, ALU.add),
                          waits=[(ac_e, ac_e.n)],
                          incs=[(ve_kf, 1)])
                    kf_done[(j, n)] = ve_kf.n
                    p.add("scalar",
                          lambda h, jj=jj, n=n: h.activation(
                              ap3(vh_sb[:], VW * NPAIR * jj + VW * 4 * n,
                                  [[VW, 4], [1, 128]]),
                              v_banks[n][:], AF.Copy),
                          waits=[(pe_v, pe_v.n - (1 - n)),
                                 (pe_kv, pe_kv.n)],
                          incs=[(ac_vh, 1)])
                    vh_done[(j, n)] = ac_vh.n

                if jj == 7:
                    kf_need = ve_kf.n
                    vh_need = ac_vh.n
                    for pr in range(NPAIR):
                        for t in range(8):
                            p.add("tensor",
                                  lambda h, pr=pr, t=t: h.matmul(
                                      pb[6][:, 0:129],
                                      kf_sb[:, D * t + 128 * pr:
                                            D * t + 128 * pr + 128],
                                      vh_sb[:, VW * NPAIR * t + VW * pr:
                                            VW * NPAIR * t + VW * pr + 129],
                                      start=(t == 0), stop=(t == 7)),
                                  waits=[(ve_kf, kf_need), (ac_vh, vh_need),
                                         (ve_acc, ve_acc.n)],
                                  incs=([(pe_kv, 1)] if t == 7 else []))
                        p.add("vector",
                              lambda h, pr=pr: h.tensor_tensor(
                                  acc_kv[:, VW * pr:VW * pr + 129],
                                  acc_kv[:, VW * pr:VW * pr + 129],
                                  pb[6][:, 0:129], ALU.add),
                              waits=[(pe_kv, pe_kv.n), (ve_acc, ve_acc.n)],
                              incs=[(ve_acc, 1)])

            p.add("gpsimd",
                  lambda h: h.dma_start(out=cc_in.ap(), in_=acc_kv[:]),
                  waits=[(ve_acc, ve_acc.n)], incs=[(cc, 16)])
            p.add("gpsimd",
                  lambda h: h.collective_compute(
                      "AllReduce", ALU.add,
                      replica_groups=[[0, 1], [2, 3], [4, 5], [6, 7]],
                      ins=[cc_in.ap().opt()], outs=[cc_out.ap().opt()]),
                  waits=[(cc, 16)], incs=[(cc2, 1)])
            p.add("gpsimd",
                  lambda h: h.dma_start(out=kvr[:], in_=cc_out.ap()),
                  waits=[(cc2, 1)], incs=[(cc3, 16)])
            for pr in range(NPAIR):
                o = VW * pr
                p.add("vector", lambda h, o=o: h.tensor_copy(
                    bd[0:64, o:o + 64], kvr[0:64, o:o + 64]),
                    waits=[(cc3, 16), (ve_bd, 1)])
                p.add("vector", lambda h, o=o: h.tensor_copy(
                    bd[0:64, o + 64:o + 65], kvr[0:64, o + 128:o + 129]))
                p.add("vector", lambda h, o=o: h.tensor_copy(
                    bd[64:128, o + 65:o + 129], kvr[64:128, o + 64:o + 128]))
                p.add("vector", lambda h, o=o: h.tensor_copy(
                    bd[64:128, o + 129:o + 130], kvr[64:128, o + 128:o + 129]),
                    incs=[(ve_bd, 1)])

            p.add("tensor", lambda h: h.nop())
            p.add("tensor", lambda h: h.nop(), incs=[(pe_endA, 1)])

        with contextlib.ExitStack() as stB:
            eB = stB.enter_context
            pt = eB(nc.psum_tensor("pt", [128, 1024], BF16))
            wq_sb = eB(nc.sbuf_tensor("wq_sb", [128, KC * D], BF16))
            wo_sb = eB(nc.sbuf_tensor("wo_sb", [128, KC * D], BF16))
            qf_sb = [eB(nc.sbuf_tensor(f"qf_sb{i}", [128, KC * 512],
                                       BF16)) for i in range(2)]
            qtmp = eB(nc.sbuf_tensor("qtmp0", [128, 512], BF16))
            eqtmp = eB(nc.sbuf_tensor("eqtmp0", [128, 512], BF16))
            attn = [eB(nc.sbuf_tensor(f"attn{i}", [128, D], BF16))
                    for i in range(2)]
            attnT = [eB(nc.sbuf_tensor(f"attnT{i}", [128, D], BF16))
                     for i in range(2)]
            xf = [eB(nc.sbuf_tensor(f"xf{i}", [128, D], F32))
                  for i in range(2)]
            r1 = [eB(nc.sbuf_tensor(f"r1_{i}", [128, D], F32))
                  for i in range(2)]
            x1 = eB(nc.sbuf_tensor("x1_0", [128, D], BF16))

            BW = [(pe_endA, 1), (ve_kf, ve_kf.n), (ac_vh, ac_vh.n),
                  (ac_e, ac_e.n), (ve_km, ve_km.n), (ve_acc, ve_acc.n)]
            p.add("scalar", lambda h: h.dma_start(out=wq_sb[:],
                                                   in_=wq_bf.ap()),
                  waits=BW + [(ld_wqo, 32)], incs=[(ld_wq, 16)])
            p.add("scalar", lambda h: h.dma_start(out=wo_sb[:],
                                                  in_=wo_bf.ap()),
                  waits=BW + [(ld_wqo, 32)], incs=[(ld_wq, 16)])

            div_done = {}
            tc_done = {}
            wo_done = {}
            r1_done = {}
            x1_done = {}
            xf_need = {}
            num_banks = [(pb[2], 3, 0), (pb[3], 3, 3), (pb[4], 2, 6)]
            q_done = {}
            qf_need_p = {}

            ln_state = {"i": 0, "applies": []}

            def add_ln(src_buf, dst_buf, src_sem, war_waits, store_fn):
                i = ln_state["i"]
                base = 46 + 6 * (i % 2)
                mv, std, inv, ng = base, base + 2, base + 3, base + 4
                guard = (ln_state["applies"][i - 2] if i >= 2 else 0)
                p.add("vector", lambda h: h.bn_stats(
                    smalls[:, 0:6], src_buf[:, 0:512]),
                    waits=[src_sem, (ve_bn, ve_bn.n)],
                    incs=[(ve_bs, 1)])
                p.add("vector", lambda h: h.bn_stats(
                    smalls[:, 6:12], src_buf[:, 512:1024]),
                    waits=[src_sem], incs=[(ve_bs, 1)])
                p.add("vector", lambda h: h.bn_aggr(
                    smalls[:, mv:mv + 2], smalls[:, 0:12]),
                    waits=[(ve_bs, ve_bs.n), (ac_x1, guard)],
                    incs=[(ve_bn, 1)])
                p.add("scalar", lambda h: h.activation(
                    smalls[:, std:std + 1], smalls[:, mv + 1:mv + 2], AF.Sqrt,
                    bias=smalls[:, 62:63]),
                    waits=[(ve_bn, ve_bn.n)], incs=[(ac_std, 1)])
                p.add("vector", lambda h: h.reciprocal(
                    smalls[:, inv:inv + 1], smalls[:, std:std + 1]),
                    waits=[(ac_std, ac_std.n)], incs=[(ve_rc, 1)])
                p.add("vector", lambda h: h.scalar_tensor_tensor(
                    smalls[:, ng:ng + 1], smalls[:, mv:mv + 1], -1.0,
                    smalls[:, inv:inv + 1], ALU.mult, ALU.mult),
                    waits=[(ve_rc, ve_rc.n), (ve_bn, ve_bn.n)],
                    incs=[(ve_ln, 1)])
                p.add("scalar",
                      lambda h: h.activation(
                          dst_buf[:], src_buf[:], AF.Identity,
                          scale=smalls[:, inv:inv + 1],
                          bias=smalls[:, ng:ng + 1]),
                      waits=[(ve_ln, ve_ln.n)] + war_waits,
                      incs=[(ac_x1, 1)])
                ln_state["applies"].append(ac_x1.n)
                ln_state["i"] = i + 1
                store_fn(ac_x1.n)

            for s in range(NSUP):
                def do_q(sq):
                    for k in range(KC):
                        p.add("sync",
                              lambda h, sq=sq, k=k: h.dma_start(
                                  out=xt[sq % 2][:, 512 * k:512 * (k + 1)],
                                  in_=x_bf.ap()[512 * sq:512 * (sq + 1),
                                                128 * k:128 * (k + 1)],
                                  transpose=True),
                              waits=BW + [(ld_xc[sq], 16),
                                          (pe_q, q_done.get(sq - 2, 0))],
                              incs=[(xt_sp[sq % 2], 16)])
                    xt_need = xt_sp[sq % 2].n
                    for m in range(KC):
                        qb = pb[m % 2]
                        for k in range(KC):
                            p.add("tensor",
                                  lambda h, m=m, k=k, sq=sq, qb=qb: h.matmul(
                                      qb[:],
                                      wq_sb[:, D * k + 128 * m:
                                            D * k + 128 * m + 128],
                                      xt[sq % 2][:, 512 * k:512 * (k + 1)],
                                      start=(k == 0), stop=(k == KC - 1)),
                                  waits=[(xt_sp[sq % 2], xt_need), (ld_wq, 32),
                                         (ve_qf, max(0, ve_qf.n - 1))],
                                  incs=([(pe_q, 1)] if k == KC - 1 else []))
                        if m == KC - 1:
                            q_done[sq] = pe_q.n
                        p.add("vector",
                              lambda h, qb=qb: h.tensor_scalar_min(
                                  qtmp[:], qb[:], 0.0),
                              waits=[(pe_q, pe_q.n), (ac_qe, ac_qe.n)],
                              incs=[(ve_qm, 1)])
                        p.add("scalar",
                              lambda h: h.activation(
                                  eqtmp[:], qtmp[:], AF.Exp),
                              waits=[(ve_qm, ve_qm.n),
                                     (ve_qf, ve_qf.n)],
                              incs=[(ac_qe, 1)])
                        p.add("vector",
                              lambda h, m=m, sq=sq, qb=qb:
                              h.scalar_tensor_tensor(
                                  qf_sb[sq % 2][:, 512 * m:512 * (m + 1)],
                                  qb[:], 0.0, eqtmp[:],
                                  ALU.max, ALU.add),
                              waits=[(ac_qe, ac_qe.n)],
                              incs=[(ve_qf, 1)])
                    qf_need_p[sq] = ve_qf.n

                if s == 0:
                    do_q(0)
                    do_q(1)
                elif s + 1 < NSUP:
                    do_q(s + 1)
                qf_need = qf_need_p[s]

                num_done = {}

                def stage_xf_num(jq, s=s):
                    prev = (s, jq - 2) if jq >= 2 else (s - 1, jq + 2)
                    p.add("gpsimd",
                          lambda h: h.dma_start(
                              out=xf[jq % 2][:],
                              in_=x_in[512 * s + 128 * jq:
                                       512 * s + 128 * jq + 128, :]),
                          waits=BW + [(ve_r1, r1_done.get(prev, 0))],
                          incs=[(ld_xfp[jq % 2], 16)])
                    xf_need[(s, jq)] = ld_xfp[jq % 2].n
                    for bank, cnt, p0 in num_banks:
                        for q in range(cnt):
                            pr = p0 + q
                            p.add("tensor",
                                  lambda h, bank=bank, q=q, pr=pr: h.matmul(
                                      bank[:, VW * q:VW * q + VW],
                                      qf_sb[s % 2][:, 512 * pr + 128 * jq:
                                                   512 * pr + 128 * jq + 128],
                                      bd[:, VW * pr:VW * (pr + 1)],
                                      start=True, stop=True),
                                  waits=[(ve_qf, qf_need), (ve_bd, 9),
                                         (ve_div, div_done.get(
                                             (s, jq - 1),
                                             div_done.get((s - 1, 3), 0)))],
                                  incs=([(pe_num, 1)] if q == cnt - 1 else []))
                    num_done[(s, jq)] = pe_num.n

                def stage_div(jq, s=s):
                    for bank, cnt, p0 in num_banks:
                        p.add("vector",
                              lambda h, bank=bank, cnt=cnt, p0=p0:
                              h.tensor_scalar(
                                  smalls[:, 12 + 2 * p0:12 + 2 * (p0 + cnt)],
                                  ap3(bank[:], 64, [[VW, cnt], [65, 2]]),
                                  EPS_ATTN, None, ALU.add),
                              waits=[(pe_num, num_done[(s, jq)]),
                                     (ve_rc, ve_rc.n)],
                              incs=[(ve_dv, 1)])
                    dv_now = ve_dv.n
                    for bank, cnt, p0 in num_banks:
                        p.add("vector",
                              lambda h, cnt=cnt, p0=p0: h.reciprocal(
                                  smalls[:, 28 + 2 * p0:28 + 2 * (p0 + cnt)],
                                  smalls[:, 12 + 2 * p0:12 + 2 * (p0 + cnt)]),
                              waits=[(ve_dv, dv_now),
                                     (ve_div, ve_div.n)],
                              incs=[(ve_rc, 1)])
                    rc_now = ve_rc.n
                    for bank, cnt, p0 in num_banks:
                        p.add("vector",
                              lambda h, bank=bank, cnt=cnt, p0=p0:
                              h.scalar_tensor_tensor(
                                  ap3(attn[jq % 2][:], 128 * p0,
                                      [[128, cnt], [64, 2], [1, 64]]),
                                  ap3(bank[:], 0, [[VW, cnt], [65, 2], [1, 64]]),
                                  1.0,
                                  ap3(smalls[:], 28 + 2 * p0,
                                      [[2, cnt], [1, 2], [0, 64]]),
                                  ALU.mult, ALU.mult),
                              waits=[(ve_rc, rc_now),
                                     (ac_tc, tc_done.get((s, jq - 2), 0))],
                              incs=[(ve_div, 1)])
                    div_done[(s, jq)] = ve_div.n

                def stage_tail(jq, s=s):
                    for k in range(KC):
                        p.add("tensor",
                              lambda h, k=k: h.transpose(
                                  pt[:, 128 * k:128 * (k + 1)],
                                  attn[jq % 2][:, 128 * k:128 * (k + 1)],
                                  ident[:]),
                              waits=[(ve_div, div_done[(s, jq)]),
                                     (ac_tc, tc_done.get((s, jq - 1), 0)),
                                     (dma_misc, 16)],
                              incs=([(pe_t, 1)] if k == KC - 1 else []))
                    if jq + 1 < 4:
                        stage_xf_num(jq + 1)
                        stage_div(jq + 1)
                    p.add("scalar",
                          lambda h: h.activation(
                              attnT[jq % 2][:], pt[:], AF.Copy),
                          waits=[(pe_t, pe_t.n),
                                 (pe_wo, wo_done.get((s, jq - 2), 0))],
                          incs=[(ac_tc, 1)])
                    tc_done[(s, jq)] = ac_tc.n
                    if jq >= 1:
                        stage_ln1(jq - 1)
                    for k in range(KC):
                        for n in range(2):
                            p.add("tensor",
                                  lambda h, k=k, n=n: h.matmul(
                                      pb[5 + n][:],
                                      attnT[jq % 2][:, 128 * k:128 * (k + 1)],
                                      wo_sb[:, D * k + 512 * n:
                                            D * k + 512 * (n + 1)],
                                      start=(k == 0), stop=(k == KC - 1)),
                                  waits=[(ac_tc, ac_tc.n), (ld_wq, 32),
                                         (ve_r1, r1_done.get((s, jq - 1), 0))],
                                  incs=([(pe_wo, 1)] if k == KC - 1 else []))
                    wo_done[(s, jq)] = pe_wo.n
                    for n in range(2):
                        p.add("vector",
                              lambda h, n=n: h.scalar_tensor_tensor(
                                  r1[jq % 2][:, 512 * n:512 * (n + 1)],
                                  pb[5 + n][:], 1.0,
                                  xf[jq % 2][:, 512 * n:512 * (n + 1)],
                                  ALU.mult, ALU.add),
                              waits=[(pe_wo, pe_wo.n),
                                     (ld_xfp[jq % 2], xf_need[(s, jq)]),
                                     (ac_x1, x1_done.get((s, jq - 2), 0))],
                              incs=[(ve_r1, 1)])
                    r1_done[(s, jq)] = ve_r1.n

                def stage_ln1(jq, s=s):
                    def store_x1(apply_n):
                        p.add("scalar",
                              lambda h: h.dma_start(
                                  out=x1_bf.ap()[512 * s + 128 * jq:
                                                 512 * s + 128 * jq + 128, :],
                                  in_=x1[:]),
                              waits=[(ac_x1, apply_n)],
                              incs=[(st_x1p[jq % 2], 16)])
                    add_ln(r1[jq % 2], x1,
                           (ve_r1, r1_done[(s, jq)]),
                           [(st_x1p[0], st_x1p[0].n),
                            (st_x1p[1], st_x1p[1].n)], store_x1)
                    x1_done[(s, jq)] = ac_x1.n

                stage_xf_num(0)
                stage_div(0)
                for jq in range(4):
                    stage_tail(jq)
                stage_ln1(3)
            p.add("tensor", lambda h: h.transpose(
                pt[:, 0:128], attn[1][:, 0:128], ident[:]),
                waits=[(ac_tc, ac_tc.n)], incs=[(bar_b, 1)])
            p.add("vector", lambda h: h.nop(),
                  waits=[(ac_x1, ac_x1.n)], incs=[(bar_b, 1)])
            p.add("scalar", lambda h: h.nop(),
                  incs=[(bar_b, 1)])

        with contextlib.ExitStack() as stC:
            eC = stC.enter_context
            pb7 = eC(nc.psum_tensor("pb7", [128, 512], F32))
            h2t = eC(nc.sbuf_tensor("h2t", [128, MC * 512], BF16))
            sg = [eC(nc.sbuf_tensor(f"sg{i}", [128, 512], BF16))
                  for i in range(2)]
            ub = [eC(nc.sbuf_tensor(f"ub{i}", [128, 512], BF16))
                  for i in range(2)]
            x1j = [eC(nc.sbuf_tensor(f"x1j{i}", [128, D], BF16))
                   for i in range(4)]
            dbanks = pb + [pb7]
            r2 = [eC(nc.sbuf_tensor(f"r2_{i}", [128, D], F32))
                  for i in range(2)]
            x2 = [eC(nc.sbuf_tensor(f"x2_{i}", [128, D], F32))
                  for i in range(2)]

            CW = [(bar_b, 3)]
            u_done = {}
            h2_done = {}
            h2_seq = []
            r2_done = {}
            x2_done = {}
            x1j_need = {}
            strip_done = []

            x1t_need_p = {}
            pe_d_bases = {}

            def do_r2_ln2(sp, jq):
                for n in range(2):
                    p.add("vector",
                          lambda h, sp=sp, jq=jq, n=n:
                          h.scalar_tensor_tensor(
                              r2[jq % 2][:, 512 * n:512 * (n + 1)],
                              dbanks[2 * jq + n][:], S127SQ,
                              x1j[jq][:, 512 * n:512 * (n + 1)],
                              ALU.mult, ALU.add),
                          waits=[(pe_d, pe_d_bases[sp] + jq + 1),
                                 (ld_x1jq[jq], x1j_need[(sp, jq)]),
                                 (ac_x1, x2_done.get((sp, jq - 2), 0))],
                          incs=[(ve_r2, 1)])
                r2_done[(sp, jq)] = ve_r2.n

                def store_out(apply_n, sp=sp, jq=jq):
                    p.add("scalar",
                          lambda h: h.dma_start(
                              out=out[512 * sp + 128 * jq:
                                      512 * sp + 128 * jq + 128, :],
                              in_=x2[jq % 2][:]),
                          waits=[(ac_x1, apply_n)],
                          incs=[(st_outp[jq % 2], 16)])
                add_ln(r2[jq % 2], x2[jq % 2], (ve_r2, ve_r2.n),
                       [(st_outp[jq % 2], st_outp[jq % 2].n)],
                       store_out)
                x2_done[(sp, jq)] = ac_x1.n

            def add_x1t(s):
                sn = [(st_x1p[0], 32 * (s + 1)), (st_x1p[1], 32 * (s + 1))]
                war = ([(pe_u, u_done[s - 2])] if s >= 2
                       else [(pe_q, q_done[6 + s])])
                for k in range(KC):
                    p.add("sync",
                          lambda h, s=s, k=k: h.dma_start(
                              out=xt[s % 2][:, 512 * k:512 * (k + 1)],
                              in_=x1_bf.ap()[512 * s:512 * (s + 1),
                                             128 * k:128 * (k + 1)],
                              transpose=True),
                          waits=sn + war,
                          incs=[(xt_sp[s % 2], 16)])
                x1t_need_p[s] = xt_sp[s % 2].n

            add_x1t(0)
            for s in range(NSUP):
                st_need = [(st_x1p[0], 32 * (s + 1)), (st_x1p[1], 32 * (s + 1))]
                x1t_need = x1t_need_p[s]
                for m in range(MC):
                    if s >= 1 and m < 4:
                        do_r2_ln2(s - 1, m)
                    gb, ubk = pb[m % 2 * 2], pb[m % 2 * 2 + 1]
                    gw = ([(ve_r2, r2_done.get((s - 1, m), 0))]
                          if m < 2 else [])
                    for k in range(KC):
                        p.add("tensor",
                              lambda h, m=m, k=k, s=s, gb=gb: h.matmul(
                                  gb[:],
                                  wgt[:, HID * k + 128 * m:HID * k + 128 * m + 128],
                                  xt[s % 2][:, 512 * k:512 * (k + 1)],
                                  start=(k == 0), stop=(k == KC - 1)),
                              waits=CW + [(xt_sp[s % 2], x1t_need),
                                          (ld_wgt, 128),
                                          (ac_sg, max(0, ac_sg.n - 1))] +
                              (gw if k == 0 else []),
                              incs=([(pe_g, 1)] if k == KC - 1 else []))
                    for k in range(KC):
                        p.add("tensor",
                              lambda h, m=m, k=k, s=s, ubk=ubk: h.matmul(
                                  ubk[:],
                                  wut[:, HID * k + 128 * m:HID * k + 128 * m + 128],
                                  xt[s % 2][:, 512 * k:512 * (k + 1)],
                                  start=(k == 0), stop=(k == KC - 1)),
                              waits=CW + [(ld_wut, 128),
                                          (ac_ub, max(0, ac_ub.n - 1))] +
                              (gw if k == 0 else []),
                              incs=([(pe_u, 1)] if k == KC - 1 else []))
                    h2_war = h2_seq[-2] if len(h2_seq) >= 2 else 0
                    p.add("scalar",
                          lambda h, m=m, gb=gb: h.activation(
                              sg[m % 2][:], gb[:], AF.Silu, scale=S127),
                          waits=[(pe_g, pe_g.n), (ve_h2, h2_war)],
                          incs=[(ac_sg, 1)])
                    p.add("scalar",
                          lambda h, m=m, ubk=ubk: h.activation(
                              ub[m % 2][:], ubk[:], AF.Copy),
                          waits=[(pe_u, pe_u.n), (ve_h2, h2_war)],
                          incs=[(ac_ub, 1)])
                    p.add("vector",
                          lambda h, m=m: h.tensor_tensor(
                              h2t[:, 512 * m:512 * (m + 1)],
                              sg[m % 2][:], ub[m % 2][:], ALU.mult),
                          waits=[(ac_sg, ac_sg.n), (ac_ub, ac_ub.n),
                                 (pe_d, 4 * s)],
                          incs=[(ve_h2, 1)])
                    h2_done[(s, m)] = ve_h2.n
                    h2_seq.append(ve_h2.n)
                u_done[s] = pe_u.n
                if s + 1 < NSUP:
                    add_x1t(s + 1)
                for jq in range(4):
                    p.add("scalar",
                          lambda h, s=s, jq=jq: h.dma_start(
                              out=x1j[jq][:],
                              in_=x1_bf.ap()[512 * s + 128 * jq:
                                             512 * s + 128 * jq + 128, :]),
                          waits=CW + st_need +
                          [(ve_r2, r2_done.get((s - 1, jq), 0))],
                          incs=[(ld_x1jq[jq], 16)])
                    x1j_need[(s, jq)] = ld_x1jq[jq].n
                pe_d_base = pe_d.n
                pe_d_bases[s] = pe_d_base
                for k in range(MC):
                    kp = k // 2
                    if k % 2 == 0:
                        g = len(strip_done)
                        strip_war = (strip_done[g - 2] if g >= 2
                                     else (pe_ds, 0))
                        p.add("sync",
                              lambda h, k=k: h.dma_start(
                                  out=wdt[(k // 2) % 2][:]
                                  .rearrange("p (b d) -> p b d", b=2),
                                  in_=wdt_bf.ap()[128 * k:128 * (k + 2), :]
                                  .rearrange("(b p) d -> p b d", p=128)),
                              waits=[(mk_s[0], 16 * (MC // 4)),
                                     (mk_s[1], 16 * (MC // 4)),
                                     (mk_s[2], 16 * (MC // 4)),
                                     (mk_s[3], 16 * (MC // 4)),
                                     strip_war],
                              incs=[(ld_wdtp[(k // 2) % 2], 16)])
                        wdt_need = ld_wdtp[kp % 2].n
                    for n in range(2):
                        for jq in range(4):
                            xw = []
                            if k == 0:
                                xw = [(ac_sg, ac_sg.n), (ac_ub, ac_ub.n),
                                      (ve_r2, r2_done.get((s - 1, 3), 0))]
                            incs = []
                            if k == MC - 1 and n == 1:
                                incs = [(pe_d, 1)]
                            elif n == 1 and jq == 3 and k % 2 == 1:
                                incs = [(pe_ds, 1)]
                            elif k == MC - 1 and n == 0 and jq == 3:
                                incs = [(pe_ds, 1)]
                            p.add("tensor",
                                  lambda h, k=k, jq=jq, n=n: h.matmul(
                                      dbanks[2 * jq + n][:],
                                      h2t[:, 512 * k + 128 * jq:
                                           512 * k + 128 * jq + 128],
                                      wdt[(k // 2) % 2][:, 1024 * (k % 2) +
                                                        512 * n:
                                                        1024 * (k % 2) +
                                                        512 * (n + 1)],
                                      start=(k == 0), stop=(k == MC - 1)),
                                  waits=[(ve_h2, h2_done[(s, k)]),
                                         (ld_wdtp[kp % 2], wdt_need)] + xw,
                                  incs=incs)
                    if k % 2 == 1:
                        strip_done.append((pe_d, pe_d_base + 4)
                                          if k == MC - 1
                                          else (pe_ds, pe_ds.n))

                if s == NSUP - 1:
                    for jq in range(4):
                        do_r2_ln2(s, jq)
            p.add("scalar", lambda h: h.nop(),
                  waits=[(st_outp[0], 16 * NJ // 2),
                         (st_outp[1], 16 * NJ // 2)])
            if PERTURB == 1:
                p.add("scalar", lambda h: h.nop())
            elif PERTURB == 2:
                p.add("scalar", lambda h: h.nop())
                p.add("vector", lambda h: h.nop())
            elif PERTURB == 5:
                p.add("tensor", lambda h: h.nop())
            elif PERTURB == 6:
                for _ in range(3):
                    p.add("scalar", lambda h: h.nop())
                    p.add("vector", lambda h: h.nop())
                    p.add("tensor", lambda h: h.nop())

        with nc.Block() as block:
            p.emit(block)

    nc._prog_n_ops = p.n_ops
    return nc


_NC_CACHE = {}


def kernel(L0_attn_x, L0_attn_Wq, L0_attn_Wk, L0_attn_Wv, L0_attn_Wo,
           L0_ffn_w_gate, L0_ffn_w_up, L0_ffn_w_down):
    x = np.ascontiguousarray(np.asarray(L0_attn_x, dtype=np.float32))
    wq = np.ascontiguousarray(np.asarray(L0_attn_Wq, dtype=np.float32))
    wk = np.ascontiguousarray(np.asarray(L0_attn_Wk, dtype=np.float32))
    wv = np.ascontiguousarray(np.asarray(L0_attn_Wv, dtype=np.float32))
    wo = np.ascontiguousarray(np.asarray(L0_attn_Wo, dtype=np.float32))
    wg = np.ascontiguousarray(np.asarray(L0_ffn_w_gate, dtype=np.int32))
    wu = np.ascontiguousarray(np.asarray(L0_ffn_w_up, dtype=np.int32))
    wd = np.ascontiguousarray(np.asarray(L0_ffn_w_down, dtype=np.int32))
    ident = np.eye(128, dtype=ml_dtypes.bfloat16)

    if "nc" not in _NC_CACHE:
        _NC_CACHE["nc"] = build_program()
    nc = _NC_CACHE["nc"]

    in_maps = []
    for c in range(8):
        b, half = c // 2, c % 2
        xo = np.ascontiguousarray(x[b, half * T:(half + 1) * T, :])
        in_maps.append(dict(x=xo, wq=wq, wk=wk, wv=wv, wo=wo,
                            wg=wg, wu=wu, wd=wd, ident=ident))

    res = run_bass_kernel_spmd(nc, in_maps, core_ids=list(range(8)))

    outp = np.empty((4, 8192, 1024), dtype=np.float32)
    for c in range(8):
        b, half = c // 2, c % 2
        outp[b, half * T:(half + 1) * T, :] = np.asarray(res.results[c]["out"])
    return outp


if __name__ == "__main__":
    nc = build_program()
    print("built OK; instructions:", nc.n_instructions())
